# version 05
# baseline (speedup 1.0000x reference)
import contextlib

import numpy as np

import concourse.bass as bass
import concourse.mybir as mybir
from concourse.bass_utils import run_bass_kernel_spmd

F32 = mybir.dt.float32
BF16 = mybir.dt.bfloat16
AF = mybir.ActivationFunctionType
ALU = mybir.AluOpType
AX = mybir.AxisListType

T = 2048
D = 1024
H = 16
HD = 64
FF = 4096
NT = T // 128
NC_ = D // 128
NSEQ = 2
EPS = 1e-5

EMIT = {"attn_a": True, "ffn1": True, "attn_b": True, "ffn2": True}
FFN_STAGE = 4
DBG = False
YT_STAGE = 3


class Ctr:

    def __init__(self, nc, st, name):
        self.h = st.enter_context(nc.semaphore(name))
        self.n = 0

    def inc(self, instr, k=1):
        instr.then_inc(self.h, k)
        self.n += k
        return instr

    def wait(self, eng, lag=0):
        if self.n - lag > 0:
            eng.wait_ge(self.h, self.n - lag)


def build_nc():
    nc = bass.Bass("TRN2", target_bir_lowering=False)

    x_h = nc.dram_tensor("x", [NSEQ * T, D], F32, kind="ExternalInput").ap()
    out_h = nc.dram_tensor("out", [NSEQ * T, D], F32, kind="ExternalOutput").ap()
    wqkv_t = nc.dram_tensor("wqkv_t", [128, 24 * NC_ * 128], F32, kind="ExternalInput").ap()
    wout_t = nc.dram_tensor("wout_t", [128, NC_ * D], F32, kind="ExternalInput").ap()
    wq_t = nc.dram_tensor("wq_t", [128, NC_ * D], F32, kind="ExternalInput").ap()
    wk_t = nc.dram_tensor("wk_t", [128, NC_ * D], F32, kind="ExternalInput").ap()
    wv_t = nc.dram_tensor("wv_t", [128, NC_ * D], F32, kind="ExternalInput").ap()
    w1_h = nc.dram_tensor("w1", [128, NC_ * FF], F32, kind="ExternalInput").ap()
    w2_h = nc.dram_tensor("w2", [128, 32 * D], F32, kind="ExternalInput").ap()
    b1_h = nc.dram_tensor("b1r", [128, FF // 128], F32, kind="ExternalInput").ap()
    b2_h = nc.dram_tensor("b2r", [1, D], F32, kind="ExternalInput").ap()
    bq_h = nc.dram_tensor("bqr", [128, NC_], F32, kind="ExternalInput").ap()
    bk_h = nc.dram_tensor("bkr", [128, NC_], F32, kind="ExternalInput").ap()
    bv_h = nc.dram_tensor("bvr", [1, D], F32, kind="ExternalInput").ap()
    sa_h = nc.dram_tensor("s_a", [1, 1], F32, kind="ExternalInput").ap()
    sb_h = nc.dram_tensor("s_b", [1, 1], F32, kind="ExternalInput").ap()
    sdt_h = nc.dram_tensor("s_dt", [1, 1], F32, kind="ExternalInput").ap()
    eye_h = nc.dram_tensor("c_eye", [128, 128], F32, kind="ExternalInput").ap()
    tri_h = nc.dram_tensor("c_tri", [128, 128], F32, kind="ExternalInput").ap()
    trit_h = nc.dram_tensor("c_trit", [128, 128], F32, kind="ExternalInput").ap()
    ones_h = nc.dram_tensor("c_ones", [128, 128], F32, kind="ExternalInput").ap()
    sel16_h = nc.dram_tensor("c_sel16", [8, 128], F32, kind="ExternalInput").ap()

    if DBG:
        dbg_yt = nc.dram_tensor("dbg_yt", [128, NC_ * T], F32, kind="ExternalOutput").ap()
        dbg_stats = nc.dram_tensor("dbg_stats", [128, 6 * NT], F32, kind="ExternalOutput").ap()
        dbg_h1 = nc.dram_tensor("dbg_h1", [128, 16 * 512], BF16, kind="ExternalOutput").ap()
    st = contextlib.ExitStack()
    with st:
        X = st.enter_context(nc.sbuf_tensor("X", [128, NT * D], F32))
        YT = st.enter_context(nc.sbuf_tensor("YT", [128, NC_ * T], BF16))
        eye_bf = st.enter_context(nc.sbuf_tensor("eye_bf", [128, 128], BF16))
        eye_f = st.enter_context(nc.sbuf_tensor("eye_f", [128, 128], F32))
        tri_bf = st.enter_context(nc.sbuf_tensor("tri_bf", [128, 128], BF16))
        trit_bf = st.enter_context(nc.sbuf_tensor("trit_bf", [128, 128], BF16))
        ones_bf = st.enter_context(nc.sbuf_tensor("ones_bf", [128, 128], BF16))
        ones_f = st.enter_context(nc.sbuf_tensor("ones_f", [128, 128], F32))
        b1t = st.enter_context(nc.sbuf_tensor("b1t", [128, FF // 128], F32))
        bqt = st.enter_context(nc.sbuf_tensor("bqt", [128, NC_], F32))
        bkt = st.enter_context(nc.sbuf_tensor("bkt", [128, NC_], F32))
        s_abdt = st.enter_context(nc.sbuf_tensor("s_abdt", [1, 4], F32))
        sel16_bf = st.enter_context(nc.sbuf_tensor("sel16_bf", [8, 128], BF16))
        abdt_c = st.enter_context(nc.sbuf_tensor("abdt_c", [128, 4], F32))
        negu = st.enter_context(nc.sbuf_tensor("negu", [128, 16], F32))
        fT = st.enter_context(nc.sbuf_tensor("fT", [128, 16], F32))
        sums = st.enter_context(nc.sbuf_tensor("sums", [128, NT], F32))
        sumsq = st.enter_context(nc.sbuf_tensor("sumsq", [128, NT], F32))
        mean = st.enter_context(nc.sbuf_tensor("mean", [128, NT], F32))
        rstd = st.enter_context(nc.sbuf_tensor("rstd", [128, NT], F32))
        st1 = st.enter_context(nc.sbuf_tensor("st1", [128, NT], F32))
        st2 = st.enter_context(nc.sbuf_tensor("st2", [128, NT], F32))
        sq_scr = [st.enter_context(nc.sbuf_tensor(f"sqg{i}", [128, D], BF16)) for i in range(2)]
        ytok = [st.enter_context(nc.sbuf_tensor(f"ytok{i}", [128, D], BF16)) for i in range(2)]

        c_in = Ctr(nc, st, "c_in")
        c_w = Ctr(nc, st, "c_w")
        c_w2 = Ctr(nc, st, "c_w2")
        c_mm = Ctr(nc, st, "c_mm")
        c_v = Ctr(nc, st, "c_v")
        c_s = Ctr(nc, st, "c_s")
        c_st = Ctr(nc, st, "c_st")
        c_g = Ctr(nc, st, "c_g")
        c_tr = Ctr(nc, st, "c_tr")
        c_ev = Ctr(nc, st, "c_ev")
        c_dm = Ctr(nc, st, "c_dm")

        with nc.Block("setup") as blk:
            @blk.sync
            def _(sync):
                c_w2.inc(sync.dma_start(out=eye_f[:], in_=eye_h), 16)
                c_w2.inc(sync.dma_start(out=ones_f[:], in_=ones_h), 16)
                c_w2.inc(sync.dma_start(out=b1t[:], in_=b1_h), 16)
                c_w2.inc(sync.dma_start(out=bqt[:], in_=bq_h), 16)
                c_w2.inc(sync.dma_start(out=bkt[:], in_=bk_h), 16)
                c_w2.inc(sync.dma_start(out=s_abdt[0:1, 0:1], in_=sa_h), 16)
                c_w2.inc(sync.dma_start(out=s_abdt[0:1, 1:2], in_=sb_h), 16)
                c_w2.inc(sync.dma_start(out=s_abdt[0:1, 2:3], in_=sdt_h), 16)

            @blk.gpsimd
            def _(g):
                c_w.inc(g.dma_start(out=eye_bf[:], in_=eye_h), 16)
                c_w.inc(g.dma_start(out=tri_bf[:], in_=tri_h), 16)
                c_w.inc(g.dma_start(out=trit_bf[:], in_=trit_h), 16)
                c_w.inc(g.dma_start(out=ones_bf[:], in_=ones_h), 16)
                c_w.inc(g.dma_start(out=sel16_bf[:], in_=sel16_h), 16)

        with (
            nc.psum_tensor("ps_abdt", [128, 4], F32) as ps_ab,
            nc.Block("abdt") as blk,
        ):
            @blk.tensor
            def _(t):
                c_w2.wait(t)
                c_mm.inc(t.matmul(ps_ab[:, 0:3], ones_f[0:1, 0:128], s_abdt[0:1, 0:3],
                                  start=True, stop=True))
            @blk.vector
            def _(v):
                c_mm.wait(v)
                c_v.inc(v.tensor_copy(abdt_c[:, 0:3], ps_ab[:, 0:3]))


        def xs(i):
            return X[:, i * D:(i + 1) * D]

        def yt_sl(c, lo, hi):
            return YT[:, c * T + lo: c * T + hi]

        def emit_load_x(seq):
            with nc.Block(f"load{seq}") as blk:
                @blk.sync
                def _(sync):
                    c_st.wait(sync)
                    src = x_h[seq * T:(seq + 1) * T, :].rearrange("(i p) f -> p i f", p=128)
                    c_in.inc(sync.dma_start(out=X[:].rearrange("p (i f) -> p i f", f=D), in_=src), 16)

        def emit_store_x(seq):
            with nc.Block(f"store{seq}") as blk:
                @blk.sync
                def _(sync):
                    dst = out_h[seq * T:(seq + 1) * T, :].rearrange("(i p) f -> p i f", p=128)
                    c_st.inc(sync.dma_start(out=dst, in_=X[:].rearrange("p (i f) -> p i f", f=D)), 16)
                    c_st.wait(sync)

        def emit_ln(tag):
            with nc.Block(f"ln_{tag}") as blk:
                s0, v0 = c_s.n, c_v.n

                @blk.scalar
                def _(s):
                    c_in.wait(s)
                    for i in range(NT):
                        if i >= 2:
                            s.wait_ge(c_v.h, v0 + i - 1)
                        c_s.inc(s.activation(sq_scr[i % 2][:], xs(i), AF.Square))

                @blk.vector
                def _(v):
                    c_in.wait(v)
                    for i in range(NT):
                        v.reduce_sum(sums[:, i:i + 1], xs(i), axis=AX.X)
                        v.wait_ge(c_s.h, s0 + i + 1)
                        c_v.inc(v.reduce_sum(sumsq[:, i:i + 1], sq_scr[i % 2][:], axis=AX.X))
                    def ch(ins):
                        ins.then_inc(c_v.h, 1)
                        c_v.n += 1
                        v.wait_ge(c_v.h, c_v.n)
                    v.wait_ge(c_v.h, c_v.n)
                    ch(v.tensor_scalar_mul(mean[:], sums[:], 1.0 / D))
                    ch(v.tensor_mul(st1[:], mean[:], mean[:]))
                    ch(v.tensor_scalar_mul(st2[:], sumsq[:], 1.0 / D))
                    ch(v.tensor_sub(st1[:], st2[:], st1[:]))
                    ch(v.tensor_scalar_add(st1[:], st1[:], EPS))

                @blk.scalar
                def _(s):
                    c_v.wait(s)
                    s.activation(st2[:], st1[:], AF.Ln).then_inc(c_s.h, 1)
                    s.wait_ge(c_s.h, c_s.n + 1)
                    s.activation(rstd[:], st2[:], AF.Exp, scale=-0.5).then_inc(c_s.h, 1)
                    s.wait_ge(c_s.h, c_s.n + 2)
                c_s.n += 2

        def emit_yt(tag):
            with (
                nc.psum_tensor(f"ps_yt0{tag}", [128, D], BF16) as pt0,
                nc.psum_tensor(f"ps_yt1{tag}", [128, D], BF16) as pt1,
                nc.Block(f"yt_{tag}") as blk,
            ):
                pts = [pt0, pt1]
                v0, t0_, e0 = c_v.n, c_tr.n, c_ev.n

                @blk.vector
                def _(v):
                    for i in range(NT):
                        if i >= 2 and YT_STAGE >= 2:
                            v.wait_ge(c_tr.h, t0_ + (i - 1) * 8)
                        v.tensor_scalar(ytok[i % 2][:], xs(i), mean[:, i:i + 1],
                                        rstd[:, i:i + 1], ALU.subtract,
                                        ALU.mult).then_inc(c_v.h, 1)
                        if i >= 1 and YT_STAGE >= 3:
                            v.wait_ge(c_tr.h, t0_ + i * 8)
                            v.tensor_copy(
                                YT[:].rearrange("p (c t) -> p c t", t=T)[:, :, (i - 1) * 128:i * 128],
                                pts[(i - 1) % 2][:].rearrange("p (c t) -> p c t", t=128)).then_inc(c_ev.h, 1)
                    if YT_STAGE >= 3:
                        v.wait_ge(c_tr.h, t0_ + NT * 8)
                        v.tensor_copy(
                            YT[:].rearrange("p (c t) -> p c t", t=T)[:, :, (NT - 1) * 128:NT * 128],
                            pts[(NT - 1) % 2][:].rearrange("p (c t) -> p c t", t=128)).then_inc(c_ev.h, 1)
                c_v.n = v0 + NT
                if YT_STAGE >= 3:
                    c_ev.n = e0 + NT

                @blk.tensor
                def _(t):
                    if YT_STAGE < 2:
                        return
                    c_w.wait(t)
                    for i in range(NT):
                        t.wait_ge(c_v.h, v0 + i + 1)
                        if i >= 2 and YT_STAGE >= 3:
                            t.wait_ge(c_ev.h, e0 + i - 1)
                        for c in range(NC_):
                            t.transpose(pts[i % 2][:, c * 128:(c + 1) * 128],
                                        ytok[i % 2][:, c * 128:(c + 1) * 128],
                                        eye_bf[:]).then_inc(c_tr.h, 1)
                if YT_STAGE >= 2:
                    c_tr.n = t0_ + NT * 8

        def stats_epi_scalar(s, ready, s_base, v_base):
            for i in range(NT):
                r = ready(i)
                if r:
                    s.wait_ge(c_ev.h, r)
                if i >= 2:
                    s.wait_ge(c_v.h, v_base + 2 * (i - 2) + 2)
                s.activation(sq_scr[i % 2][:], xs(i), AF.Square).then_inc(c_s.h, 1)
            s.wait_ge(c_v.h, v_base + 37)
            s.activation(st2[:], st1[:], AF.Ln).then_inc(c_s.h, 1)
            s.wait_ge(c_s.h, s_base + 17)
            s.activation(rstd[:], st2[:], AF.Exp, scale=-0.5).then_inc(c_s.h, 1)
            s.wait_ge(c_s.h, s_base + 18)

        def stats_epi_vector(v, ready, s_base, v_base):
            nv = [v_base]

            def ch(ins):
                ins.then_inc(c_v.h, 1)
                nv[0] += 1
                v.wait_ge(c_v.h, nv[0])
            for i in range(NT):
                r = ready(i)
                if r:
                    v.wait_ge(c_ev.h, r)
                v.reduce_sum(sums[:, i:i + 1], xs(i), axis=AX.X).then_inc(c_v.h, 1)
                nv[0] += 1
                v.wait_ge(c_s.h, s_base + i + 1)
                v.reduce_sum(sumsq[:, i:i + 1], sq_scr[i % 2][:],
                             axis=AX.X).then_inc(c_v.h, 1)
                nv[0] += 1
            v.wait_ge(c_v.h, nv[0])
            ch(v.tensor_scalar_mul(mean[:], sums[:], 1.0 / D))
            ch(v.tensor_mul(st1[:], mean[:], mean[:]))
            ch(v.tensor_scalar_mul(st2[:], sumsq[:], 1.0 / D))
            ch(v.tensor_sub(st1[:], st2[:], st1[:]))
            ch(v.tensor_scalar_add(st1[:], st1[:], EPS))

        def emit_ffn(seq, idx, skip_ln=False, stats_next=False):
            tag = f"f{seq}{idx}"
            if not skip_ln:
                emit_ln(tag)
            with (
                nc.sbuf_tensor(f"w1b_{tag}", [128, NC_ * 16 * 128], BF16) as w1b,
                nc.sbuf_tensor(f"w2b_{tag}", [128, 16 * D], BF16) as w2b,
                nc.sbuf_tensor(f"h1_{tag}", [128, 16 * 512], BF16) as h1,
                nc.sbuf_tensor(f"b2r_{tag}", [1, D], BF16) as b2row,
            ):
                for half in range(2):
                    hb = half * 16
                    with nc.Block(f"ffnw_{tag}{half}") as blk:
                        @blk.gpsimd
                        def _(g):
                            if half == 0:
                                c_w.inc(g.dma_start(out=b2row[:], in_=b2_h), 16)
                            c_mm.wait(g)
                            for pc in range(4):
                                c_w.inc(g.dma_start(
                                    out=w1b[:, pc * 4096:(pc + 1) * 4096],
                                    in_=w1_h[:, half * 16384 + pc * 4096:
                                             half * 16384 + (pc + 1) * 4096]), 16)
                            for pc in range(4):
                                c_w.inc(g.dma_start(
                                    out=w2b[:, pc * 4096:(pc + 1) * 4096],
                                    in_=w2_h[:, half * 16384 + pc * 4096:
                                             half * 16384 + (pc + 1) * 4096]), 16)

                    with (
                        nc.psum_tensor(f"ps_h0{tag}{half}", [128, 512], F32) as ph0,
                        nc.psum_tensor(f"ps_h1{tag}{half}", [128, 512], F32) as ph1,
                        nc.psum_tensor(f"ps_o0{tag}{half}", [128, 512], F32) as po0,
                        nc.psum_tensor(f"ps_o1{tag}{half}", [128, 512], F32) as po1,
                        nc.psum_tensor(f"ps_yt0{tag}{half}", [128, D], BF16) as pt0,
                        nc.psum_tensor(f"ps_yt1{tag}{half}", [128, D], BF16) as pt1,
                        nc.Block(f"ffn_{tag}{half}") as blk,
                    ):
                        phs, pos, pts = [ph0, ph1], [po0, po1], [pt0, pt1]
                        nyt = NT if half == 0 else 0
                        v0, s0, m0, t0_ = c_v.n, c_s.n, c_mm.n, c_tr.n
                        ev_yt = c_ev.n
                        e0 = ev_yt + nyt
                        wbase = c_w.n - 8 * 16

                        def f_ready(i):
                            mt_, s2 = i // 4, i % 4
                            return e0 + mt_ * 8 + s2 * 2 + 2

                        @blk.scalar
                        def _(s):
                            c_w2.wait(s)
                            for mt in range(4):
                                for c in range(16):
                                    s.wait_ge(c_mm.h, m0 + mt * 24 + c + 1)
                                    s.activation(h1[:, c * 512:(c + 1) * 512],
                                                 phs[c % 2][:], AF.Gelu,
                                                 bias=b1t[:, hb + c:hb + c + 1]).then_inc(c_s.h, 1)
                            if half == 1 and stats_next:
                                stats_epi_scalar(s, f_ready, s0 + 64, v0)
                        c_s.n = s0 + 64 + (18 if stats_next else 0) if half == 1 else s0 + 64

                        @blk.vector
                        def _(v):
                            if half == 0:
                                for i in range(NT):
                                    if i >= 2:
                                        v.wait_ge(c_tr.h, t0_ + (i - 1) * 8)
                                    v.tensor_scalar(ytok[i % 2][:], xs(i), mean[:, i:i + 1],
                                                    rstd[:, i:i + 1], ALU.subtract,
                                                    ALU.mult).then_inc(c_v.h, 1)
                                    if i >= 1:
                                        v.wait_ge(c_tr.h, t0_ + i * 8)
                                        v.tensor_copy(
                                            YT[:].rearrange("p (c t) -> p c t", t=T)[:, :, (i - 1) * 128:i * 128],
                                            pts[(i - 1) % 2][:].rearrange("p (c t) -> p c t", t=128)).then_inc(c_ev.h, 1)
                                v.wait_ge(c_tr.h, t0_ + NT * 8)
                                v.tensor_copy(
                                    YT[:].rearrange("p (c t) -> p c t", t=T)[:, :, (NT - 1) * 128:NT * 128],
                                    pts[(NT - 1) % 2][:].rearrange("p (c t) -> p c t", t=128)).then_inc(c_ev.h, 1)
                            for mt in range(4):
                                for j in range(8):
                                    s_, hf = j // 2, j % 2
                                    i = 4 * mt + s_
                                    v.wait_ge(c_mm.h, m0 + mt * 24 + 16 + j + 1)
                                    v.tensor_add(
                                        X[:, i * D + hf * 512: i * D + hf * 512 + 512],
                                        X[:, i * D + hf * 512: i * D + hf * 512 + 512],
                                        pos[j % 2][:]).then_inc(c_ev.h, 1)
                            if half == 1 and stats_next:
                                stats_epi_vector(v, f_ready, s0 + 64, v0)
                        c_v.n = v0 + nyt + ((37 if stats_next else 0) if half == 1 else 0)
                        c_ev.n = e0 + 32

                        @blk.tensor
                        def _(t):
                            if half == 0:
                                for i in range(NT):
                                    t.wait_ge(c_v.h, v0 + i + 1)
                                    if i >= 2:
                                        t.wait_ge(c_ev.h, ev_yt + i - 1)
                                    for c in range(NC_):
                                        t.transpose(pts[i % 2][:, c * 128:(c + 1) * 128],
                                                    ytok[i % 2][:, c * 128:(c + 1) * 128],
                                                    eye_bf[:]).then_inc(c_tr.h, 1)
                            for mt in range(4):
                                if half == 0:
                                    t.wait_ge(c_ev.h, ev_yt + min(NT, 4 * (mt + 1)))
                                for c in range(16):
                                    if c >= 2:
                                        t.wait_ge(c_s.h, s0 + mt * 16 + c - 1)
                                    if mt == 0:
                                        t.wait_ge(c_w.h, wbase + (c // 4 + 1) * 16)
                                    for i in range(NC_):
                                        w_sl = w1b[:, (c * NC_ + i) * 128:
                                                   (c * NC_ + i + 1) * 128]
                                        ins = t.matmul(phs[c % 2][:], w_sl,
                                                 yt_sl(i, mt * 512, (mt + 1) * 512),
                                                 start=(i == 0), stop=(i == NC_ - 1))
                                        if i == NC_ - 1:
                                            ins.then_inc(c_mm.h, 1)
                                t.wait_ge(c_s.h, s0 + (mt + 1) * 16)
                                for j in range(8):
                                    s_, hf = j // 2, j % 2
                                    i = 4 * mt + s_
                                    g_ = mt * 8 + j
                                    if g_ >= 2:
                                        t.wait_ge(c_ev.h, e0 + g_ - 1)
                                    for c in range(16):
                                        if mt == 0 and j == 0 and c % 4 == 0:
                                            t.wait_ge(c_w.h, wbase + (4 + c // 4 + 1) * 16)
                                        h_sl = h1[:, c * 512 + s_ * 128: c * 512 + s_ * 128 + 128]
                                        w_sl = w2b[:, c * D + hf * 512: c * D + hf * 512 + 512]
                                        ins = t.matmul(pos[j % 2][:], h_sl, w_sl,
                                                 start=(c == 0),
                                                 stop=(half == 0 and c == 15))
                                        if half == 0 and c == 15:
                                            ins.then_inc(c_mm.h, 1)
                                    if half == 1:
                                        t.matmul(pos[j % 2][:], ones_bf[0:1, 0:128],
                                                 b2row[0:1, hf * 512:hf * 512 + 512],
                                                 start=False, stop=True).then_inc(c_mm.h, 1)
                        c_mm.n = m0 + 4 * 24
                        if half == 0:
                            c_tr.n = t0_ + NT * 8

        def emit_attn_a(seq, stats_next=False):
            tag = f"a{seq}"
            emit_ln(tag)
            emit_yt(tag)
            with contextlib.ExitStack() as ast_:
                QTg = ast_.enter_context(nc.sbuf_tensor(f"QT_{tag}", [128, 4 * T], BF16))
                KTg = ast_.enter_context(nc.sbuf_tensor(f"KT_{tag}", [128, 4 * T], BF16))
                Vg = ast_.enter_context(nc.sbuf_tensor(f"Vg_{tag}", [128, 16 * 528], BF16))
                wqk0 = ast_.enter_context(nc.sbuf_tensor(f"wqk_{tag}0", [128, 8 * 128], BF16))
                wqk1 = ast_.enter_context(nc.sbuf_tensor(f"wqk_{tag}1", [128, 8 * 128], BF16))
                vrhs = ast_.enter_context(nc.sbuf_tensor(f"vrhs_{tag}", [128, 8 * 512], BF16))
                wog = ast_.enter_context(nc.sbuf_tensor(f"wog_{tag}", [128, 4 * 1024], BF16))
                PT0 = ast_.enter_context(nc.sbuf_tensor(f"PT_{tag}0", [128, 512], BF16))
                PT1 = ast_.enter_context(nc.sbuf_tensor(f"PT_{tag}1", [128, 512], BF16))
                sqb = ast_.enter_context(nc.sbuf_tensor(f"sqb_{tag}", [128, 512], BF16))
                Lsb = ast_.enter_context(nc.sbuf_tensor(f"Lsb_{tag}", [128, 512], F32))
                rLsb = ast_.enter_context(nc.sbuf_tensor(f"rLsb_{tag}", [128, 512], F32))
                ATs0 = ast_.enter_context(nc.sbuf_tensor(f"ATs_{tag}0", [64, T], BF16))
                ATs1 = ast_.enter_context(nc.sbuf_tensor(f"ATs_{tag}1", [64, T], BF16))
                bcsb = ast_.enter_context(nc.sbuf_tensor(f"bcsb_{tag}", [64, 512], F32))
                Ldist = ast_.enter_context(nc.sbuf_tensor(f"Ld_{tag}", [128, 128], F32))
                rLdist = ast_.enter_context(nc.sbuf_tensor(f"rLd_{tag}", [128, 128], F32))
                fhn = ast_.enter_context(nc.sbuf_tensor(f"fhn_{tag}", [128, 14 * 128], F32))
                ucol = ast_.enter_context(nc.sbuf_tensor(f"ucol_{tag}", [1, 16], F32))
                ubf = ast_.enter_context(nc.sbuf_tensor(f"ubf_{tag}", [1, 16], BF16))
                fcols = ast_.enter_context(nc.sbuf_tensor(f"fc_{tag}", [128, 4], F32))
                ATs = [ATs0, ATs1]
                PTs = [PT0, PT1]
                wqks = [wqk0, wqk1]

                def fsl(k):
                    return fhn[:, k * 128:(k + 1) * 128]

                for g in range(2):
                    hb8 = 8 * g

                    with contextlib.ExitStack() as pst_:
                        pss0 = pst_.enter_context(nc.psum_tensor(f"ps_s{tag}{g}", [128, 512], F32))
                        pss1 = pst_.enter_context(nc.psum_tensor(f"ps_t{tag}{g}", [128, 512], F32))
                        pv0 = pst_.enter_context(nc.psum_tensor(f"ps_pv0{tag}{g}", [65, 512], F32))
                        pv1 = pst_.enter_context(nc.psum_tensor(f"ps_pv1{tag}{g}", [65, 512], F32))
                        pv2 = pst_.enter_context(nc.psum_tensor(f"ps_pv2{tag}{g}", [65, 512], F32))
                        pv3 = pst_.enter_context(nc.psum_tensor(f"ps_pv3{tag}{g}", [65, 512], F32))
                        psbc = pst_.enter_context(nc.psum_tensor(f"ps_bc{tag}{g}", [64, 512], F32))
                        pssm = pst_.enter_context(nc.psum_tensor(f"ps_sm{tag}{g}", [128, 128], F32))
                        pss = [pss0, pss1]
                        pvs = [pv0, pv1, pv2, pv3]
                        with nc.Block(f"aproj_{tag}{g}") as blk:
                            w0 = c_w.n
                            m0 = c_mm.n
                            e0 = c_ev.n
                            g0_ = c_g.n

                            @blk.gpsimd
                            def _(gp):
                                for oc in range(8):
                                    if oc >= 2:
                                        gp.wait_ge(c_mm.h, m0 + (oc - 1) * 4)
                                    ocg = (4 * g + oc) if oc < 4 else (8 + 4 * g + (oc - 4))
                                    c_w.inc(gp.dma_start(
                                        out=wqks[oc % 2][:],
                                        in_=wqkv_t[:, ocg * 1024:(ocg + 1) * 1024]), 16)
                                c_w.inc(gp.dma_start(
                                    out=vrhs[:],
                                    in_=wqkv_t[:, (16 + 4 * g) * 1024:(16 + 4 * g + 4) * 1024]), 16)
                                c_w.inc(gp.dma_start(
                                    out=wog[:],
                                    in_=wout_t[:, 4 * g * 1024:(4 * g + 4) * 1024]), 16)

                            @blk.tensor
                            def _(t):
                                nmm = m0
                                for oc in range(8):
                                    t.wait_ge(c_w.h, w0 + (oc + 1) * 16)
                                    for m4 in range(4):
                                        gidx = oc * 4 + m4
                                        if gidx >= 2:
                                            t.wait_ge(c_ev.h, e0 + gidx - 1)
                                        for i in range(NC_):
                                            ins = t.matmul(pss[gidx % 2][:],
                                                           wqks[oc % 2][:, i * 128:(i + 1) * 128],
                                                           yt_sl(i, m4 * 512, (m4 + 1) * 512),
                                                           start=(i == 0), stop=(i == NC_ - 1))
                                        ins.then_inc(c_mm.h, 1)
                                        nmm += 1
                                t.wait_ge(c_w.h, w0 + 9 * 16)
                                for kt in range(NT):
                                    gidx = 32 + kt
                                    t.wait_ge(c_ev.h, e0 + gidx - 1)
                                    for i in range(NC_):
                                        v_rhs = vrhs[:].rearrange("p (a i o) -> p a i o",
                                                                  a=4, i=NC_)[:, :, i, :]
                                        ins = t.matmul(pss[gidx % 2][:],
                                                       yt_sl(i, kt * 128, (kt + 1) * 128),
                                                       v_rhs,
                                                       start=(i == 0), stop=(i == NC_ - 1))
                                    ins.then_inc(c_mm.h, 1)
                                    nmm += 1
                            c_mm.n = m0 + 48

                            @blk.vector
                            def _(v):
                                v.memset(Vg[:].rearrange("p (k h w) -> p k h w", h=8, w=66)
                                         [:, :, :, 64:65], 1.0).then_inc(c_g.h, 1)
                                for oc in range(8):
                                    for m4 in range(4):
                                        gidx = oc * 4 + m4
                                        v.wait_ge(c_mm.h, m0 + gidx + 1)
                                        dst = QTg if oc < 4 else KTg
                                        v.tensor_copy(
                                            dst[:, (oc % 4) * T + m4 * 512:(oc % 4) * T + (m4 + 1) * 512],
                                            pss[gidx % 2][:]).then_inc(c_ev.h, 1)
                                for kt in range(NT):
                                    gidx = 32 + kt
                                    v.wait_ge(c_mm.h, m0 + gidx + 1)
                                    v.tensor_copy(
                                        Vg[:, kt * 528:(kt + 1) * 528]
                                        .rearrange("p (h w) -> p h w", w=66)[:, :, 0:64],
                                        pss[gidx % 2][:].rearrange("p (h w) -> p h w", w=64)
                                    ).then_inc(c_ev.h, 1)
                            c_ev.n = e0 + 48
                            c_g.n = g0_ + 1

                        with nc.Block(f"auph_{tag}{g}") as blk:
                            v0 = c_v.n
                            m0 = c_mm.n
                            s0 = c_s.n

                            @blk.vector
                            def _(v):
                                nv = [c_v.n]
                                nmm = [m0]

                                def ch(ins):
                                    ins.then_inc(c_v.h, 1)
                                    nv[0] += 1
                                    v.wait_ge(c_v.h, nv[0])
                                for hp in range(8):
                                    j, rb = hp // 2, 64 * (hp % 2)
                                    for srci, srcbuf in ((0, QTg), (1, KTg)):
                                        for mc in range(4):
                                            ch(v.tensor_mul(
                                                sqb[:], srcbuf[:, j * T + mc * 512:j * T + (mc + 1) * 512],
                                                srcbuf[:, j * T + mc * 512:j * T + (mc + 1) * 512]))
                                            nmm[0] += 1
                                            v.wait_ge(c_mm.h, nmm[0])
                                            ch(v.reduce_max(ucol[0:1, srci * 4 + mc:srci * 4 + mc + 1],
                                                            psbc[0:1, 0:512], axis=AX.X))
                                    ch(v.reduce_max(ucol[0:1, 8:9], ucol[0:1, 0:4], axis=AX.X))
                                    ch(v.reduce_max(ucol[0:1, 9:10], ucol[0:1, 4:8], axis=AX.X))
                                    ch(v.tensor_add(ucol[0:1, 10:11], ucol[0:1, 8:9], ucol[0:1, 9:10]))
                                    ch(v.tensor_scalar_mul(ucol[0:1, 11:12], ucol[0:1, 10:11], 1.0 / 16.0))
                                    ch(v.tensor_copy(ubf[0:1, 0:1], ucol[0:1, 11:12]))
                                    nmm[0] += 1
                                    v.wait_ge(c_mm.h, nmm[0])
                                    ch(v.tensor_scalar_mul(negu[:, hb8 + hp:hb8 + hp + 1],
                                                           pssm[:, 0:1], -1.0))
                                c_v.n = nv[0]

                            @blk.tensor
                            def _(t):
                                nv = v0
                                nmm = m0
                                for hp in range(8):
                                    rb = 64 * (hp % 2)
                                    for srci in range(2):
                                        for mc in range(4):
                                            nv += 1
                                            t.wait_ge(c_v.h, nv)
                                            if nmm > m0:
                                                t.wait_ge(c_v.h, nv)
                                            t.matmul(psbc[0:1, 0:512],
                                                     ones_bf[rb:rb + 64, 0:1],
                                                     sqb[rb:rb + 64, :],
                                                     start=True, stop=True).then_inc(c_mm.h, 1)
                                            nmm += 1
                                            nv += 1
                                    nv += 5
                                    t.wait_ge(c_v.h, nv)
                                    t.matmul(pssm[:, 0:1], ones_bf[0:1, 0:128],
                                             ubf[0:1, 0:1], start=True, stop=True).then_inc(c_mm.h, 1)
                                    nmm += 1
                                    nv += 1
                            c_mm.n = m0 + 8 * 9

                        with nc.Block(f"aflash_{tag}{g}") as blk:
                            pairs = []
                            for m in range(4):
                                for kb in range(4 * m + 4):
                                    pairs.append((m, kb))
                            NP = len(pairs)

                            heads = []
                            mm_n = [c_mm.n]
                            s_n = [c_s.n]
                            v_n = [c_v.n]
                            dm_n = [c_dm.n]
                            g_n = [c_g.n]
                            for hp in range(8):
                                ev = {}
                                ids_S = {}
                                ids_PV = {}
                                ids_bc = {}
                                cnt = mm_n[0]
                                ids_S[0] = cnt = cnt + 1
                                for n in range(NP):
                                    if n + 1 < NP:
                                        ids_S[n + 1] = cnt = cnt + 1
                                    ids_PV[n] = cnt = cnt + 1
                                for m in range(4):
                                    ids_bc[m] = cnt = cnt + 1
                                mm_n[0] = cnt
                                ids_exp = {}
                                ids_lc = {}
                                cnt = s_n[0]
                                for n in range(NP):
                                    ids_exp[n] = cnt = cnt + 1
                                for m in range(4):
                                    ids_lc[m] = cnt = cnt + 1
                                s_n[0] = cnt
                                ids_mask = {}
                                cnt = v_n[0]
                                for n, (m, kb) in enumerate(pairs):
                                    if kb >= 4 * m:
                                        ids_mask[n] = cnt = cnt + 1
                                ids_evbc = {}
                                ids_att = {}
                                for m in range(4):
                                    ids_evbc[m] = cnt = cnt + 1
                                    ids_att[m] = cnt = cnt + 1
                                v_n[0] = cnt
                                ids_rec = {}
                                for m in range(4):
                                    ids_rec[m] = g_n[0] = g_n[0] + 1
                                ids_dml = {}
                                ids_dmr = {}
                                for m in range(4):
                                    ids_dml[m] = dm_n[0] = dm_n[0] + 16
                                    ids_dmr[m] = dm_n[0] = dm_n[0] + 16
                                id_dma = dm_n[0] = dm_n[0] + 16
                                heads.append(dict(S=ids_S, PV=ids_PV, bc=ids_bc, exp=ids_exp,
                                                  lc=ids_lc, mask=ids_mask, evbc=ids_evbc,
                                                  att=ids_att, rec=ids_rec, dml=ids_dml,
                                                  dmr=ids_dmr, dma=id_dma))

                            g_entry = c_g.n

                            @blk.tensor
                            def _(t):
                                t.wait_ge(c_g.h, g_entry)
                                c_ev.wait(t)
                                for hp in range(8):
                                    E = heads[hp]
                                    j, rb = hp // 2, 64 * (hp % 2)
                                    if hp >= 1:
                                        t.wait_ge(c_v.h, heads[hp - 1]["att"][3])

                                    def emit_S(n):
                                        m, kb = pairs[n]
                                        off = max(0, (kb - 4 * m) * 128)
                                        if n >= 2:
                                            t.wait_ge(c_s.h, E["exp"][n - 2])
                                        t.matmul(pss[n % 2][:, off:512],
                                                 KTg[rb:rb + 64, j * T + kb * 128:j * T + (kb + 1) * 128],
                                                 QTg[rb:rb + 64, j * T + m * 512 + off:j * T + (m + 1) * 512],
                                                 start=True, stop=True).then_inc(c_mm.h, 1)

                                    def emit_PV(n):
                                        m, kb = pairs[n]
                                        off = max(0, (kb - 4 * m) * 128)
                                        t.wait_ge(c_s.h, E["exp"][n])
                                        if n in E["mask"]:
                                            t.wait_ge(c_v.h, E["mask"][n])
                                        t.matmul(pvs[m][:, off:512],
                                                 Vg[:, kb * 528 + 66 * hp:kb * 528 + 66 * hp + 65],
                                                 PTs[n % 2][:, 0:512 - off],
                                                 start=(kb == 0), stop=(kb == 4 * m + 3)
                                                 ).then_inc(c_mm.h, 1)

                                    emit_S(0)
                                    for n in range(NP):
                                        if n + 1 < NP:
                                            emit_S(n + 1)
                                        emit_PV(n)
                                    for m in range(4):
                                        t.wait_ge(c_g.h, E["rec"][m])
                                        if m >= 1:
                                            t.wait_ge(c_v.h, E["evbc"][m - 1])
                                        t.matmul(psbc[:, :], ones_f[64:65, 0:64],
                                                 rLsb[64:65, 0:512],
                                                 start=True, stop=True).then_inc(c_mm.h, 1)
                            c_mm.n = mm_n[0]

                            @blk.scalar
                            def _(s):
                                for hp in range(8):
                                    E = heads[hp]
                                    for n in range(NP):
                                        m, kb = pairs[n]
                                        off = max(0, (kb - 4 * m) * 128)
                                        w = E["S"][n]
                                        if n >= 2:
                                            w = max(w, E["PV"][n - 2])
                                        s.wait_ge(c_mm.h, w)
                                        s.activation(PTs[n % 2][:, 0:512 - off],
                                                     pss[n % 2][:, off:512], AF.Exp,
                                                     bias=negu[:, hb8 + hp:hb8 + hp + 1],
                                                     scale=0.125).then_inc(c_s.h, 1)
                                    for m in range(4):
                                        last_n = [n for n, (mm_, _kb) in enumerate(pairs) if mm_ == m][-1]
                                        s.wait_ge(c_mm.h, E["PV"][last_n])
                                        if m >= 1:
                                            s.wait_ge(c_g.h, E["rec"][m - 1])
                                            s.wait_ge(c_dm.h, E["dml"][m - 1])
                                        s.copy(Lsb[64:65, 0:512],
                                               pvs[m][64:65, 0:512]).then_inc(c_s.h, 1)
                            c_s.n = s_n[0]

                            @blk.vector
                            def _(v):
                                for hp in range(8):
                                    E = heads[hp]
                                    for n, (m, kb) in enumerate(pairs):
                                        if n in E["mask"]:
                                            v.wait_ge(c_s.h, E["exp"][n])
                                            v.tensor_mul(PTs[n % 2][:, 0:128],
                                                         PTs[n % 2][:, 0:128],
                                                         tri_bf[:]).then_inc(c_v.h, 1)
                                    for m in range(4):
                                        v.wait_ge(c_s.h, E["lc"][m])
                                        if m >= 1:
                                            v.wait_ge(c_mm.h, E["bc"][m - 1])
                                            v.wait_ge(c_dm.h, E["dmr"][m - 1])
                                        v.reciprocal(rLsb[64:65, 0:512],
                                                     Lsb[64:65, 0:512]).then_inc(c_g.h, 1)
                                        v.wait_ge(c_g.h, E["rec"][m])
                                        v.wait_ge(c_mm.h, E["bc"][m])
                                        v.tensor_copy(bcsb[:], psbc[:, :]).then_inc(c_v.h, 1)
                                        v.wait_ge(c_v.h, E["evbc"][m])
                                        if hp >= 2 and m == 0:
                                            v.wait_ge(c_dm.h, heads[hp - 2]["dma"])
                                        v.tensor_mul(ATs[hp % 2][:, m * 512:(m + 1) * 512],
                                                     pvs[m][0:64, 0:512],
                                                     bcsb[:]).then_inc(c_v.h, 1)
                            c_v.n = v_n[0]

                            @blk.sync
                            def _(sy):
                                for hp in range(8):
                                    E = heads[hp]
                                    j, rb = hp // 2, 64 * (hp % 2)
                                    for m in range(4):
                                        sy.wait_ge(c_s.h, E["lc"][m])
                                        c_dm.inc(sy.dma_start(
                                            out=Ldist[16 * hp + 4 * m:16 * hp + 4 * m + 4, :],
                                            in_=Lsb[64:65, 0:512]), 16)
                                        sy.wait_ge(c_g.h, E["rec"][m])
                                        c_dm.inc(sy.dma_start(
                                            out=rLdist[16 * hp + 4 * m:16 * hp + 4 * m + 4, :],
                                            in_=rLsb[64:65, 0:512]), 16)
                                    sy.wait_ge(c_v.h, E["att"][3])
                                    c_dm.inc(sy.dma_start(
                                        out=QTg[rb:rb + 64, j * T:(j + 1) * T],
                                        in_=ATs[hp % 2][:, :]), 16)
                            c_g.n = g_n[0]
                            c_dm.n = dm_n[0]

                        with nc.Block(f"afhn_{tag}{g}") as blk:
                            v0 = c_v.n
                            s0 = c_s.n
                            m0 = c_mm.n
                            vops = []
                            if g == 0:
                                vops += ["col1", "col2", "col3", "col4"]
                            vops += ["E", "ecol", "erow", "st8", "stmax", "st8bf",
                                     "stimcol", "rstim", "sn", "aff", "tg2", "I", "v1",
                                     "w1a", "w1b", "w1c", "vsq", "vcu", "tm3",
                                     "pv1", "mw1", "pI", "mdt", "v2", "c", "cE",
                                     "cE1", "rq", "f"]
                            if g == 0:
                                vops.append("fTcp")
                            vid = {nm: v0 + i + 1 for i, nm in enumerate(vops)}
                            tids = {"trE": m0 + 1, "trS8": m0 + 2, "sel": m0 + 3}
                            if g == 0:
                                tids["trF"] = m0 + 4
                            sid = {"Aabs": s0 + 1, "tg": s0 + 2, "gsig": s0 + 3}
                            ecol = fhn[:, 12 * 128:12 * 128 + 1]
                            st8 = fhn[0:1, 12 * 128 + 8:12 * 128 + 16]
                            stimcol = fhn[:, 12 * 128 + 20:12 * 128 + 21]
                            rstim = fhn[:, 12 * 128 + 21:12 * 128 + 22]
                            erow = fhn[0:1, 13 * 128:14 * 128]

                            @blk.vector
                            def _(v):
                                nv = [v0]

                                def ch(ins):
                                    ins.then_inc(c_v.h, 1)
                                    nv[0] += 1
                                    v.wait_ge(c_v.h, nv[0])
                                c_dm.wait(v)
                                if g == 0:
                                    ch(v.tensor_scalar_mul(fcols[:, 0:1], abdt_c[:, 2:3], 0.08))
                                    ch(v.tensor_mul(fcols[:, 1:2], fcols[:, 0:1], abdt_c[:, 1:2]))
                                    ch(v.tensor_scalar_add(fcols[:, 1:2], fcols[:, 1:2], 1.0))
                                    ch(v.reciprocal(fcols[:, 2:3], fcols[:, 1:2]))
                                ch(v.tensor_mul(fsl(0), Ldist[:], rLdist[:]))
                                ch(v.reduce_max(ecol, fsl(0), axis=AX.X,
                                                apply_absolute_value=True))
                                v.wait_ge(c_mm.h, tids["trE"])
                                ch(v.tensor_copy(erow, pssm[0:1, 0:128]))
                                ch(v.reduce_max(st8,
                                                erow.rearrange("a (h j) -> a h j", j=16),
                                                axis=AX.X))
                                ch(v.tensor_scalar_max(st8, st8, 1e-6))
                                v.wait_ge(c_mm.h, tids["trS8"])
                                ch(v.tensor_copy(sqb[0:8, 0:1], pssm[0:8, 0:1]))
                                v.wait_ge(c_mm.h, tids["sel"])
                                ch(v.tensor_copy(stimcol, pssm[:, 2:3]))
                                ch(v.reciprocal(rstim, stimcol))
                                ch(v.tensor_scalar_mul(fsl(1), fsl(0), rstim))
                                v.wait_ge(c_s.h, sid["Aabs"])
                                ch(v.tensor_scalar(fsl(9), fsl(9), 10.0, -5.0,
                                                   ALU.mult, ALU.add))
                                v.wait_ge(c_s.h, sid["tg"])
                                ch(v.tensor_scalar(fsl(2), fsl(10), 0.9, 0.1,
                                                   ALU.mult, ALU.add))
                                ch(v.tensor_mul(fsl(3), fsl(1), fsl(2)))
                                ch(v.tensor_scalar_mul(fsl(4), fsl(3), abdt_c[:, 2:3]))
                                ch(v.tensor_scalar_add(fsl(5), fsl(4), abdt_c[:, 0:1]))
                                ch(v.tensor_scalar_mul(fsl(5), fsl(5), fcols[:, 0:1]))
                                ch(v.tensor_scalar_mul(fsl(5), fsl(5), fcols[:, 2:3]))
                                ch(v.tensor_mul(fsl(6), fsl(4), fsl(4)))
                                ch(v.tensor_mul(fsl(6), fsl(6), fsl(4)))
                                ch(v.tensor_scalar_mul(fsl(6), fsl(6), -1.0 / 3.0))
                                ch(v.tensor_add(fsl(6), fsl(6), fsl(4)))
                                ch(v.tensor_sub(fsl(6), fsl(6), fsl(5)))
                                ch(v.tensor_add(fsl(6), fsl(6), fsl(3)))
                                ch(v.tensor_scalar_mul(fsl(6), fsl(6), abdt_c[:, 2:3]))
                                ch(v.tensor_add(fsl(7), fsl(6), fsl(4)))
                                v.wait_ge(c_s.h, sid["gsig"])
                                ch(v.tensor_scalar(fsl(8), fsl(11), 0.5, 0.5,
                                                   ALU.mult, ALU.add))
                                ch(v.tensor_mul(fsl(6), fsl(8), fsl(0)))
                                ch(v.tensor_scalar_add(fsl(6), fsl(6), 1e-8))
                                ch(v.reciprocal(fsl(7), fsl(6)))
                                ch(v.tensor_mul(fsl(8), fsl(8), fsl(7)))
                                if g == 0:
                                    v.wait_ge(c_mm.h, tids["trF"])
                                    ch(v.tensor_copy(fT[:], pssm[:, 16:32]))

                            @blk.scalar
                            def _(s):
                                s.wait_ge(c_v.h, vid["E"])
                                s.activation(fsl(9), fsl(0), AF.Abs).then_inc(c_s.h, 1)
                                s.wait_ge(c_v.h, vid["aff"])
                                s.activation(fsl(10), fsl(9), AF.Sigmoid).then_inc(c_s.h, 1)
                                s.wait_ge(c_v.h, vid["v2"])
                                s.activation(fsl(11), fsl(7), AF.Sigmoid).then_inc(c_s.h, 1)

                            @blk.tensor
                            def _(t):
                                t.wait_ge(c_v.h, vid["ecol"])
                                t.transpose(pssm[0:1, 0:128], ecol,
                                            eye_f[:]).then_inc(c_mm.h, 1)
                                t.wait_ge(c_v.h, vid["stmax"])
                                t.transpose(pssm[0:8, 0:1], st8,
                                            eye_f[0:1, 0:1]).then_inc(c_mm.h, 1)
                                t.wait_ge(c_v.h, vid["st8bf"])
                                t.matmul(pssm[:, 2:3], sel16_bf[:, :], sqb[0:8, 0:1],
                                         start=True, stop=True).then_inc(c_mm.h, 1)
                                if g == 0:
                                    t.wait_ge(c_v.h, vid["f"])
                                    t.transpose(pssm[:, 16:32], fhn[0:16, 8 * 128:9 * 128],
                                                eye_f[0:16, 0:16]).then_inc(c_mm.h, 1)
                            c_v.n = v0 + len(vops)
                            c_s.n = s0 + 3
                            c_mm.n = m0 + len(tids)

                    with (
                        nc.psum_tensor(f"ps_x0{tag}{g}", [128, 512], F32) as px0,
                        nc.psum_tensor(f"ps_x1{tag}{g}", [128, 512], F32) as px1,
                        nc.Block(f"awout_{tag}{g}") as blk,
                    ):
                        pxs = [px0, px1]
                        m0 = c_mm.n
                        e0 = c_ev.n
                        s0w = c_s.n
                        v0w = c_v.n

                        def a_ready(i):
                            return e0 + i * 2 + 2

                        @blk.tensor
                        def _(t):
                            c_dm.wait(t)
                            c_v.wait(t)
                            for i in range(NT):
                                for hf in range(2):
                                    gidx = i * 2 + hf
                                    if gidx >= 2:
                                        t.wait_ge(c_ev.h, e0 + gidx - 1)
                                    for j in range(4):
                                        ins = t.matmul(pxs[gidx % 2][:],
                                                       QTg[:, j * T + i * 128:j * T + (i + 1) * 128],
                                                       wog[:, j * 1024 + hf * 512:j * 1024 + hf * 512 + 512],
                                                       start=(j == 0), stop=(j == 3))
                                    ins.then_inc(c_mm.h, 1)
                        c_mm.n = m0 + 32

                        @blk.vector
                        def _(v):
                            for i in range(NT):
                                for hf in range(2):
                                    gidx = i * 2 + hf
                                    v.wait_ge(c_mm.h, m0 + gidx + 1)
                                    v.scalar_tensor_tensor(
                                        X[:, i * D + hf * 512:i * D + hf * 512 + 512],
                                        pxs[gidx % 2][:], fT[:, i:i + 1],
                                        X[:, i * D + hf * 512:i * D + hf * 512 + 512],
                                        ALU.mult, ALU.add).then_inc(c_ev.h, 1)
                            if g == 1 and stats_next:
                                stats_epi_vector(v, a_ready, s0w, v0w)

                        if g == 1 and stats_next:
                            @blk.scalar
                            def _(s):
                                stats_epi_scalar(s, a_ready, s0w, v0w)
                            c_s.n = s0w + 18
                            c_v.n = v0w + 37
                        c_ev.n = e0 + 32


        def emit_attn_b(seq, skip_ln=False, stats_next=False):
            tag = f"b{seq}"
            if not skip_ln:
                emit_ln(tag)
            with contextlib.ExitStack() as bst_:
                KT = bst_.enter_context(nc.sbuf_tensor(f"KTb_{tag}", [128, NC_ * T], BF16))
                Vb = bst_.enter_context(nc.sbuf_tensor(f"Vb_{tag}", [128, 16 * 1028], BF16))
                qbuf = bst_.enter_context(nc.sbuf_tensor(f"qb_{tag}", [128, NC_ * 512], BF16))
                Pb = bst_.enter_context(nc.sbuf_tensor(f"Pb_{tag}", [128, T], BF16))
                PTb = bst_.enter_context(nc.sbuf_tensor(f"PTb_{tag}", [128, T], BF16))
                wk0 = bst_.enter_context(nc.sbuf_tensor(f"wk_{tag}0", [128, 8 * 128], BF16))
                wk1 = bst_.enter_context(nc.sbuf_tensor(f"wk_{tag}1", [128, 8 * 128], BF16))
                wvh = bst_.enter_context(nc.sbuf_tensor(f"wvh_{tag}", [128, 8 * 512], BF16))
                sqb = bst_.enter_context(nc.sbuf_tensor(f"sqbb_{tag}", [128, 512], BF16))
                bvrow = bst_.enter_context(nc.sbuf_tensor(f"bvr_{tag}", [1, D], BF16))
                ubf = bst_.enter_context(nc.sbuf_tensor(f"ubfb_{tag}", [1, 16], BF16))
                ucol = bst_.enter_context(nc.sbuf_tensor(f"ucolb_{tag}", [1, 16], F32))
                rlq = bst_.enter_context(nc.sbuf_tensor(f"rlq_{tag}", [128, 4], F32))

                wks = [wk0, wk1]

                with contextlib.ExitStack() as pstA_:
                    pss0 = pstA_.enter_context(nc.psum_tensor(f"bsA0{tag}", [128, 512], F32))
                    pss1 = pstA_.enter_context(nc.psum_tensor(f"bsA1{tag}", [128, 512], F32))
                    pss2 = pstA_.enter_context(nc.psum_tensor(f"bsA2{tag}", [128, 512], F32))
                    pss3 = pstA_.enter_context(nc.psum_tensor(f"bsA3{tag}", [128, 512], F32))
                    pt0 = pstA_.enter_context(nc.psum_tensor(f"bsyt0{tag}", [128, D], BF16))
                    pt1 = pstA_.enter_context(nc.psum_tensor(f"bsyt1{tag}", [128, D], BF16))
                    pss = [pss0, pss1, pss2, pss3]
                    pts = [pt0, pt1]

                    with nc.Block(f"bproj_{tag}") as blk:
                        w0 = c_w.n
                        m0 = c_mm.n
                        v0y = c_v.n
                        t0y = c_tr.n
                        ev_yt = c_ev.n
                        e0 = ev_yt + NT

                        @blk.gpsimd
                        def _(gp):
                            c_w.inc(gp.dma_start(out=bvrow[:], in_=bv_h), 16)
                            for oc in range(8):
                                if oc >= 2:
                                    gp.wait_ge(c_mm.h, m0 + (oc - 1) * 4)
                                c_w.inc(gp.dma_start(
                                    out=wks[oc % 2][:],
                                    in_=wk_t[:, oc * 1024:(oc + 1) * 1024]), 16)
                            for half in range(2):
                                if half == 1:
                                    gp.wait_ge(c_mm.h, m0 + 32 + 16)
                                c_w.inc(gp.dma_start(
                                    out=wvh[:],
                                    in_=wv_t[:, half * 4096:(half + 1) * 4096]), 16)

                        @blk.tensor
                        def _(t):
                            for i in range(NT):
                                t.wait_ge(c_v.h, v0y + i + 1)
                                if i >= 2:
                                    t.wait_ge(c_ev.h, ev_yt + i - 1)
                                for c in range(NC_):
                                    t.transpose(pts[i % 2][:, c * 128:(c + 1) * 128],
                                                ytok[i % 2][:, c * 128:(c + 1) * 128],
                                                eye_bf[:]).then_inc(c_tr.h, 1)
                            for oc in range(8):
                                t.wait_ge(c_w.h, w0 + (oc + 2) * 16)
                                for m4 in range(4):
                                    gidx = oc * 4 + m4
                                    if gidx >= 4:
                                        t.wait_ge(c_ev.h, e0 + gidx - 3)
                                    t.wait_ge(c_ev.h, ev_yt + min(NT, 4 * (m4 + 1)))
                                    for i in range(NC_):
                                        ins = t.matmul(pss[gidx % 4][:],
                                                       wks[oc % 2][:, i * 128:(i + 1) * 128],
                                                       yt_sl(i, m4 * 512, (m4 + 1) * 512),
                                                       start=(i == 0), stop=(i == NC_ - 1))
                                    ins.then_inc(c_mm.h, 1)
                            c_tr.n = t0y + NT * 8
                            for half in range(2):
                                t.wait_ge(c_w.h, w0 + (10 + half) * 16)
                                for kt in range(NT):
                                    gidx = 32 + half * 16 + kt
                                    t.wait_ge(c_ev.h, e0 + gidx - 3)
                                    for i in range(NC_):
                                        t.matmul(pss[gidx % 4][:],
                                                 yt_sl(i, kt * 128, (kt + 1) * 128),
                                                 wvh[:, i * 512:(i + 1) * 512],
                                                 start=(i == 0), stop=False)
                                    ins = t.matmul(pss[gidx % 4][:], ones_bf[0:1, 0:128],
                                                   bvrow[0:1, half * 512:(half + 1) * 512],
                                                   start=False, stop=True)
                                    ins.then_inc(c_mm.h, 1)
                        c_mm.n = m0 + 64

                        @blk.vector
                        def _(v):
                            v.memset(Vb[:].rearrange("p (k w) -> p k w", w=1028)
                                     [:, :, 1024:1025], 1.0).then_inc(c_g.h, 1)
                            for i in range(NT):
                                if i >= 2:
                                    v.wait_ge(c_tr.h, t0y + (i - 1) * 8)
                                v.tensor_scalar(ytok[i % 2][:], xs(i), mean[:, i:i + 1],
                                                rstd[:, i:i + 1], ALU.subtract,
                                                ALU.mult).then_inc(c_v.h, 1)
                                if i >= 1:
                                    v.wait_ge(c_tr.h, t0y + i * 8)
                                    v.tensor_copy(
                                        YT[:].rearrange("p (c t) -> p c t", t=T)[:, :, (i - 1) * 128:i * 128],
                                        pts[(i - 1) % 2][:].rearrange("p (c t) -> p c t", t=128)).then_inc(c_ev.h, 1)
                            v.wait_ge(c_tr.h, t0y + NT * 8)
                            v.tensor_copy(
                                YT[:].rearrange("p (c t) -> p c t", t=T)[:, :, (NT - 1) * 128:NT * 128],
                                pts[(NT - 1) % 2][:].rearrange("p (c t) -> p c t", t=128)).then_inc(c_ev.h, 1)
                            c_v.n += NT
                            for oc in range(8):
                                for m4 in range(4):
                                    gidx = oc * 4 + m4
                                    v.wait_ge(c_mm.h, m0 + gidx + 1)
                                    v.tensor_scalar_add(
                                        KT[:, oc * T + m4 * 512:oc * T + (m4 + 1) * 512],
                                        pss[gidx % 4][:],
                                        bkt[:, oc:oc + 1]).then_inc(c_ev.h, 1)
                            for half in range(2):
                                for kt in range(NT):
                                    gidx = 32 + half * 16 + kt
                                    v.wait_ge(c_mm.h, m0 + gidx + 1)
                                    v.tensor_copy(
                                        Vb[:, kt * 1028 + half * 512:kt * 1028 + half * 512 + 512],
                                        pss[gidx % 4][:]).then_inc(c_ev.h, 1)
                        c_ev.n = e0 + 64
                        c_g.n += 1

                with contextlib.ExitStack() as pst_:
                    pss0 = pst_.enter_context(nc.psum_tensor(f"bs0{tag}", [128, 512], F32))
                    pss1 = pst_.enter_context(nc.psum_tensor(f"bs1{tag}", [128, 512], F32))
                    pss2 = pst_.enter_context(nc.psum_tensor(f"bs2{tag}", [128, 512], F32))
                    pss3 = pst_.enter_context(nc.psum_tensor(f"bs3{tag}", [128, 512], F32))
                    pv0 = pst_.enter_context(nc.psum_tensor(f"bpv0{tag}", [128, 512], F32))
                    pv1 = pst_.enter_context(nc.psum_tensor(f"bpv1{tag}", [128, 512], F32))
                    pvL = pst_.enter_context(nc.psum_tensor(f"bpvL{tag}", [128, 1], F32))
                    ptr = pst_.enter_context(nc.psum_tensor(f"bptr{tag}", [128, 128], BF16))
                    pss = [pss0, pss1, pss2, pss3]
                    pvs = [pv0, pv1]

                    with nc.Block(f"bknorm_{tag}") as blk:
                        v0 = c_v.n
                        m0 = c_mm.n

                        @blk.vector
                        def _(v):
                            nv = [v0]

                            def ch(ins):
                                ins.then_inc(c_v.h, 1)
                                nv[0] += 1
                                v.wait_ge(c_v.h, nv[0])
                            nm = m0
                            for mc in range(4):
                                for i in range(NC_):
                                    if nm > m0:
                                        v.wait_ge(c_mm.h, nm)
                                    ch(v.tensor_mul(sqb[:],
                                                    KT[:, i * T + mc * 512:i * T + (mc + 1) * 512],
                                                    KT[:, i * T + mc * 512:i * T + (mc + 1) * 512]))
                                    nm += 1
                                v.wait_ge(c_mm.h, nm)
                                ch(v.reduce_max(ucol[0:1, mc:mc + 1], pss0[0:1, 0:512], axis=AX.X))
                            ch(v.reduce_max(ucol[0:1, 4:5], ucol[0:1, 0:4], axis=AX.X))
                        c_v.n = v0 + 4 * 9 + 1

                        @blk.tensor
                        def _(t):
                            nv = v0
                            for mc in range(4):
                                for i in range(NC_):
                                    nv += 1
                                    t.wait_ge(c_v.h, nv)
                                    t.matmul(pss0[0:1, 0:512], ones_bf[:, 0:1], sqb[:],
                                             start=(i == 0), stop=(i == NC_ - 1)
                                             ).then_inc(c_mm.h, 1)
                                nv += 1
                        c_mm.n = m0 + 32

                    for qd in range(4):
                        with nc.Block(f"bflash_{tag}{qd}") as blk:
                            w0 = c_w.n
                            m0 = c_mm.n
                            s0 = c_s.n
                            v0 = c_v.n
                            e0 = c_ev.n
                            g0_ = c_g.n
                            t0_ = c_tr.n

                            id_qproj = {o: m0 + o + 1 for o in range(8)}
                            id_qnorm = {i: m0 + 8 + i + 1 for i in range(NC_)}
                            id_ubc = m0 + 17
                            mmn = m0 + 17
                            trn = t0_
                            sn_ = s0
                            evn = e0 + 8
                            gn_ = g0_
                            tabs = []
                            for s_ in range(4):
                                qt = qd * 4 + s_
                                nb = (qt * 128 + 128 + 511) // 512
                                d = {"S": {}, "exp": {}, "tr": {}, "trev": {}, "pv": None, "rec": None, "stt": {}}
                                for b in range(nb):
                                    d["S"][b] = mmn = mmn + 1
                                for b in range(nb):
                                    d["exp"][b] = sn_ = sn_ + 1
                                for kb in range(qt + 1):
                                    d["tr"][kb] = trn = trn + 1
                                    d["trev"][kb] = evn = evn + 1
                                d["pv"] = mmn = mmn + 1
                                d["rec"] = gn_ = gn_ + 1
                                for hf in range(2):
                                    d["stt"][hf] = evn = evn + 1
                                tabs.append(d)

                            @blk.gpsimd
                            def _(gp):
                                for oc in range(8):
                                    if oc >= 2:
                                        gp.wait_ge(c_mm.h, id_qproj[oc - 2])
                                    c_w.inc(gp.dma_start(
                                        out=wks[oc % 2][:],
                                        in_=wq_t[:, oc * 1024:(oc + 1) * 1024]), 16)

                            @blk.tensor
                            def _(t):
                                for oc in range(8):
                                    t.wait_ge(c_w.h, w0 + (oc + 1) * 16)
                                    if oc >= 2:
                                        t.wait_ge(c_ev.h, e0 + oc - 1)
                                    for i in range(NC_):
                                        ins = t.matmul(pss[oc % 2][:],
                                                       wks[oc % 2][:, i * 128:(i + 1) * 128],
                                                       yt_sl(i, qd * 512, (qd + 1) * 512),
                                                       start=(i == 0), stop=(i == NC_ - 1))
                                    ins.then_inc(c_mm.h, 1)
                                nv = v0
                                for i in range(NC_):
                                    nv += 1
                                    t.wait_ge(c_v.h, nv)
                                    t.matmul(pss2[0:1, 0:512], ones_bf[:, 0:1], sqb[:],
                                             start=(i == 0), stop=(i == NC_ - 1)
                                             ).then_inc(c_mm.h, 1)
                                t.wait_ge(c_v.h, v0 + 8 + 4)
                                t.matmul(pss3[0:1 * 128, 0:1], ones_bf[0:1, 0:128],
                                         ubf[0:1, 0:1], start=True, stop=True).then_inc(c_mm.h, 1)
                                for s_ in range(4):
                                    qt = qd * 4 + s_
                                    d = tabs[s_]
                                    nb = len(d["S"])
                                    for b in range(nb):
                                        if s_ >= 1 and b in tabs[s_ - 1]["exp"]:
                                            t.wait_ge(c_s.h, tabs[s_ - 1]["exp"][b])
                                        elif s_ == 0 and b < 2:
                                            t.wait_ge(c_ev.h, e0 + 8)
                                        elif s_ == 0 and b >= 2:
                                            t.wait_ge(c_v.h, v0 + 13)
                                        hi = min(512, qt * 128 + 128 - b * 512)
                                        for i in range(NC_):
                                            ins = t.matmul(
                                                pss[b][:, 0:hi],
                                                qbuf[:, i * 512 + s_ * 128:i * 512 + (s_ + 1) * 128],
                                                KT[:, i * T + b * 512:i * T + b * 512 + hi],
                                                start=(i == 0), stop=(i == NC_ - 1))
                                        ins.then_inc(c_mm.h, 1)
                                    for kb in range(qt + 1):
                                        t.wait_ge(c_s.h, d["exp"][kb // 4])
                                        if kb == qt:
                                            t.wait_ge(c_v.h, v0 + 13 + s_ + 1)
                                        if kb >= 1:
                                            t.wait_ge(c_ev.h, d["trev"][kb - 1])
                                        elif s_ >= 1:
                                            t.wait_ge(c_ev.h, tabs[s_ - 1]["trev"][qt - 1])
                                        t.transpose(ptr[:, :], Pb[:, kb * 128:(kb + 1) * 128],
                                                    eye_bf[:]).then_inc(c_tr.h, 1)
                                    if s_ >= 1:
                                        t.wait_ge(c_ev.h, tabs[s_ - 1]["stt"][1])
                                    for kb in range(qt + 1):
                                        t.wait_ge(c_ev.h, d["trev"][kb])
                                        t.matmul(pvs[0][:], PTb[:, kb * 128:(kb + 1) * 128],
                                                 Vb[:, kb * 1028:kb * 1028 + 512],
                                                 start=(kb == 0), stop=(kb == qt))
                                        t.matmul(pvs[1][:], PTb[:, kb * 128:(kb + 1) * 128],
                                                 Vb[:, kb * 1028 + 512:kb * 1028 + 1024],
                                                 start=(kb == 0), stop=(kb == qt))
                                        ins = t.matmul(pvL[:], PTb[:, kb * 128:(kb + 1) * 128],
                                                       Vb[:, kb * 1028 + 1024:kb * 1028 + 1025],
                                                       start=(kb == 0), stop=(kb == qt))
                                    ins.then_inc(c_mm.h, 1)
                            c_mm.n = mmn
                            c_tr.n = trn

                            def b_ready(i):
                                if i < qd * 4:
                                    return 0
                                return tabs[i - qd * 4]["stt"][1]

                            @blk.scalar
                            def _(s):
                                for s_ in range(4):
                                    qt = qd * 4 + s_
                                    d = tabs[s_]
                                    for b in range(len(d["S"])):
                                        s.wait_ge(c_mm.h, d["S"][b])
                                        if s_ >= 1:
                                            s.wait_ge(c_tr.h, tabs[s_ - 1]["tr"][qt - 1])
                                        hi = min(512, qt * 128 + 128 - b * 512)
                                        s.activation(Pb[:, b * 512:b * 512 + hi],
                                                     pss[b][:, 0:hi], AF.Exp,
                                                     bias=negu[:, qd:qd + 1],
                                                     scale=1.0 / 32.0).then_inc(c_s.h, 1)
                                if qd == 3 and stats_next:
                                    stats_epi_scalar(s, b_ready, sn_, v0 + 13 + 4)
                            c_s.n = sn_ + (18 if (qd == 3 and stats_next) else 0)

                            @blk.vector
                            def _(v):
                                nv = [v0]

                                def ch(ins):
                                    ins.then_inc(c_v.h, 1)
                                    nv[0] += 1
                                    v.wait_ge(c_v.h, nv[0])
                                for oc in range(8):
                                    v.wait_ge(c_mm.h, id_qproj[oc])
                                    v.tensor_scalar_add(qbuf[:, oc * 512:(oc + 1) * 512],
                                                        pss[oc % 2][:],
                                                        bqt[:, oc:oc + 1]).then_inc(c_ev.h, 1)
                                for i in range(NC_):
                                    if i >= 1:
                                        v.wait_ge(c_mm.h, id_qnorm[i - 1])
                                    ch(v.tensor_mul(sqb[:], qbuf[:, i * 512:(i + 1) * 512],
                                                    qbuf[:, i * 512:(i + 1) * 512]))
                                v.wait_ge(c_mm.h, id_qnorm[NC_ - 1])
                                ch(v.reduce_max(ucol[0:1, 5:6], pss2[0:1, 0:512], axis=AX.X))
                                ch(v.tensor_add(ucol[0:1, 6:7], ucol[0:1, 4:5], ucol[0:1, 5:6]))
                                ch(v.tensor_scalar_mul(ucol[0:1, 7:8], ucol[0:1, 6:7], 1.0 / 64.0))
                                ch(v.tensor_copy(ubf[0:1, 0:1], ucol[0:1, 7:8]))
                                v.wait_ge(c_mm.h, id_ubc)
                                ch(v.tensor_scalar_mul(negu[:, qd:qd + 1], pss3[:, 0:1], -1.0))
                                for s_ in range(4):
                                    qt = qd * 4 + s_
                                    d = tabs[s_]
                                    v.wait_ge(c_s.h, d["exp"][qt // 4])
                                    ch(v.tensor_mul(Pb[:, qt * 128:(qt + 1) * 128],
                                                    Pb[:, qt * 128:(qt + 1) * 128],
                                                    trit_bf[:]))
                                    for kb in range(qt + 1):
                                        v.wait_ge(c_tr.h, d["tr"][kb])
                                        v.tensor_copy(PTb[:, kb * 128:(kb + 1) * 128],
                                                      ptr[:, :]).then_inc(c_ev.h, 1)
                                    v.wait_ge(c_mm.h, d["pv"])
                                    v.reciprocal(rlq[:, s_:s_ + 1], pvL[:, :]).then_inc(c_g.h, 1)
                                    v.wait_ge(c_g.h, d["rec"])
                                    for hf in range(2):
                                        v.scalar_tensor_tensor(
                                            X[:, qt * D + hf * 512:qt * D + hf * 512 + 512],
                                            pvs[hf][:], rlq[:, s_:s_ + 1],
                                            X[:, qt * D + hf * 512:qt * D + hf * 512 + 512],
                                            ALU.mult, ALU.add).then_inc(c_ev.h, 1)
                                if qd == 3 and stats_next:
                                    stats_epi_vector(v, b_ready, sn_, v0 + 13 + 4)
                            c_v.n = v0 + 13 + 4 + (37 if (qd == 3 and stats_next) else 0)
                            c_ev.n = evn
                            c_g.n = gn_

        for seq in range(NSEQ):
            emit_load_x(seq)
            prev_stats = False
            if EMIT["attn_a"]:
                emit_attn_a(seq, stats_next=EMIT["ffn1"])
                prev_stats = EMIT["ffn1"]
            if EMIT["ffn1"]:
                emit_ffn(seq, 1, skip_ln=prev_stats, stats_next=EMIT["attn_b"])
                prev_stats = EMIT["attn_b"]
            if EMIT["attn_b"]:
                emit_attn_b(seq, skip_ln=prev_stats, stats_next=EMIT["ffn2"])
                prev_stats = EMIT["ffn2"]
            if EMIT["ffn2"]:
                emit_ffn(seq, 2, skip_ln=prev_stats)
            emit_store_x(seq)

    return nc


def kernel(L0_attn_a_x, L0_attn_a_Wqkv, L0_attn_a_Wout, L0_attn_a_a, L0_attn_a_b,
           L0_attn_a_dt, L0_attn_b_Wq, L0_attn_b_bq, L0_attn_b_Wk, L0_attn_b_bk,
           L0_attn_b_Wv, L0_attn_b_bv, L0_ffn_w1, L0_ffn_b1, L0_ffn_w2, L0_ffn_b2):
    x = np.ascontiguousarray(np.asarray(L0_attn_a_x, dtype=np.float32))
    B = x.shape[0]
    ncores = 8
    per = B // ncores

    wqkv_t = np.ascontiguousarray(
        np.asarray(L0_attn_a_Wqkv, np.float32).T.reshape(8, 128, 24, 128)
        .transpose(1, 2, 0, 3).reshape(128, 24 * 8 * 128))
    wout_t = np.ascontiguousarray(
        np.asarray(L0_attn_a_Wout, np.float32).T.reshape(8, 128, 1024)
        .transpose(1, 0, 2).reshape(128, 8 * 1024))
    wq_t = np.ascontiguousarray(
        np.asarray(L0_attn_b_Wq, np.float32).T.reshape(8, 128, 8, 128)
        .transpose(1, 2, 0, 3).reshape(128, 8 * 8 * 128))
    wk_t = np.ascontiguousarray(
        np.asarray(L0_attn_b_Wk, np.float32).T.reshape(8, 128, 8, 128)
        .transpose(1, 2, 0, 3).reshape(128, 8 * 8 * 128))
    wv_t = np.ascontiguousarray(
        np.asarray(L0_attn_b_Wv, np.float32).T.reshape(8, 128, 2, 512)
        .transpose(1, 2, 0, 3).reshape(128, 2 * 8 * 512))
    w1 = np.ascontiguousarray(
        np.asarray(L0_ffn_w1, np.float32).reshape(8, 128, 2, 16, 128)
        .transpose(1, 2, 3, 0, 4).reshape(128, 2 * 16 * 8 * 128))
    w2 = np.ascontiguousarray(
        np.asarray(L0_ffn_w2, np.float32).reshape(2, 16, 128, 1024)
        .transpose(2, 0, 1, 3).reshape(128, 2 * 16 * 1024))
    b1r = np.ascontiguousarray(np.asarray(L0_ffn_b1, np.float32).reshape(FF // 128, 128).T)
    b2r = np.ascontiguousarray(np.asarray(L0_ffn_b2, np.float32).reshape(1, D))
    bqr = np.ascontiguousarray(np.asarray(L0_attn_b_bq, np.float32).reshape(NC_, 128).T)
    bkr = np.ascontiguousarray(np.asarray(L0_attn_b_bk, np.float32).reshape(NC_, 128).T)
    bvr = np.ascontiguousarray(np.asarray(L0_attn_b_bv, np.float32).reshape(1, D))
    s_a = np.asarray(L0_attn_a_a, np.float32).reshape(1, 1)
    s_b = np.asarray(L0_attn_a_b, np.float32).reshape(1, 1)
    s_dt = np.asarray(L0_attn_a_dt, np.float32).reshape(1, 1)
    c_eye = np.eye(128, dtype=np.float32)
    kk, qq = np.meshgrid(np.arange(128), np.arange(128), indexing="ij")
    c_tri = (kk <= qq).astype(np.float32)
    c_trit = np.ascontiguousarray(c_tri.T)
    c_ones = np.ones((128, 128), dtype=np.float32)
    c_sel16 = np.zeros((8, 128), dtype=np.float32)
    for hh in range(8):
        c_sel16[hh, hh * 16:(hh + 1) * 16] = 1.0

    nc = build_nc()
    in_maps = []
    for c in range(ncores):
        xc = np.ascontiguousarray(x[c * per:(c + 1) * per].reshape(per * T, D))
        in_maps.append({
            "x": xc, "wqkv_t": wqkv_t, "wout_t": wout_t, "wq_t": wq_t,
            "wk_t": wk_t, "wv_t": wv_t, "w1": w1, "w2": w2, "b1r": b1r,
            "b2r": b2r, "bqr": bqr, "bkr": bkr, "bvr": bvr, "s_a": s_a,
            "s_b": s_b, "s_dt": s_dt, "c_eye": c_eye, "c_tri": c_tri,
            "c_trit": c_trit, "c_ones": c_ones, "c_sel16": c_sel16,
        })
    res = run_bass_kernel_spmd(nc, in_maps, core_ids=list(range(ncores)))
    globals()["LAST_RES"] = res.results
    out = np.concatenate([r["out"].reshape(per, T, D) for r in res.results], axis=0)
    return out.astype(np.float32)


# version 08
# speedup vs baseline: 1.0640x; 1.0640x over previous
import contextlib

import numpy as np

import concourse.bass as bass
import concourse.mybir as mybir
from concourse.bass_utils import run_bass_kernel_spmd

F32 = mybir.dt.float32
BF16 = mybir.dt.bfloat16
AF = mybir.ActivationFunctionType
ALU = mybir.AluOpType
AX = mybir.AxisListType

T = 2048
D = 1024
H = 16
HD = 64
FF = 4096
NT = T // 128
NC_ = D // 128
NSEQ = 2
EPS = 1e-5

EMIT = {"attn_a": True, "ffn1": True, "attn_b": True, "ffn2": True}
FFN_STAGE = 4
DBG = False
YT_STAGE = 3
INTERLEAVE_BC = False


class Ctr:

    def __init__(self, nc, st, name):
        self.h = st.enter_context(nc.semaphore(name))
        self.n = 0

    def inc(self, instr, k=1):
        instr.then_inc(self.h, k)
        self.n += k
        return instr

    def wait(self, eng, lag=0):
        if self.n - lag > 0:
            eng.wait_ge(self.h, self.n - lag)


def build_nc():
    nc = bass.Bass("TRN2", target_bir_lowering=False)

    x_h = nc.dram_tensor("x", [NSEQ * T, D], F32, kind="ExternalInput").ap()
    out_h = nc.dram_tensor("out", [NSEQ * T, D], F32, kind="ExternalOutput").ap()
    wqkv_t = nc.dram_tensor("wqkv_t", [128, 24 * NC_ * 128], F32, kind="ExternalInput").ap()
    wout_t = nc.dram_tensor("wout_t", [128, NC_ * D], F32, kind="ExternalInput").ap()
    wq_t = nc.dram_tensor("wq_t", [128, NC_ * D], F32, kind="ExternalInput").ap()
    wk_t = nc.dram_tensor("wk_t", [128, NC_ * D], F32, kind="ExternalInput").ap()
    wv_t = nc.dram_tensor("wv_t", [128, NC_ * D], F32, kind="ExternalInput").ap()
    w1_h = nc.dram_tensor("w1", [128, NC_ * FF], F32, kind="ExternalInput").ap()
    w2_h = nc.dram_tensor("w2", [128, 32 * D], F32, kind="ExternalInput").ap()
    b1_h = nc.dram_tensor("b1r", [128, FF // 128], F32, kind="ExternalInput").ap()
    b2_h = nc.dram_tensor("b2r", [1, D], F32, kind="ExternalInput").ap()
    bq_h = nc.dram_tensor("bqr", [128, NC_], F32, kind="ExternalInput").ap()
    bk_h = nc.dram_tensor("bkr", [128, NC_], F32, kind="ExternalInput").ap()
    bv_h = nc.dram_tensor("bvr", [1, D], F32, kind="ExternalInput").ap()
    sa_h = nc.dram_tensor("s_a", [1, 1], F32, kind="ExternalInput").ap()
    sb_h = nc.dram_tensor("s_b", [1, 1], F32, kind="ExternalInput").ap()
    sdt_h = nc.dram_tensor("s_dt", [1, 1], F32, kind="ExternalInput").ap()
    eye_h = nc.dram_tensor("c_eye", [128, 128], F32, kind="ExternalInput").ap()
    tri_h = nc.dram_tensor("c_tri", [128, 128], F32, kind="ExternalInput").ap()
    trit_h = nc.dram_tensor("c_trit", [128, 128], F32, kind="ExternalInput").ap()
    ones_h = nc.dram_tensor("c_ones", [128, 128], F32, kind="ExternalInput").ap()
    sel16_h = nc.dram_tensor("c_sel16", [8, 128], F32, kind="ExternalInput").ap()

    if DBG:
        dbg_yt = nc.dram_tensor("dbg_yt", [128, NC_ * T], F32, kind="ExternalOutput").ap()
        dbg_stats = nc.dram_tensor("dbg_stats", [128, 6 * NT], F32, kind="ExternalOutput").ap()
        dbg_h1 = nc.dram_tensor("dbg_h1", [128, 16 * 512], BF16, kind="ExternalOutput").ap()
    st = contextlib.ExitStack()
    with st:
        X = st.enter_context(nc.sbuf_tensor("X", [128, NT * D], F32))
        YT = st.enter_context(nc.sbuf_tensor("YT", [128, NC_ * T], BF16))
        eye_bf = st.enter_context(nc.sbuf_tensor("eye_bf", [128, 128], BF16))
        eye_f = st.enter_context(nc.sbuf_tensor("eye_f", [128, 128], F32))
        tri_bf = st.enter_context(nc.sbuf_tensor("tri_bf", [128, 128], BF16))
        trit_bf = st.enter_context(nc.sbuf_tensor("trit_bf", [128, 128], BF16))
        ones_bf = st.enter_context(nc.sbuf_tensor("ones_bf", [128, 128], BF16))
        ones_f = st.enter_context(nc.sbuf_tensor("ones_f", [128, 128], F32))
        b1t = st.enter_context(nc.sbuf_tensor("b1t", [128, FF // 128], F32))
        bqt = st.enter_context(nc.sbuf_tensor("bqt", [128, NC_], F32))
        bkt = st.enter_context(nc.sbuf_tensor("bkt", [128, NC_], F32))
        s_abdt = st.enter_context(nc.sbuf_tensor("s_abdt", [1, 4], F32))
        sel16_bf = st.enter_context(nc.sbuf_tensor("sel16_bf", [8, 128], BF16))
        abdt_c = st.enter_context(nc.sbuf_tensor("abdt_c", [128, 4], F32))
        negu = st.enter_context(nc.sbuf_tensor("negu", [128, 16], F32))
        fT = st.enter_context(nc.sbuf_tensor("fT", [128, 16], F32))
        sums = st.enter_context(nc.sbuf_tensor("sums", [128, NT], F32))
        sumsq = st.enter_context(nc.sbuf_tensor("sumsq", [128, NT], F32))
        mean = st.enter_context(nc.sbuf_tensor("mean", [128, NT], F32))
        rstd = st.enter_context(nc.sbuf_tensor("rstd", [128, NT], F32))
        st1 = st.enter_context(nc.sbuf_tensor("st1", [128, NT], F32))
        st2 = st.enter_context(nc.sbuf_tensor("st2", [128, NT], F32))
        sq_scr = [st.enter_context(nc.sbuf_tensor(f"sqg{i}", [128, D], BF16)) for i in range(2)]
        ytok = [st.enter_context(nc.sbuf_tensor(f"ytok{i}", [128, D], BF16)) for i in range(2)]

        c_in = Ctr(nc, st, "c_in")
        c_w = Ctr(nc, st, "c_w")
        c_w2 = Ctr(nc, st, "c_w2")
        c_mm = Ctr(nc, st, "c_mm")
        c_v = Ctr(nc, st, "c_v")
        c_s = Ctr(nc, st, "c_s")
        c_st = Ctr(nc, st, "c_st")
        c_g = Ctr(nc, st, "c_g")
        c_tr = Ctr(nc, st, "c_tr")
        c_ev = Ctr(nc, st, "c_ev")
        c_dm = Ctr(nc, st, "c_dm")

        with nc.Block("setup") as blk:
            @blk.sync
            def _(sync):
                c_w2.inc(sync.dma_start(out=eye_f[:], in_=eye_h), 16)
                c_w2.inc(sync.dma_start(out=ones_f[:], in_=ones_h), 16)
                c_w2.inc(sync.dma_start(out=b1t[:], in_=b1_h), 16)
                c_w2.inc(sync.dma_start(out=bqt[:], in_=bq_h), 16)
                c_w2.inc(sync.dma_start(out=bkt[:], in_=bk_h), 16)
                c_w2.inc(sync.dma_start(out=s_abdt[0:1, 0:1], in_=sa_h), 16)
                c_w2.inc(sync.dma_start(out=s_abdt[0:1, 1:2], in_=sb_h), 16)
                c_w2.inc(sync.dma_start(out=s_abdt[0:1, 2:3], in_=sdt_h), 16)

            @blk.gpsimd
            def _(g):
                c_w.inc(g.dma_start(out=eye_bf[:], in_=eye_h), 16)
                c_w.inc(g.dma_start(out=tri_bf[:], in_=tri_h), 16)
                c_w.inc(g.dma_start(out=trit_bf[:], in_=trit_h), 16)
                c_w.inc(g.dma_start(out=ones_bf[:], in_=ones_h), 16)
                c_w.inc(g.dma_start(out=sel16_bf[:], in_=sel16_h), 16)

        with (
            nc.psum_tensor("ps_abdt", [128, 4], F32) as ps_ab,
            nc.Block("abdt") as blk,
        ):
            @blk.tensor
            def _(t):
                c_w2.wait(t)
                c_mm.inc(t.matmul(ps_ab[:, 0:3], ones_f[0:1, 0:128], s_abdt[0:1, 0:3],
                                  start=True, stop=True))
            @blk.vector
            def _(v):
                c_mm.wait(v)
                c_v.inc(v.tensor_copy(abdt_c[:, 0:3], ps_ab[:, 0:3]))


        def xs(i):
            return X[:, i * D:(i + 1) * D]

        def yt_sl(c, lo, hi):
            return YT[:, c * T + lo: c * T + hi]

        def emit_load_x(seq):
            with nc.Block(f"load{seq}") as blk:
                @blk.sync
                def _(sync):
                    c_st.wait(sync)
                    src = x_h[seq * T:(seq + 1) * T, :].rearrange("(i p) f -> p i f", p=128)
                    c_in.inc(sync.dma_start(out=X[:].rearrange("p (i f) -> p i f", f=D), in_=src), 16)

        def emit_store_x(seq):
            with nc.Block(f"store{seq}") as blk:
                @blk.sync
                def _(sync):
                    dst = out_h[seq * T:(seq + 1) * T, :].rearrange("(i p) f -> p i f", p=128)
                    c_st.inc(sync.dma_start(out=dst, in_=X[:].rearrange("p (i f) -> p i f", f=D)), 16)
                    c_st.wait(sync)

        def emit_ln(tag):
            with nc.Block(f"ln_{tag}") as blk:
                s0, v0 = c_s.n, c_v.n

                @blk.scalar
                def _(s):
                    c_in.wait(s)
                    for i in range(NT):
                        if i >= 2:
                            s.wait_ge(c_v.h, v0 + i - 1)
                        c_s.inc(s.activation(sq_scr[i % 2][:], xs(i), AF.Square))

                @blk.vector
                def _(v):
                    c_in.wait(v)
                    for i in range(NT):
                        v.reduce_sum(sums[:, i:i + 1], xs(i), axis=AX.X)
                        v.wait_ge(c_s.h, s0 + i + 1)
                        c_v.inc(v.reduce_sum(sumsq[:, i:i + 1], sq_scr[i % 2][:], axis=AX.X))
                    def ch(ins):
                        ins.then_inc(c_v.h, 1)
                        c_v.n += 1
                        v.wait_ge(c_v.h, c_v.n)
                    v.wait_ge(c_v.h, c_v.n)
                    ch(v.tensor_scalar_mul(mean[:], sums[:], 1.0 / D))
                    ch(v.tensor_mul(st1[:], mean[:], mean[:]))
                    ch(v.tensor_scalar_mul(st2[:], sumsq[:], 1.0 / D))
                    ch(v.tensor_sub(st1[:], st2[:], st1[:]))
                    ch(v.tensor_scalar_add(st1[:], st1[:], EPS))

                @blk.scalar
                def _(s):
                    c_v.wait(s)
                    s.activation(st2[:], st1[:], AF.Ln).then_inc(c_s.h, 1)
                    s.wait_ge(c_s.h, c_s.n + 1)
                    s.activation(rstd[:], st2[:], AF.Exp, scale=-0.5).then_inc(c_s.h, 1)
                    s.wait_ge(c_s.h, c_s.n + 2)
                c_s.n += 2

        def emit_yt(tag):
            with (
                nc.psum_tensor(f"ps_yt0{tag}", [128, D], BF16) as pt0,
                nc.psum_tensor(f"ps_yt1{tag}", [128, D], BF16) as pt1,
                nc.Block(f"yt_{tag}") as blk,
            ):
                pts = [pt0, pt1]
                v0, t0_, e0 = c_v.n, c_tr.n, c_ev.n

                @blk.vector
                def _(v):
                    for i in range(NT):
                        if i >= 2 and YT_STAGE >= 2:
                            v.wait_ge(c_tr.h, t0_ + (i - 1) * 8)
                        v.tensor_scalar(ytok[i % 2][:], xs(i), mean[:, i:i + 1],
                                        rstd[:, i:i + 1], ALU.subtract,
                                        ALU.mult).then_inc(c_v.h, 1)
                        if i >= 1 and YT_STAGE >= 3:
                            v.wait_ge(c_tr.h, t0_ + i * 8)
                            v.tensor_copy(
                                YT[:].rearrange("p (c t) -> p c t", t=T)[:, :, (i - 1) * 128:i * 128],
                                pts[(i - 1) % 2][:].rearrange("p (c t) -> p c t", t=128)).then_inc(c_ev.h, 1)
                    if YT_STAGE >= 3:
                        v.wait_ge(c_tr.h, t0_ + NT * 8)
                        v.tensor_copy(
                            YT[:].rearrange("p (c t) -> p c t", t=T)[:, :, (NT - 1) * 128:NT * 128],
                            pts[(NT - 1) % 2][:].rearrange("p (c t) -> p c t", t=128)).then_inc(c_ev.h, 1)
                c_v.n = v0 + NT
                if YT_STAGE >= 3:
                    c_ev.n = e0 + NT

                @blk.tensor
                def _(t):
                    if YT_STAGE < 2:
                        return
                    c_w.wait(t)
                    for i in range(NT):
                        t.wait_ge(c_v.h, v0 + i + 1)
                        if i >= 2 and YT_STAGE >= 3:
                            t.wait_ge(c_ev.h, e0 + i - 1)
                        for c in range(NC_):
                            t.transpose(pts[i % 2][:, c * 128:(c + 1) * 128],
                                        ytok[i % 2][:, c * 128:(c + 1) * 128],
                                        eye_bf[:]).then_inc(c_tr.h, 1)
                if YT_STAGE >= 2:
                    c_tr.n = t0_ + NT * 8

        def stats_epi_scalar(s, ready, s_base, v_base):
            for h in range(2):
                for k in range(NT // 2):
                    i = h * 8 + k
                    r = ready(i)
                    if r:
                        s.wait_ge(c_ev.h, r)
                    if i >= 2:
                        pi = i - 2
                        s.wait_ge(c_v.h, v_base + (pi // 8) * 21 + (pi % 8) * 2 + 2)
                    s.activation(sq_scr[i % 2][:], xs(i), AF.Square).then_inc(c_s.h, 1)
                s.wait_ge(c_v.h, v_base + h * 21 + 21)
                s.activation(st2[:, h * 8:(h + 1) * 8],
                             st1[:, h * 8:(h + 1) * 8], AF.Ln).then_inc(c_s.h, 1)
                s.wait_ge(c_s.h, s_base + h * 10 + 9)
                s.activation(rstd[:, h * 8:(h + 1) * 8], st2[:, h * 8:(h + 1) * 8],
                             AF.Exp, scale=-0.5).then_inc(c_s.h, 1)
                s.wait_ge(c_s.h, s_base + h * 10 + 10)

        def stats_epi_vector(v, ready, s_base, v_base):
            for h in range(2):
                nv = [v_base + h * 21]

                def ch(ins):
                    ins.then_inc(c_v.h, 1)
                    nv[0] += 1
                    v.wait_ge(c_v.h, nv[0])
                hs = slice(h * 8, (h + 1) * 8)
                for k in range(NT // 2):
                    i = h * 8 + k
                    r = ready(i)
                    if r:
                        v.wait_ge(c_ev.h, r)
                    v.reduce_sum(sums[:, i:i + 1], xs(i), axis=AX.X).then_inc(c_v.h, 1)
                    nv[0] += 1
                    v.wait_ge(c_s.h, s_base + h * 10 + k + 1)
                    v.reduce_sum(sumsq[:, i:i + 1], sq_scr[i % 2][:],
                                 axis=AX.X).then_inc(c_v.h, 1)
                    nv[0] += 1
                v.wait_ge(c_v.h, nv[0])
                ch(v.tensor_scalar_mul(mean[:, hs], sums[:, hs], 1.0 / D))
                ch(v.tensor_mul(st1[:, hs], mean[:, hs], mean[:, hs]))
                ch(v.tensor_scalar_mul(st2[:, hs], sumsq[:, hs], 1.0 / D))
                ch(v.tensor_sub(st1[:, hs], st2[:, hs], st1[:, hs]))
                ch(v.tensor_scalar_add(st1[:, hs], st1[:, hs], EPS))

        def emit_ffn(seq, idx, skip_ln=False, stats_next=False):
            tag = f"f{seq}{idx}"
            if not skip_ln:
                emit_ln(tag)
            with (
                nc.sbuf_tensor(f"w1b_{tag}", [128, NC_ * 16 * 128], BF16) as w1b,
                nc.sbuf_tensor(f"w2b_{tag}", [128, 16 * D], BF16) as w2b,
                nc.sbuf_tensor(f"h1_{tag}", [128, 16 * 512], BF16) as h1,
                nc.sbuf_tensor(f"b2r_{tag}", [1, D], BF16) as b2row,
            ):
                for half in range(2):
                    hb = half * 16
                    with nc.Block(f"ffnw_{tag}{half}") as blk:
                        @blk.gpsimd
                        def _(g):
                            if half == 0:
                                c_w.inc(g.dma_start(out=b2row[:], in_=b2_h), 16)
                            c_mm.wait(g)
                            for pc in range(4):
                                c_w.inc(g.dma_start(
                                    out=w1b[:, pc * 4096:(pc + 1) * 4096],
                                    in_=w1_h[:, half * 16384 + pc * 4096:
                                             half * 16384 + (pc + 1) * 4096]), 16)
                            for pc in range(4):
                                c_w.inc(g.dma_start(
                                    out=w2b[:, pc * 4096:(pc + 1) * 4096],
                                    in_=w2_h[:, half * 16384 + pc * 4096:
                                             half * 16384 + (pc + 1) * 4096]), 16)

                    with (
                        nc.psum_tensor(f"ps_h0{tag}{half}", [128, 512], F32) as ph0,
                        nc.psum_tensor(f"ps_h1{tag}{half}", [128, 512], F32) as ph1,
                        nc.psum_tensor(f"ps_o0{tag}{half}", [128, 512], F32) as po0,
                        nc.psum_tensor(f"ps_o1{tag}{half}", [128, 512], F32) as po1,
                        nc.psum_tensor(f"ps_yt0{tag}{half}", [128, D], BF16) as pt0,
                        nc.psum_tensor(f"ps_yt1{tag}{half}", [128, D], BF16) as pt1,
                        nc.Block(f"ffn_{tag}{half}") as blk,
                    ):
                        phs, pos, pts = [ph0, ph1], [po0, po1], [pt0, pt1]
                        nyt = NT if half == 0 else 0
                        v0, s0, m0, t0_ = c_v.n, c_s.n, c_mm.n, c_tr.n
                        ev_yt = c_ev.n
                        e0 = ev_yt + nyt
                        wbase = c_w.n - 8 * 16

                        def f_ready(i):
                            mt_, s2 = i // 4, i % 4
                            return e0 + mt_ * 8 + s2 * 2 + 2

                        @blk.scalar
                        def _(s):
                            c_w2.wait(s)
                            for mt in range(4):
                                for c in range(16):
                                    s.wait_ge(c_mm.h, m0 + mt * 24 + c + 1)
                                    s.activation(h1[:, c * 512:(c + 1) * 512],
                                                 phs[c % 2][:], AF.Gelu,
                                                 bias=b1t[:, hb + c:hb + c + 1]).then_inc(c_s.h, 1)
                            if half == 1 and stats_next:
                                stats_epi_scalar(s, f_ready, s0 + 64, v0)
                        c_s.n = s0 + 64 + (20 if stats_next else 0) if half == 1 else s0 + 64

                        @blk.vector
                        def _(v):
                            if half == 0:
                                for i in range(NT):
                                    if i >= 2:
                                        v.wait_ge(c_tr.h, t0_ + (i - 1) * 8)
                                    v.tensor_scalar(ytok[i % 2][:], xs(i), mean[:, i:i + 1],
                                                    rstd[:, i:i + 1], ALU.subtract,
                                                    ALU.mult).then_inc(c_v.h, 1)
                                    if i >= 1:
                                        v.wait_ge(c_tr.h, t0_ + i * 8)
                                        v.tensor_copy(
                                            YT[:].rearrange("p (c t) -> p c t", t=T)[:, :, (i - 1) * 128:i * 128],
                                            pts[(i - 1) % 2][:].rearrange("p (c t) -> p c t", t=128)).then_inc(c_ev.h, 1)
                                v.wait_ge(c_tr.h, t0_ + NT * 8)
                                v.tensor_copy(
                                    YT[:].rearrange("p (c t) -> p c t", t=T)[:, :, (NT - 1) * 128:NT * 128],
                                    pts[(NT - 1) % 2][:].rearrange("p (c t) -> p c t", t=128)).then_inc(c_ev.h, 1)
                            for mt in range(4):
                                for j in range(8):
                                    s_, hf = j // 2, j % 2
                                    i = 4 * mt + s_
                                    v.wait_ge(c_mm.h, m0 + mt * 24 + 16 + j + 1)
                                    v.tensor_add(
                                        X[:, i * D + hf * 512: i * D + hf * 512 + 512],
                                        X[:, i * D + hf * 512: i * D + hf * 512 + 512],
                                        pos[j % 2][:]).then_inc(c_ev.h, 1)
                            if half == 1 and stats_next:
                                stats_epi_vector(v, f_ready, s0 + 64, v0)
                        c_v.n = v0 + nyt + ((42 if stats_next else 0) if half == 1 else 0)
                        c_ev.n = e0 + 32

                        @blk.tensor
                        def _(t):
                            if half == 0:
                                for i in range(NT):
                                    t.wait_ge(c_v.h, v0 + i + 1)
                                    if i >= 2:
                                        t.wait_ge(c_ev.h, ev_yt + i - 1)
                                    for c in range(NC_):
                                        t.transpose(pts[i % 2][:, c * 128:(c + 1) * 128],
                                                    ytok[i % 2][:, c * 128:(c + 1) * 128],
                                                    eye_bf[:]).then_inc(c_tr.h, 1)
                            for mt in range(4):
                                if half == 0:
                                    t.wait_ge(c_ev.h, ev_yt + min(NT, 4 * (mt + 1)))
                                for c in range(16):
                                    if c >= 2:
                                        t.wait_ge(c_s.h, s0 + mt * 16 + c - 1)
                                    if mt == 0:
                                        t.wait_ge(c_w.h, wbase + (c // 4 + 1) * 16)
                                    for i in range(NC_):
                                        w_sl = w1b[:, (c * NC_ + i) * 128:
                                                   (c * NC_ + i + 1) * 128]
                                        ins = t.matmul(phs[c % 2][:], w_sl,
                                                 yt_sl(i, mt * 512, (mt + 1) * 512),
                                                 start=(i == 0), stop=(i == NC_ - 1))
                                        if i == NC_ - 1:
                                            ins.then_inc(c_mm.h, 1)
                                t.wait_ge(c_s.h, s0 + (mt + 1) * 16)
                                for j in range(8):
                                    s_, hf = j // 2, j % 2
                                    i = 4 * mt + s_
                                    g_ = mt * 8 + j
                                    if g_ >= 2:
                                        t.wait_ge(c_ev.h, e0 + g_ - 1)
                                    for c in range(16):
                                        if mt == 0 and j == 0 and c % 4 == 0:
                                            t.wait_ge(c_w.h, wbase + (4 + c // 4 + 1) * 16)
                                        h_sl = h1[:, c * 512 + s_ * 128: c * 512 + s_ * 128 + 128]
                                        w_sl = w2b[:, c * D + hf * 512: c * D + hf * 512 + 512]
                                        ins = t.matmul(pos[j % 2][:], h_sl, w_sl,
                                                 start=(c == 0),
                                                 stop=(half == 0 and c == 15))
                                        if half == 0 and c == 15:
                                            ins.then_inc(c_mm.h, 1)
                                    if half == 1:
                                        t.matmul(pos[j % 2][:], ones_bf[0:1, 0:128],
                                                 b2row[0:1, hf * 512:hf * 512 + 512],
                                                 start=False, stop=True).then_inc(c_mm.h, 1)
                        c_mm.n = m0 + 4 * 24
                        if half == 0:
                            c_tr.n = t0_ + NT * 8

        def emit_attn_a(seq, stats_next=False):
            tag = f"a{seq}"
            emit_ln(tag)
            emit_yt(tag)
            with contextlib.ExitStack() as ast_:
                QTg = ast_.enter_context(nc.sbuf_tensor(f"QT_{tag}", [128, 4 * T], BF16))
                KTg = ast_.enter_context(nc.sbuf_tensor(f"KT_{tag}", [128, 4 * T], BF16))
                Vg = ast_.enter_context(nc.sbuf_tensor(f"Vg_{tag}", [128, 16 * 528], BF16))
                wqk0 = ast_.enter_context(nc.sbuf_tensor(f"wqk_{tag}0", [128, 8 * 128], BF16))
                wqk1 = ast_.enter_context(nc.sbuf_tensor(f"wqk_{tag}1", [128, 8 * 128], BF16))
                vrhs = ast_.enter_context(nc.sbuf_tensor(f"vrhs_{tag}", [128, 8 * 512], BF16))
                wog = ast_.enter_context(nc.sbuf_tensor(f"wog_{tag}", [128, 4 * 1024], BF16))
                PT0 = ast_.enter_context(nc.sbuf_tensor(f"PT_{tag}0", [128, 512], BF16))
                PT1 = ast_.enter_context(nc.sbuf_tensor(f"PT_{tag}1", [128, 512], BF16))
                sqb = ast_.enter_context(nc.sbuf_tensor(f"sqb_{tag}", [128, 512], BF16))
                Lsb = ast_.enter_context(nc.sbuf_tensor(f"Lsb_{tag}", [128, 512], F32))
                rLsb = ast_.enter_context(nc.sbuf_tensor(f"rLsb_{tag}", [128, 512], F32))
                ATs0 = ast_.enter_context(nc.sbuf_tensor(f"ATs_{tag}0", [64, T], BF16))
                ATs1 = ast_.enter_context(nc.sbuf_tensor(f"ATs_{tag}1", [64, T], BF16))
                bcsb = ast_.enter_context(nc.sbuf_tensor(f"bcsb_{tag}", [64, 512], F32))
                Ldist = ast_.enter_context(nc.sbuf_tensor(f"Ld_{tag}", [128, 128], F32))
                rLdist = ast_.enter_context(nc.sbuf_tensor(f"rLd_{tag}", [128, 128], F32))
                fhn = ast_.enter_context(nc.sbuf_tensor(f"fhn_{tag}", [128, 14 * 128], F32))
                ucol = ast_.enter_context(nc.sbuf_tensor(f"ucol_{tag}", [1, 16], F32))
                ubf = ast_.enter_context(nc.sbuf_tensor(f"ubf_{tag}", [1, 16], BF16))
                fcols = ast_.enter_context(nc.sbuf_tensor(f"fc_{tag}", [128, 4], F32))
                ATs = [ATs0, ATs1]
                PTs = [PT0, PT1]
                wqks = [wqk0, wqk1]

                def fsl(k):
                    return fhn[:, k * 128:(k + 1) * 128]

                for g in range(2):
                    hb8 = 8 * g

                    with contextlib.ExitStack() as pst_:
                        pss0 = pst_.enter_context(nc.psum_tensor(f"ps_s{tag}{g}", [128, 512], F32))
                        pss1 = pst_.enter_context(nc.psum_tensor(f"ps_t{tag}{g}", [128, 512], F32))
                        pv0 = pst_.enter_context(nc.psum_tensor(f"ps_pv0{tag}{g}", [65, 512], F32))
                        pv1 = pst_.enter_context(nc.psum_tensor(f"ps_pv1{tag}{g}", [65, 512], F32))
                        pv2 = pst_.enter_context(nc.psum_tensor(f"ps_pv2{tag}{g}", [65, 512], F32))
                        pv3 = pst_.enter_context(nc.psum_tensor(f"ps_pv3{tag}{g}", [65, 512], F32))
                        psbc = pst_.enter_context(nc.psum_tensor(f"ps_bc{tag}{g}", [64, 512], F32))
                        pssm = pst_.enter_context(nc.psum_tensor(f"ps_sm{tag}{g}", [128, 128], F32))
                        pss = [pss0, pss1]
                        pvs = [pv0, pv1, pv2, pv3]
                        with nc.Block(f"aproj_{tag}{g}") as blk:
                            w0 = c_w.n
                            m0 = c_mm.n
                            e0 = c_ev.n
                            g0_ = c_g.n

                            @blk.gpsimd
                            def _(gp):
                                for oc in range(8):
                                    if oc >= 2:
                                        gp.wait_ge(c_mm.h, m0 + (oc - 1) * 4)
                                    ocg = (4 * g + oc) if oc < 4 else (8 + 4 * g + (oc - 4))
                                    c_w.inc(gp.dma_start(
                                        out=wqks[oc % 2][:],
                                        in_=wqkv_t[:, ocg * 1024:(ocg + 1) * 1024]), 16)
                                c_w.inc(gp.dma_start(
                                    out=vrhs[:],
                                    in_=wqkv_t[:, (16 + 4 * g) * 1024:(16 + 4 * g + 4) * 1024]), 16)
                                c_w.inc(gp.dma_start(
                                    out=wog[:],
                                    in_=wout_t[:, 4 * g * 1024:(4 * g + 4) * 1024]), 16)

                            @blk.tensor
                            def _(t):
                                nmm = m0
                                for oc in range(8):
                                    t.wait_ge(c_w.h, w0 + (oc + 1) * 16)
                                    for m4 in range(4):
                                        gidx = oc * 4 + m4
                                        if gidx >= 2:
                                            t.wait_ge(c_ev.h, e0 + gidx - 1)
                                        for i in range(NC_):
                                            ins = t.matmul(pss[gidx % 2][:],
                                                           wqks[oc % 2][:, i * 128:(i + 1) * 128],
                                                           yt_sl(i, m4 * 512, (m4 + 1) * 512),
                                                           start=(i == 0), stop=(i == NC_ - 1))
                                        ins.then_inc(c_mm.h, 1)
                                        nmm += 1
                                t.wait_ge(c_w.h, w0 + 9 * 16)
                                for kt in range(NT):
                                    gidx = 32 + kt
                                    t.wait_ge(c_ev.h, e0 + gidx - 1)
                                    for i in range(NC_):
                                        v_rhs = vrhs[:].rearrange("p (a i o) -> p a i o",
                                                                  a=4, i=NC_)[:, :, i, :]
                                        ins = t.matmul(pss[gidx % 2][:],
                                                       yt_sl(i, kt * 128, (kt + 1) * 128),
                                                       v_rhs,
                                                       start=(i == 0), stop=(i == NC_ - 1))
                                    ins.then_inc(c_mm.h, 1)
                                    nmm += 1
                            c_mm.n = m0 + 48

                            @blk.vector
                            def _(v):
                                v.memset(Vg[:].rearrange("p (k h w) -> p k h w", h=8, w=66)
                                         [:, :, :, 64:65], 1.0).then_inc(c_g.h, 1)
                                for oc in range(8):
                                    for m4 in range(4):
                                        gidx = oc * 4 + m4
                                        v.wait_ge(c_mm.h, m0 + gidx + 1)
                                        dst = QTg if oc < 4 else KTg
                                        v.tensor_copy(
                                            dst[:, (oc % 4) * T + m4 * 512:(oc % 4) * T + (m4 + 1) * 512],
                                            pss[gidx % 2][:]).then_inc(c_ev.h, 1)
                                for kt in range(NT):
                                    gidx = 32 + kt
                                    v.wait_ge(c_mm.h, m0 + gidx + 1)
                                    v.tensor_copy(
                                        Vg[:, kt * 528:(kt + 1) * 528]
                                        .rearrange("p (h w) -> p h w", w=66)[:, :, 0:64],
                                        pss[gidx % 2][:].rearrange("p (h w) -> p h w", w=64)
                                    ).then_inc(c_ev.h, 1)
                            c_ev.n = e0 + 48
                            c_g.n = g0_ + 1

                        with nc.Block(f"auph_{tag}{g}") as blk:
                            v0 = c_v.n
                            m0 = c_mm.n

                            @blk.vector
                            def _(v):
                                nv = [v0]

                                def ch(ins):
                                    ins.then_inc(c_v.h, 1)
                                    nv[0] += 1
                                    v.wait_ge(c_v.h, nv[0])
                                nm = [m0]
                                for k in range(8):
                                    srci, j = k // 4, k % 4
                                    srcbuf = QTg if srci == 0 else KTg
                                    for mc in range(4):
                                        kk = k * 4 + mc
                                        if kk >= 1:
                                            v.wait_ge(c_mm.h, m0 + kk)
                                        ch(v.tensor_mul(
                                            sqb[:],
                                            srcbuf[:, j * T + mc * 512:j * T + (mc + 1) * 512],
                                            srcbuf[:, j * T + mc * 512:j * T + (mc + 1) * 512]))
                                        nm[0] += 1
                                        v.wait_ge(c_mm.h, nm[0])
                                        ch(v.reduce_max(ucol[0:1, mc:mc + 1],
                                                        psbc[0:1, 0:512], axis=AX.X))
                                        if mc == 3:
                                            ch(v.reduce_max(ucol[0:1, 4 + k:5 + k],
                                                            ucol[0:1, 0:4], axis=AX.X))
                                for j in range(4):
                                    ch(v.tensor_add(ucol[0:1, 12:13], ucol[0:1, 4 + j:5 + j],
                                                    ucol[0:1, 8 + j:9 + j]))
                                    ch(v.tensor_scalar_mul(ucol[0:1, 13:14],
                                                           ucol[0:1, 12:13], 1.0 / 16.0))
                                    ch(v.tensor_copy(ubf[0:1, 0:1], ucol[0:1, 13:14]))
                                    nm[0] += 1
                                    v.wait_ge(c_mm.h, nm[0])
                                    ch(v.tensor_scalar_mul(
                                        negu[:, hb8 + 2 * j:hb8 + 2 * j + 1],
                                        pssm[:, 0:1], -1.0))
                                    ch(v.tensor_copy(
                                        negu[:, hb8 + 2 * j + 1:hb8 + 2 * j + 2],
                                        negu[:, hb8 + 2 * j:hb8 + 2 * j + 1]))
                                c_v.n = nv[0]

                            @blk.tensor
                            def _(t):
                                nv = v0
                                nm = m0
                                for k in range(8):
                                    for mc in range(4):
                                        nv += 1
                                        t.wait_ge(c_v.h, nv)
                                        t.matmul(psbc[0:1, 0:512], ones_bf[:, 0:1], sqb[:],
                                                 start=True, stop=True).then_inc(c_mm.h, 1)
                                        nm += 1
                                        nv += 1
                                        if mc == 3:
                                            nv += 1
                                for j in range(4):
                                    nv += 3
                                    t.wait_ge(c_v.h, nv)
                                    t.matmul(pssm[:, 0:1], ones_bf[0:1, 0:128],
                                             ubf[0:1, 0:1], start=True,
                                             stop=True).then_inc(c_mm.h, 1)
                                    nm += 1
                                    nv += 2
                                c_mm.n = nm

                        with nc.Block(f"aflash_{tag}{g}") as blk:
                            pairs = []
                            for m in range(4):
                                for kb in range(4 * m + 4):
                                    pairs.append((m, kb))
                            NP = len(pairs)

                            heads = []
                            mm_n = [c_mm.n]
                            s_n = [c_s.n]
                            v_n = [c_v.n]
                            dm_n = [c_dm.n]
                            g_n = [c_g.n]
                            for hp in range(8):
                                ev = {}
                                ids_S = {}
                                ids_PV = {}
                                prev_bc = {}
                                cnt = mm_n[0]
                                ids_S[0] = cnt = cnt + 1
                                for n in range(NP):
                                    if n + 1 < NP:
                                        ids_S[n + 1] = cnt = cnt + 1
                                    if INTERLEAVE_BC and hp >= 1 and 1 <= n <= 4:
                                        prev_bc[n - 1] = cnt = cnt + 1
                                    ids_PV[n] = cnt = cnt + 1
                                if hp == 7 or not INTERLEAVE_BC:
                                    ids_bc = {}
                                    for m in range(4):
                                        ids_bc[m] = cnt = cnt + 1
                                else:
                                    ids_bc = None
                                if INTERLEAVE_BC and hp >= 1:
                                    heads[hp - 1]["bc"] = prev_bc
                                mm_n[0] = cnt
                                ids_exp = {}
                                ids_lc = {}
                                cnt = s_n[0]
                                for n in range(NP):
                                    ids_exp[n] = cnt = cnt + 1
                                for m in range(4):
                                    ids_lc[m] = cnt = cnt + 1
                                s_n[0] = cnt
                                ids_mask = {}
                                cnt = v_n[0]
                                for n, (m, kb) in enumerate(pairs):
                                    if kb >= 4 * m:
                                        ids_mask[n] = cnt = cnt + 1
                                ids_evbc = {}
                                ids_att = {}
                                for m in range(4):
                                    ids_evbc[m] = cnt = cnt + 1
                                    ids_att[m] = cnt = cnt + 1
                                v_n[0] = cnt
                                ids_rec = {}
                                for m in range(4):
                                    ids_rec[m] = g_n[0] = g_n[0] + 1
                                ids_dml = {}
                                ids_dmr = {}
                                for m in range(4):
                                    ids_dml[m] = dm_n[0] = dm_n[0] + 16
                                    ids_dmr[m] = dm_n[0] = dm_n[0] + 16
                                id_dma = dm_n[0] = dm_n[0] + 16
                                heads.append(dict(S=ids_S, PV=ids_PV, bc=ids_bc, exp=ids_exp,
                                                  lc=ids_lc, mask=ids_mask, evbc=ids_evbc,
                                                  att=ids_att, rec=ids_rec, dml=ids_dml,
                                                  dmr=ids_dmr, dma=id_dma))

                            g_entry = c_g.n

                            @blk.tensor
                            def _(t):
                                t.wait_ge(c_g.h, g_entry)
                                c_ev.wait(t)
                                for hp in range(8):
                                    E = heads[hp]
                                    j, rb = hp // 2, 64 * (hp % 2)

                                    def emit_S(n):
                                        m, kb = pairs[n]
                                        off = max(0, (kb - 4 * m) * 128)
                                        if n >= 2:
                                            t.wait_ge(c_s.h, E["exp"][n - 2])
                                        t.matmul(pss[n % 2][:, off:512],
                                                 KTg[rb:rb + 64, j * T + kb * 128:j * T + (kb + 1) * 128],
                                                 QTg[rb:rb + 64, j * T + m * 512 + off:j * T + (m + 1) * 512],
                                                 start=True, stop=True).then_inc(c_mm.h, 1)

                                    def emit_PV(n):
                                        m, kb = pairs[n]
                                        off = max(0, (kb - 4 * m) * 128)
                                        t.wait_ge(c_s.h, E["exp"][n])
                                        if n in E["mask"]:
                                            t.wait_ge(c_v.h, E["mask"][n])
                                        if kb == 0 and hp >= 1:
                                            t.wait_ge(c_v.h, heads[hp - 1]["att"][m] if INTERLEAVE_BC else heads[hp - 1]["att"][3])
                                        t.matmul(pvs[m][:, off:512],
                                                 Vg[:, kb * 528 + 66 * hp:kb * 528 + 66 * hp + 65],
                                                 PTs[n % 2][:, 0:512 - off],
                                                 start=(kb == 0), stop=(kb == 4 * m + 3)
                                                 ).then_inc(c_mm.h, 1)

                                    def emit_bc(pe, m):
                                        t.wait_ge(c_g.h, pe["rec"][m])
                                        if m >= 1:
                                            t.wait_ge(c_v.h, pe["evbc"][m - 1])
                                        t.matmul(psbc[:, :], ones_f[64:65, 0:64],
                                                 rLsb[64:65, 0:512],
                                                 start=True, stop=True).then_inc(c_mm.h, 1)

                                    emit_S(0)
                                    for n in range(NP):
                                        if n + 1 < NP:
                                            emit_S(n + 1)
                                        if INTERLEAVE_BC and hp >= 1 and 1 <= n <= 4:
                                            emit_bc(heads[hp - 1], n - 1)
                                        emit_PV(n)
                                    if hp == 7 or not INTERLEAVE_BC:
                                        for m in range(4):
                                            emit_bc(E, m)
                            c_mm.n = mm_n[0]

                            @blk.scalar
                            def _(s):
                                for hp in range(8):
                                    E = heads[hp]
                                    for n in range(NP):
                                        m, kb = pairs[n]
                                        off = max(0, (kb - 4 * m) * 128)
                                        w = E["S"][n]
                                        if n >= 2:
                                            w = max(w, E["PV"][n - 2])
                                        s.wait_ge(c_mm.h, w)
                                        s.activation(PTs[n % 2][:, 0:512 - off],
                                                     pss[n % 2][:, off:512], AF.Exp,
                                                     bias=negu[:, hb8 + hp:hb8 + hp + 1],
                                                     scale=0.125).then_inc(c_s.h, 1)
                                    for m in range(4):
                                        last_n = [n for n, (mm_, _kb) in enumerate(pairs) if mm_ == m][-1]
                                        s.wait_ge(c_mm.h, E["PV"][last_n])
                                        if m >= 1:
                                            s.wait_ge(c_g.h, E["rec"][m - 1])
                                            s.wait_ge(c_dm.h, E["dml"][m - 1])
                                        s.copy(Lsb[64:65, 0:512],
                                               pvs[m][64:65, 0:512]).then_inc(c_s.h, 1)
                            c_s.n = s_n[0]

                            @blk.vector
                            def _(v):
                                for hp in range(8):
                                    E = heads[hp]
                                    for n, (m, kb) in enumerate(pairs):
                                        if n in E["mask"]:
                                            v.wait_ge(c_s.h, E["exp"][n])
                                            v.tensor_mul(PTs[n % 2][:, 0:128],
                                                         PTs[n % 2][:, 0:128],
                                                         tri_bf[:]).then_inc(c_v.h, 1)
                                    for m in range(4):
                                        v.wait_ge(c_s.h, E["lc"][m])
                                        if m >= 1:
                                            v.wait_ge(c_mm.h, E["bc"][m - 1])
                                            v.wait_ge(c_dm.h, E["dmr"][m - 1])
                                        v.reciprocal(rLsb[64:65, 0:512],
                                                     Lsb[64:65, 0:512]).then_inc(c_g.h, 1)
                                        v.wait_ge(c_g.h, E["rec"][m])
                                        v.wait_ge(c_mm.h, E["bc"][m])
                                        v.tensor_copy(bcsb[:], psbc[:, :]).then_inc(c_v.h, 1)
                                        v.wait_ge(c_v.h, E["evbc"][m])
                                        if hp >= 2 and m == 0:
                                            v.wait_ge(c_dm.h, heads[hp - 2]["dma"])
                                        v.tensor_mul(ATs[hp % 2][:, m * 512:(m + 1) * 512],
                                                     pvs[m][0:64, 0:512],
                                                     bcsb[:]).then_inc(c_v.h, 1)
                            c_v.n = v_n[0]

                            @blk.sync
                            def _(sy):
                                for hp in range(8):
                                    E = heads[hp]
                                    j, rb = hp // 2, 64 * (hp % 2)
                                    for m in range(4):
                                        sy.wait_ge(c_s.h, E["lc"][m])
                                        c_dm.inc(sy.dma_start(
                                            out=Ldist[16 * hp + 4 * m:16 * hp + 4 * m + 4, :],
                                            in_=Lsb[64:65, 0:512]), 16)
                                        sy.wait_ge(c_g.h, E["rec"][m])
                                        c_dm.inc(sy.dma_start(
                                            out=rLdist[16 * hp + 4 * m:16 * hp + 4 * m + 4, :],
                                            in_=rLsb[64:65, 0:512]), 16)
                                    sy.wait_ge(c_v.h, E["att"][3])
                                    c_dm.inc(sy.dma_start(
                                        out=QTg[rb:rb + 64, j * T:(j + 1) * T],
                                        in_=ATs[hp % 2][:, :]), 16)
                            c_g.n = g_n[0]
                            c_dm.n = dm_n[0]

                        with nc.Block(f"afhn_{tag}{g}") as blk:
                            v0 = c_v.n
                            s0 = c_s.n
                            m0 = c_mm.n
                            vops = []
                            if g == 0:
                                vops += ["col1", "col2", "col3", "col4"]
                            vops += ["E", "ecol", "erow", "st8", "stmax", "st8bf",
                                     "stimcol", "rstim", "sn", "aff", "tg2", "I", "v1",
                                     "w1a", "w1b", "w1c", "vsq", "vcu", "tm3",
                                     "pv1", "mw1", "pI", "mdt", "v2", "c", "cE",
                                     "cE1", "rq", "f"]
                            if g == 0:
                                vops.append("fTcp")
                            vid = {nm: v0 + i + 1 for i, nm in enumerate(vops)}
                            tids = {"trE": m0 + 1, "trS8": m0 + 2, "sel": m0 + 3}
                            if g == 0:
                                tids["trF"] = m0 + 4
                            sid = {"Aabs": s0 + 1, "tg": s0 + 2, "gsig": s0 + 3}
                            ecol = fhn[:, 12 * 128:12 * 128 + 1]
                            st8 = fhn[0:1, 12 * 128 + 8:12 * 128 + 16]
                            stimcol = fhn[:, 12 * 128 + 20:12 * 128 + 21]
                            rstim = fhn[:, 12 * 128 + 21:12 * 128 + 22]
                            erow = fhn[0:1, 13 * 128:14 * 128]

                            @blk.vector
                            def _(v):
                                nv = [v0]

                                def ch(ins):
                                    ins.then_inc(c_v.h, 1)
                                    nv[0] += 1
                                    v.wait_ge(c_v.h, nv[0])
                                c_dm.wait(v)
                                if g == 0:
                                    ch(v.tensor_scalar_mul(fcols[:, 0:1], abdt_c[:, 2:3], 0.08))
                                    ch(v.tensor_mul(fcols[:, 1:2], fcols[:, 0:1], abdt_c[:, 1:2]))
                                    ch(v.tensor_scalar_add(fcols[:, 1:2], fcols[:, 1:2], 1.0))
                                    ch(v.reciprocal(fcols[:, 2:3], fcols[:, 1:2]))
                                ch(v.tensor_mul(fsl(0), Ldist[:], rLdist[:]))
                                ch(v.reduce_max(ecol, fsl(0), axis=AX.X,
                                                apply_absolute_value=True))
                                v.wait_ge(c_mm.h, tids["trE"])
                                ch(v.tensor_copy(erow, pssm[0:1, 0:128]))
                                ch(v.reduce_max(st8,
                                                erow.rearrange("a (h j) -> a h j", j=16),
                                                axis=AX.X))
                                ch(v.tensor_scalar_max(st8, st8, 1e-6))
                                v.wait_ge(c_mm.h, tids["trS8"])
                                ch(v.tensor_copy(sqb[0:8, 0:1], pssm[0:8, 0:1]))
                                v.wait_ge(c_mm.h, tids["sel"])
                                ch(v.tensor_copy(stimcol, pssm[:, 2:3]))
                                ch(v.reciprocal(rstim, stimcol))
                                ch(v.tensor_scalar_mul(fsl(1), fsl(0), rstim))
                                v.wait_ge(c_s.h, sid["Aabs"])
                                ch(v.tensor_scalar(fsl(9), fsl(9), 10.0, -5.0,
                                                   ALU.mult, ALU.add))
                                v.wait_ge(c_s.h, sid["tg"])
                                ch(v.tensor_scalar(fsl(2), fsl(10), 0.9, 0.1,
                                                   ALU.mult, ALU.add))
                                ch(v.tensor_mul(fsl(3), fsl(1), fsl(2)))
                                ch(v.tensor_scalar_mul(fsl(4), fsl(3), abdt_c[:, 2:3]))
                                ch(v.tensor_scalar_add(fsl(5), fsl(4), abdt_c[:, 0:1]))
                                ch(v.tensor_scalar_mul(fsl(5), fsl(5), fcols[:, 0:1]))
                                ch(v.tensor_scalar_mul(fsl(5), fsl(5), fcols[:, 2:3]))
                                ch(v.tensor_mul(fsl(6), fsl(4), fsl(4)))
                                ch(v.tensor_mul(fsl(6), fsl(6), fsl(4)))
                                ch(v.tensor_scalar_mul(fsl(6), fsl(6), -1.0 / 3.0))
                                ch(v.tensor_add(fsl(6), fsl(6), fsl(4)))
                                ch(v.tensor_sub(fsl(6), fsl(6), fsl(5)))
                                ch(v.tensor_add(fsl(6), fsl(6), fsl(3)))
                                ch(v.tensor_scalar_mul(fsl(6), fsl(6), abdt_c[:, 2:3]))
                                ch(v.tensor_add(fsl(7), fsl(6), fsl(4)))
                                v.wait_ge(c_s.h, sid["gsig"])
                                ch(v.tensor_scalar(fsl(8), fsl(11), 0.5, 0.5,
                                                   ALU.mult, ALU.add))
                                ch(v.tensor_mul(fsl(6), fsl(8), fsl(0)))
                                ch(v.tensor_scalar_add(fsl(6), fsl(6), 1e-8))
                                ch(v.reciprocal(fsl(7), fsl(6)))
                                ch(v.tensor_mul(fsl(8), fsl(8), fsl(7)))
                                if g == 0:
                                    v.wait_ge(c_mm.h, tids["trF"])
                                    ch(v.tensor_copy(fT[:], pssm[:, 16:32]))

                            @blk.scalar
                            def _(s):
                                s.wait_ge(c_v.h, vid["E"])
                                s.activation(fsl(9), fsl(0), AF.Abs).then_inc(c_s.h, 1)
                                s.wait_ge(c_v.h, vid["aff"])
                                s.activation(fsl(10), fsl(9), AF.Sigmoid).then_inc(c_s.h, 1)
                                s.wait_ge(c_v.h, vid["v2"])
                                s.activation(fsl(11), fsl(7), AF.Sigmoid).then_inc(c_s.h, 1)

                            @blk.tensor
                            def _(t):
                                t.wait_ge(c_v.h, vid["ecol"])
                                t.transpose(pssm[0:1, 0:128], ecol,
                                            eye_f[:]).then_inc(c_mm.h, 1)
                                t.wait_ge(c_v.h, vid["stmax"])
                                t.transpose(pssm[0:8, 0:1], st8,
                                            eye_f[0:1, 0:1]).then_inc(c_mm.h, 1)
                                t.wait_ge(c_v.h, vid["st8bf"])
                                t.matmul(pssm[:, 2:3], sel16_bf[:, :], sqb[0:8, 0:1],
                                         start=True, stop=True).then_inc(c_mm.h, 1)
                                if g == 0:
                                    t.wait_ge(c_v.h, vid["f"])
                                    t.transpose(pssm[:, 16:32], fhn[0:16, 8 * 128:9 * 128],
                                                eye_f[0:16, 0:16]).then_inc(c_mm.h, 1)
                            c_v.n = v0 + len(vops)
                            c_s.n = s0 + 3
                            c_mm.n = m0 + len(tids)

                    with (
                        nc.psum_tensor(f"ps_x0{tag}{g}", [128, 512], F32) as px0,
                        nc.psum_tensor(f"ps_x1{tag}{g}", [128, 512], F32) as px1,
                        nc.Block(f"awout_{tag}{g}") as blk,
                    ):
                        pxs = [px0, px1]
                        m0 = c_mm.n
                        e0 = c_ev.n
                        s0w = c_s.n
                        v0w = c_v.n

                        def a_ready(i):
                            return e0 + i * 2 + 2

                        @blk.tensor
                        def _(t):
                            c_dm.wait(t)
                            c_v.wait(t)
                            for i in range(NT):
                                for hf in range(2):
                                    gidx = i * 2 + hf
                                    if gidx >= 2:
                                        t.wait_ge(c_ev.h, e0 + gidx - 1)
                                    for j in range(4):
                                        ins = t.matmul(pxs[gidx % 2][:],
                                                       QTg[:, j * T + i * 128:j * T + (i + 1) * 128],
                                                       wog[:, j * 1024 + hf * 512:j * 1024 + hf * 512 + 512],
                                                       start=(j == 0), stop=(j == 3))
                                    ins.then_inc(c_mm.h, 1)
                        c_mm.n = m0 + 32

                        @blk.vector
                        def _(v):
                            for i in range(NT):
                                for hf in range(2):
                                    gidx = i * 2 + hf
                                    v.wait_ge(c_mm.h, m0 + gidx + 1)
                                    v.scalar_tensor_tensor(
                                        X[:, i * D + hf * 512:i * D + hf * 512 + 512],
                                        pxs[gidx % 2][:], fT[:, i:i + 1],
                                        X[:, i * D + hf * 512:i * D + hf * 512 + 512],
                                        ALU.mult, ALU.add).then_inc(c_ev.h, 1)
                            if g == 1 and stats_next:
                                stats_epi_vector(v, a_ready, s0w, v0w)

                        if g == 1 and stats_next:
                            @blk.scalar
                            def _(s):
                                stats_epi_scalar(s, a_ready, s0w, v0w)
                            c_s.n = s0w + 20
                            c_v.n = v0w + 42
                        c_ev.n = e0 + 32


        def emit_attn_b(seq, skip_ln=False, stats_next=False):
            tag = f"b{seq}"
            if not skip_ln:
                emit_ln(tag)
            with contextlib.ExitStack() as bst_:
                KT = bst_.enter_context(nc.sbuf_tensor(f"KTb_{tag}", [128, NC_ * T], BF16))
                Vb = bst_.enter_context(nc.sbuf_tensor(f"Vb_{tag}", [128, 16 * 1028], BF16))
                qbuf = bst_.enter_context(nc.sbuf_tensor(f"qb_{tag}", [128, NC_ * 512], BF16))
                Pb = bst_.enter_context(nc.sbuf_tensor(f"Pb_{tag}", [128, T], BF16))
                PTb = bst_.enter_context(nc.sbuf_tensor(f"PTb_{tag}", [128, T], BF16))
                wk0 = bst_.enter_context(nc.sbuf_tensor(f"wk_{tag}0", [128, 8 * 128], BF16))
                wk1 = bst_.enter_context(nc.sbuf_tensor(f"wk_{tag}1", [128, 8 * 128], BF16))
                wvh = bst_.enter_context(nc.sbuf_tensor(f"wvh_{tag}", [128, 8 * 512], BF16))
                sqb = bst_.enter_context(nc.sbuf_tensor(f"sqbb_{tag}", [128, 512], BF16))
                bvrow = bst_.enter_context(nc.sbuf_tensor(f"bvr_{tag}", [1, D], BF16))
                ubf = bst_.enter_context(nc.sbuf_tensor(f"ubfb_{tag}", [1, 16], BF16))
                ucol = bst_.enter_context(nc.sbuf_tensor(f"ucolb_{tag}", [1, 16], F32))
                rlq = bst_.enter_context(nc.sbuf_tensor(f"rlq_{tag}", [128, 4], F32))

                wks = [wk0, wk1]

                with contextlib.ExitStack() as pstA_:
                    pss0 = pstA_.enter_context(nc.psum_tensor(f"bsA0{tag}", [128, 512], F32))
                    pss1 = pstA_.enter_context(nc.psum_tensor(f"bsA1{tag}", [128, 512], F32))
                    pss2 = pstA_.enter_context(nc.psum_tensor(f"bsA2{tag}", [128, 512], F32))
                    pss3 = pstA_.enter_context(nc.psum_tensor(f"bsA3{tag}", [128, 512], F32))
                    pt0 = pstA_.enter_context(nc.psum_tensor(f"bsyt0{tag}", [128, D], BF16))
                    pt1 = pstA_.enter_context(nc.psum_tensor(f"bsyt1{tag}", [128, D], BF16))
                    pss = [pss0, pss1, pss2, pss3]
                    pts = [pt0, pt1]

                    with nc.Block(f"bproj_{tag}") as blk:
                        w0 = c_w.n
                        m0 = c_mm.n
                        v0y = c_v.n
                        t0y = c_tr.n
                        ev_yt = c_ev.n
                        e0 = ev_yt + NT

                        @blk.gpsimd
                        def _(gp):
                            c_w.inc(gp.dma_start(out=bvrow[:], in_=bv_h), 16)
                            for oc in range(8):
                                if oc >= 2:
                                    gp.wait_ge(c_mm.h, m0 + (oc - 1) * 4)
                                c_w.inc(gp.dma_start(
                                    out=wks[oc % 2][:],
                                    in_=wk_t[:, oc * 1024:(oc + 1) * 1024]), 16)
                            for half in range(2):
                                if half == 1:
                                    gp.wait_ge(c_mm.h, m0 + 32 + 16)
                                c_w.inc(gp.dma_start(
                                    out=wvh[:],
                                    in_=wv_t[:, half * 4096:(half + 1) * 4096]), 16)

                        @blk.tensor
                        def _(t):
                            for i in range(NT):
                                t.wait_ge(c_v.h, v0y + i + 1)
                                if i >= 2:
                                    t.wait_ge(c_ev.h, ev_yt + i - 1)
                                for c in range(NC_):
                                    t.transpose(pts[i % 2][:, c * 128:(c + 1) * 128],
                                                ytok[i % 2][:, c * 128:(c + 1) * 128],
                                                eye_bf[:]).then_inc(c_tr.h, 1)
                            for oc in range(8):
                                t.wait_ge(c_w.h, w0 + (oc + 2) * 16)
                                for m4 in range(4):
                                    gidx = oc * 4 + m4
                                    if gidx >= 4:
                                        t.wait_ge(c_ev.h, e0 + gidx - 3)
                                    t.wait_ge(c_ev.h, ev_yt + min(NT, 4 * (m4 + 1)))
                                    for i in range(NC_):
                                        ins = t.matmul(pss[gidx % 4][:],
                                                       wks[oc % 2][:, i * 128:(i + 1) * 128],
                                                       yt_sl(i, m4 * 512, (m4 + 1) * 512),
                                                       start=(i == 0), stop=(i == NC_ - 1))
                                    ins.then_inc(c_mm.h, 1)
                            c_tr.n = t0y + NT * 8
                            for half in range(2):
                                t.wait_ge(c_w.h, w0 + (10 + half) * 16)
                                for kt in range(NT):
                                    gidx = 32 + half * 16 + kt
                                    t.wait_ge(c_ev.h, e0 + gidx - 3)
                                    for i in range(NC_):
                                        t.matmul(pss[gidx % 4][:],
                                                 yt_sl(i, kt * 128, (kt + 1) * 128),
                                                 wvh[:, i * 512:(i + 1) * 512],
                                                 start=(i == 0), stop=False)
                                    ins = t.matmul(pss[gidx % 4][:], ones_bf[0:1, 0:128],
                                                   bvrow[0:1, half * 512:(half + 1) * 512],
                                                   start=False, stop=True)
                                    ins.then_inc(c_mm.h, 1)
                        c_mm.n = m0 + 64

                        @blk.vector
                        def _(v):
                            v.memset(Vb[:].rearrange("p (k w) -> p k w", w=1028)
                                     [:, :, 1024:1025], 1.0).then_inc(c_g.h, 1)
                            for i in range(NT):
                                if i >= 2:
                                    v.wait_ge(c_tr.h, t0y + (i - 1) * 8)
                                v.tensor_scalar(ytok[i % 2][:], xs(i), mean[:, i:i + 1],
                                                rstd[:, i:i + 1], ALU.subtract,
                                                ALU.mult).then_inc(c_v.h, 1)
                                if i >= 1:
                                    v.wait_ge(c_tr.h, t0y + i * 8)
                                    v.tensor_copy(
                                        YT[:].rearrange("p (c t) -> p c t", t=T)[:, :, (i - 1) * 128:i * 128],
                                        pts[(i - 1) % 2][:].rearrange("p (c t) -> p c t", t=128)).then_inc(c_ev.h, 1)
                            v.wait_ge(c_tr.h, t0y + NT * 8)
                            v.tensor_copy(
                                YT[:].rearrange("p (c t) -> p c t", t=T)[:, :, (NT - 1) * 128:NT * 128],
                                pts[(NT - 1) % 2][:].rearrange("p (c t) -> p c t", t=128)).then_inc(c_ev.h, 1)
                            c_v.n += NT
                            for oc in range(8):
                                for m4 in range(4):
                                    gidx = oc * 4 + m4
                                    v.wait_ge(c_mm.h, m0 + gidx + 1)
                                    v.tensor_scalar_add(
                                        KT[:, oc * T + m4 * 512:oc * T + (m4 + 1) * 512],
                                        pss[gidx % 4][:],
                                        bkt[:, oc:oc + 1]).then_inc(c_ev.h, 1)
                            for half in range(2):
                                for kt in range(NT):
                                    gidx = 32 + half * 16 + kt
                                    v.wait_ge(c_mm.h, m0 + gidx + 1)
                                    v.tensor_copy(
                                        Vb[:, kt * 1028 + half * 512:kt * 1028 + half * 512 + 512],
                                        pss[gidx % 4][:]).then_inc(c_ev.h, 1)
                        c_ev.n = e0 + 64
                        c_g.n += 1

                with contextlib.ExitStack() as pst_:
                    pss0 = pst_.enter_context(nc.psum_tensor(f"bs0{tag}", [128, 512], F32))
                    pss1 = pst_.enter_context(nc.psum_tensor(f"bs1{tag}", [128, 512], F32))
                    pss2 = pst_.enter_context(nc.psum_tensor(f"bs2{tag}", [128, 512], F32))
                    pss3 = pst_.enter_context(nc.psum_tensor(f"bs3{tag}", [128, 512], F32))
                    pv0 = pst_.enter_context(nc.psum_tensor(f"bpv0{tag}", [128, 512], F32))
                    pv1 = pst_.enter_context(nc.psum_tensor(f"bpv1{tag}", [128, 512], F32))
                    pvL = pst_.enter_context(nc.psum_tensor(f"bpvL{tag}", [128, 1], F32))
                    ptr = pst_.enter_context(nc.psum_tensor(f"bptr{tag}", [128, 128], BF16))
                    pss = [pss0, pss1, pss2, pss3]
                    pvs = [pv0, pv1]

                    with nc.Block(f"bknorm_{tag}") as blk:
                        v0 = c_v.n
                        m0 = c_mm.n

                        @blk.vector
                        def _(v):
                            nv = [v0]

                            def ch(ins):
                                ins.then_inc(c_v.h, 1)
                                nv[0] += 1
                                v.wait_ge(c_v.h, nv[0])
                            nm = m0
                            for mc in range(4):
                                for i in range(NC_):
                                    if nm > m0:
                                        v.wait_ge(c_mm.h, nm)
                                    ch(v.tensor_mul(sqb[:],
                                                    KT[:, i * T + mc * 512:i * T + (mc + 1) * 512],
                                                    KT[:, i * T + mc * 512:i * T + (mc + 1) * 512]))
                                    nm += 1
                                v.wait_ge(c_mm.h, nm)
                                ch(v.reduce_max(ucol[0:1, mc:mc + 1], pss0[0:1, 0:512], axis=AX.X))
                            ch(v.reduce_max(ucol[0:1, 4:5], ucol[0:1, 0:4], axis=AX.X))
                        c_v.n = v0 + 4 * 9 + 1

                        @blk.tensor
                        def _(t):
                            nv = v0
                            for mc in range(4):
                                for i in range(NC_):
                                    nv += 1
                                    t.wait_ge(c_v.h, nv)
                                    t.matmul(pss0[0:1, 0:512], ones_bf[:, 0:1], sqb[:],
                                             start=(i == 0), stop=(i == NC_ - 1)
                                             ).then_inc(c_mm.h, 1)
                                nv += 1
                        c_mm.n = m0 + 32

                    for qd in range(4):
                        with nc.Block(f"bflash_{tag}{qd}") as blk:
                            w0 = c_w.n
                            m0 = c_mm.n
                            s0 = c_s.n
                            v0 = c_v.n
                            e0 = c_ev.n
                            g0_ = c_g.n
                            t0_ = c_tr.n

                            id_qproj = {o: m0 + o + 1 for o in range(8)}
                            id_qnorm = {i: m0 + 8 + i + 1 for i in range(NC_)}
                            id_ubc = m0 + 17
                            mmn = m0 + 17
                            trn = t0_
                            sn_ = s0
                            evn = e0 + 8
                            gn_ = g0_
                            tabs = []
                            for s_ in range(4):
                                qt = qd * 4 + s_
                                nb = (qt * 128 + 128 + 511) // 512
                                d = {"S": {}, "exp": {}, "tr": {}, "trev": {}, "pv": None, "rec": None, "stt": {}}
                                for b in range(nb):
                                    d["S"][b] = mmn = mmn + 1
                                for b in range(nb):
                                    d["exp"][b] = sn_ = sn_ + 1
                                for kb in range(qt + 1):
                                    d["tr"][kb] = trn = trn + 1
                                    d["trev"][kb] = evn = evn + 1
                                d["pv"] = mmn = mmn + 1
                                d["rec"] = gn_ = gn_ + 1
                                for hf in range(2):
                                    d["stt"][hf] = evn = evn + 1
                                tabs.append(d)

                            @blk.gpsimd
                            def _(gp):
                                for oc in range(8):
                                    if oc >= 2:
                                        gp.wait_ge(c_mm.h, id_qproj[oc - 2])
                                    c_w.inc(gp.dma_start(
                                        out=wks[oc % 2][:],
                                        in_=wq_t[:, oc * 1024:(oc + 1) * 1024]), 16)

                            @blk.tensor
                            def _(t):
                                for oc in range(8):
                                    t.wait_ge(c_w.h, w0 + (oc + 1) * 16)
                                    if oc >= 2:
                                        t.wait_ge(c_ev.h, e0 + oc - 1)
                                    for i in range(NC_):
                                        ins = t.matmul(pss[oc % 2][:],
                                                       wks[oc % 2][:, i * 128:(i + 1) * 128],
                                                       yt_sl(i, qd * 512, (qd + 1) * 512),
                                                       start=(i == 0), stop=(i == NC_ - 1))
                                    ins.then_inc(c_mm.h, 1)
                                nv = v0
                                for i in range(NC_):
                                    nv += 1
                                    t.wait_ge(c_v.h, nv)
                                    t.matmul(pss2[0:1, 0:512], ones_bf[:, 0:1], sqb[:],
                                             start=(i == 0), stop=(i == NC_ - 1)
                                             ).then_inc(c_mm.h, 1)
                                t.wait_ge(c_v.h, v0 + 8 + 4)
                                t.matmul(pss3[0:1 * 128, 0:1], ones_bf[0:1, 0:128],
                                         ubf[0:1, 0:1], start=True, stop=True).then_inc(c_mm.h, 1)
                                for s_ in range(4):
                                    qt = qd * 4 + s_
                                    d = tabs[s_]
                                    nb = len(d["S"])
                                    for b in range(nb):
                                        if s_ >= 1 and b in tabs[s_ - 1]["exp"]:
                                            t.wait_ge(c_s.h, tabs[s_ - 1]["exp"][b])
                                        elif s_ == 0 and b < 2:
                                            t.wait_ge(c_ev.h, e0 + 8)
                                        elif s_ == 0 and b >= 2:
                                            t.wait_ge(c_v.h, v0 + 13)
                                        hi = min(512, qt * 128 + 128 - b * 512)
                                        for i in range(NC_):
                                            ins = t.matmul(
                                                pss[b][:, 0:hi],
                                                qbuf[:, i * 512 + s_ * 128:i * 512 + (s_ + 1) * 128],
                                                KT[:, i * T + b * 512:i * T + b * 512 + hi],
                                                start=(i == 0), stop=(i == NC_ - 1))
                                        ins.then_inc(c_mm.h, 1)
                                    for kb in range(qt + 1):
                                        t.wait_ge(c_s.h, d["exp"][kb // 4])
                                        if kb == qt:
                                            t.wait_ge(c_v.h, v0 + 13 + s_ + 1)
                                        if kb >= 1:
                                            t.wait_ge(c_ev.h, d["trev"][kb - 1])
                                        elif s_ >= 1:
                                            t.wait_ge(c_ev.h, tabs[s_ - 1]["trev"][qt - 1])
                                        t.transpose(ptr[:, :], Pb[:, kb * 128:(kb + 1) * 128],
                                                    eye_bf[:]).then_inc(c_tr.h, 1)
                                    if s_ >= 1:
                                        t.wait_ge(c_ev.h, tabs[s_ - 1]["stt"][1])
                                    for kb in range(qt + 1):
                                        t.wait_ge(c_ev.h, d["trev"][kb])
                                        t.matmul(pvs[0][:], PTb[:, kb * 128:(kb + 1) * 128],
                                                 Vb[:, kb * 1028:kb * 1028 + 512],
                                                 start=(kb == 0), stop=(kb == qt))
                                        t.matmul(pvs[1][:], PTb[:, kb * 128:(kb + 1) * 128],
                                                 Vb[:, kb * 1028 + 512:kb * 1028 + 1024],
                                                 start=(kb == 0), stop=(kb == qt))
                                        ins = t.matmul(pvL[:], PTb[:, kb * 128:(kb + 1) * 128],
                                                       Vb[:, kb * 1028 + 1024:kb * 1028 + 1025],
                                                       start=(kb == 0), stop=(kb == qt))
                                    ins.then_inc(c_mm.h, 1)
                            c_mm.n = mmn
                            c_tr.n = trn

                            def b_ready(i):
                                if i < qd * 4:
                                    return 0
                                return tabs[i - qd * 4]["stt"][1]

                            @blk.scalar
                            def _(s):
                                for s_ in range(4):
                                    qt = qd * 4 + s_
                                    d = tabs[s_]
                                    for b in range(len(d["S"])):
                                        s.wait_ge(c_mm.h, d["S"][b])
                                        if s_ >= 1:
                                            s.wait_ge(c_tr.h, tabs[s_ - 1]["tr"][qt - 1])
                                        hi = min(512, qt * 128 + 128 - b * 512)
                                        s.activation(Pb[:, b * 512:b * 512 + hi],
                                                     pss[b][:, 0:hi], AF.Exp,
                                                     bias=negu[:, qd:qd + 1],
                                                     scale=1.0 / 32.0).then_inc(c_s.h, 1)
                                if qd == 3 and stats_next:
                                    stats_epi_scalar(s, b_ready, sn_, v0 + 13 + 4)
                            c_s.n = sn_ + (20 if (qd == 3 and stats_next) else 0)

                            @blk.vector
                            def _(v):
                                nv = [v0]

                                def ch(ins):
                                    ins.then_inc(c_v.h, 1)
                                    nv[0] += 1
                                    v.wait_ge(c_v.h, nv[0])
                                for oc in range(8):
                                    v.wait_ge(c_mm.h, id_qproj[oc])
                                    v.tensor_scalar_add(qbuf[:, oc * 512:(oc + 1) * 512],
                                                        pss[oc % 2][:],
                                                        bqt[:, oc:oc + 1]).then_inc(c_ev.h, 1)
                                for i in range(NC_):
                                    if i >= 1:
                                        v.wait_ge(c_mm.h, id_qnorm[i - 1])
                                    ch(v.tensor_mul(sqb[:], qbuf[:, i * 512:(i + 1) * 512],
                                                    qbuf[:, i * 512:(i + 1) * 512]))
                                v.wait_ge(c_mm.h, id_qnorm[NC_ - 1])
                                ch(v.reduce_max(ucol[0:1, 5:6], pss2[0:1, 0:512], axis=AX.X))
                                ch(v.tensor_add(ucol[0:1, 6:7], ucol[0:1, 4:5], ucol[0:1, 5:6]))
                                ch(v.tensor_scalar_mul(ucol[0:1, 7:8], ucol[0:1, 6:7], 1.0 / 64.0))
                                ch(v.tensor_copy(ubf[0:1, 0:1], ucol[0:1, 7:8]))
                                v.wait_ge(c_mm.h, id_ubc)
                                ch(v.tensor_scalar_mul(negu[:, qd:qd + 1], pss3[:, 0:1], -1.0))
                                for s_ in range(4):
                                    qt = qd * 4 + s_
                                    d = tabs[s_]
                                    v.wait_ge(c_s.h, d["exp"][qt // 4])
                                    ch(v.tensor_mul(Pb[:, qt * 128:(qt + 1) * 128],
                                                    Pb[:, qt * 128:(qt + 1) * 128],
                                                    trit_bf[:]))
                                    for kb in range(qt + 1):
                                        v.wait_ge(c_tr.h, d["tr"][kb])
                                        v.tensor_copy(PTb[:, kb * 128:(kb + 1) * 128],
                                                      ptr[:, :]).then_inc(c_ev.h, 1)
                                    v.wait_ge(c_mm.h, d["pv"])
                                    v.reciprocal(rlq[:, s_:s_ + 1], pvL[:, :]).then_inc(c_g.h, 1)
                                    v.wait_ge(c_g.h, d["rec"])
                                    for hf in range(2):
                                        v.scalar_tensor_tensor(
                                            X[:, qt * D + hf * 512:qt * D + hf * 512 + 512],
                                            pvs[hf][:], rlq[:, s_:s_ + 1],
                                            X[:, qt * D + hf * 512:qt * D + hf * 512 + 512],
                                            ALU.mult, ALU.add).then_inc(c_ev.h, 1)
                                if qd == 3 and stats_next:
                                    stats_epi_vector(v, b_ready, sn_, v0 + 13 + 4)
                            c_v.n = v0 + 13 + 4 + (42 if (qd == 3 and stats_next) else 0)
                            c_ev.n = evn
                            c_g.n = gn_

        for seq in range(NSEQ):
            emit_load_x(seq)
            prev_stats = False
            if EMIT["attn_a"]:
                emit_attn_a(seq, stats_next=EMIT["ffn1"])
                prev_stats = EMIT["ffn1"]
            if EMIT["ffn1"]:
                emit_ffn(seq, 1, skip_ln=prev_stats, stats_next=EMIT["attn_b"])
                prev_stats = EMIT["attn_b"]
            if EMIT["attn_b"]:
                emit_attn_b(seq, skip_ln=prev_stats, stats_next=EMIT["ffn2"])
                prev_stats = EMIT["ffn2"]
            if EMIT["ffn2"]:
                emit_ffn(seq, 2, skip_ln=prev_stats)
            emit_store_x(seq)

    return nc


def kernel(L0_attn_a_x, L0_attn_a_Wqkv, L0_attn_a_Wout, L0_attn_a_a, L0_attn_a_b,
           L0_attn_a_dt, L0_attn_b_Wq, L0_attn_b_bq, L0_attn_b_Wk, L0_attn_b_bk,
           L0_attn_b_Wv, L0_attn_b_bv, L0_ffn_w1, L0_ffn_b1, L0_ffn_w2, L0_ffn_b2):
    x = np.ascontiguousarray(np.asarray(L0_attn_a_x, dtype=np.float32))
    B = x.shape[0]
    ncores = 8
    per = B // ncores

    wqkv_t = np.ascontiguousarray(
        np.asarray(L0_attn_a_Wqkv, np.float32).T.reshape(8, 128, 24, 128)
        .transpose(1, 2, 0, 3).reshape(128, 24 * 8 * 128))
    wout_t = np.ascontiguousarray(
        np.asarray(L0_attn_a_Wout, np.float32).T.reshape(8, 128, 1024)
        .transpose(1, 0, 2).reshape(128, 8 * 1024))
    wq_t = np.ascontiguousarray(
        np.asarray(L0_attn_b_Wq, np.float32).T.reshape(8, 128, 8, 128)
        .transpose(1, 2, 0, 3).reshape(128, 8 * 8 * 128))
    wk_t = np.ascontiguousarray(
        np.asarray(L0_attn_b_Wk, np.float32).T.reshape(8, 128, 8, 128)
        .transpose(1, 2, 0, 3).reshape(128, 8 * 8 * 128))
    wv_t = np.ascontiguousarray(
        np.asarray(L0_attn_b_Wv, np.float32).T.reshape(8, 128, 2, 512)
        .transpose(1, 2, 0, 3).reshape(128, 2 * 8 * 512))
    w1 = np.ascontiguousarray(
        np.asarray(L0_ffn_w1, np.float32).reshape(8, 128, 2, 16, 128)
        .transpose(1, 2, 3, 0, 4).reshape(128, 2 * 16 * 8 * 128))
    w2 = np.ascontiguousarray(
        np.asarray(L0_ffn_w2, np.float32).reshape(2, 16, 128, 1024)
        .transpose(2, 0, 1, 3).reshape(128, 2 * 16 * 1024))
    b1r = np.ascontiguousarray(np.asarray(L0_ffn_b1, np.float32).reshape(FF // 128, 128).T)
    b2r = np.ascontiguousarray(np.asarray(L0_ffn_b2, np.float32).reshape(1, D))
    bqr = np.ascontiguousarray(np.asarray(L0_attn_b_bq, np.float32).reshape(NC_, 128).T)
    bkr = np.ascontiguousarray(np.asarray(L0_attn_b_bk, np.float32).reshape(NC_, 128).T)
    bvr = np.ascontiguousarray(np.asarray(L0_attn_b_bv, np.float32).reshape(1, D))
    s_a = np.asarray(L0_attn_a_a, np.float32).reshape(1, 1)
    s_b = np.asarray(L0_attn_a_b, np.float32).reshape(1, 1)
    s_dt = np.asarray(L0_attn_a_dt, np.float32).reshape(1, 1)
    c_eye = np.eye(128, dtype=np.float32)
    kk, qq = np.meshgrid(np.arange(128), np.arange(128), indexing="ij")
    c_tri = (kk <= qq).astype(np.float32)
    c_trit = np.ascontiguousarray(c_tri.T)
    c_ones = np.ones((128, 128), dtype=np.float32)
    c_sel16 = np.zeros((8, 128), dtype=np.float32)
    for hh in range(8):
        c_sel16[hh, hh * 16:(hh + 1) * 16] = 1.0

    nc = build_nc()
    in_maps = []
    for c in range(ncores):
        xc = np.ascontiguousarray(x[c * per:(c + 1) * per].reshape(per * T, D))
        in_maps.append({
            "x": xc, "wqkv_t": wqkv_t, "wout_t": wout_t, "wq_t": wq_t,
            "wk_t": wk_t, "wv_t": wv_t, "w1": w1, "w2": w2, "b1r": b1r,
            "b2r": b2r, "bqr": bqr, "bkr": bkr, "bvr": bvr, "s_a": s_a,
            "s_b": s_b, "s_dt": s_dt, "c_eye": c_eye, "c_tri": c_tri,
            "c_trit": c_trit, "c_ones": c_ones, "c_sel16": c_sel16,
        })
    res = run_bass_kernel_spmd(nc, in_maps, core_ids=list(range(ncores)))
    globals()["LAST_RES"] = res.results
    out = np.concatenate([r["out"].reshape(per, T, D) for r in res.results], axis=0)
    return out.astype(np.float32)


# version 11
# speedup vs baseline: 1.1339x; 1.0657x over previous
import contextlib

import numpy as np

import concourse.bass as bass
import concourse.mybir as mybir
from concourse.bass_utils import run_bass_kernel_spmd

F32 = mybir.dt.float32
BF16 = mybir.dt.bfloat16
AF = mybir.ActivationFunctionType
ALU = mybir.AluOpType
AX = mybir.AxisListType

T = 2048
D = 1024
H = 16
HD = 64
FF = 4096
NT = T // 128
NC_ = D // 128
NSEQ = 2
EPS = 1e-5

EMIT = {"attn_a": True, "ffn1": True, "attn_b": True, "ffn2": True}
FFN_STAGE = 4
DBG = False
YT_STAGE = 3
INTERLEAVE_BC = False


class Ctr:

    def __init__(self, nc, st, name):
        self.h = st.enter_context(nc.semaphore(name))
        self.n = 0

    def inc(self, instr, k=1):
        instr.then_inc(self.h, k)
        self.n += k
        return instr

    def wait(self, eng, lag=0):
        if self.n - lag > 0:
            eng.wait_ge(self.h, self.n - lag)


def build_nc():
    nc = bass.Bass("TRN2", target_bir_lowering=False)

    x_h = nc.dram_tensor("x", [NSEQ * T, D], F32, kind="ExternalInput").ap()
    out_h = nc.dram_tensor("out", [NSEQ * T, D], F32, kind="ExternalOutput").ap()
    wqkv_t = nc.dram_tensor("wqkv_t", [128, 24 * NC_ * 128], F32, kind="ExternalInput").ap()
    wout_t = nc.dram_tensor("wout_t", [128, NC_ * D], F32, kind="ExternalInput").ap()
    wq_t = nc.dram_tensor("wq_t", [128, NC_ * D], F32, kind="ExternalInput").ap()
    wk_t = nc.dram_tensor("wk_t", [128, NC_ * D], F32, kind="ExternalInput").ap()
    wv_t = nc.dram_tensor("wv_t", [128, NC_ * D], F32, kind="ExternalInput").ap()
    w1_h = nc.dram_tensor("w1", [128, NC_ * FF], F32, kind="ExternalInput").ap()
    w2_h = nc.dram_tensor("w2", [128, 32 * D], F32, kind="ExternalInput").ap()
    b1_h = nc.dram_tensor("b1r", [128, FF // 128], F32, kind="ExternalInput").ap()
    b2_h = nc.dram_tensor("b2r", [1, D], F32, kind="ExternalInput").ap()
    bq_h = nc.dram_tensor("bqr", [128, NC_], F32, kind="ExternalInput").ap()
    bk_h = nc.dram_tensor("bkr", [128, NC_], F32, kind="ExternalInput").ap()
    bv_h = nc.dram_tensor("bvr", [1, D], F32, kind="ExternalInput").ap()
    sa_h = nc.dram_tensor("s_a", [1, 1], F32, kind="ExternalInput").ap()
    sb_h = nc.dram_tensor("s_b", [1, 1], F32, kind="ExternalInput").ap()
    sdt_h = nc.dram_tensor("s_dt", [1, 1], F32, kind="ExternalInput").ap()
    eye_h = nc.dram_tensor("c_eye", [128, 128], F32, kind="ExternalInput").ap()
    tri_h = nc.dram_tensor("c_tri", [128, 128], F32, kind="ExternalInput").ap()
    trit_h = nc.dram_tensor("c_trit", [128, 128], F32, kind="ExternalInput").ap()
    ones_h = nc.dram_tensor("c_ones", [128, 128], F32, kind="ExternalInput").ap()
    sel16_h = nc.dram_tensor("c_sel16", [8, 128], F32, kind="ExternalInput").ap()

    if DBG:
        dbg_yt = nc.dram_tensor("dbg_yt", [128, NC_ * T], F32, kind="ExternalOutput").ap()
        dbg_stats = nc.dram_tensor("dbg_stats", [128, 6 * NT], F32, kind="ExternalOutput").ap()
        dbg_h1 = nc.dram_tensor("dbg_h1", [128, 16 * 512], BF16, kind="ExternalOutput").ap()
    st = contextlib.ExitStack()
    with st:
        X = st.enter_context(nc.sbuf_tensor("X", [128, NT * D], F32))
        YT = st.enter_context(nc.sbuf_tensor("YT", [128, NC_ * T], BF16))
        eye_bf = st.enter_context(nc.sbuf_tensor("eye_bf", [128, 128], BF16))
        eye_f = st.enter_context(nc.sbuf_tensor("eye_f", [128, 128], F32))
        tri_bf = st.enter_context(nc.sbuf_tensor("tri_bf", [128, 128], BF16))
        trit_bf = st.enter_context(nc.sbuf_tensor("trit_bf", [128, 128], BF16))
        ones_bf = st.enter_context(nc.sbuf_tensor("ones_bf", [128, 128], BF16))
        ones_f = st.enter_context(nc.sbuf_tensor("ones_f", [128, 128], F32))
        b1t = st.enter_context(nc.sbuf_tensor("b1t", [128, FF // 128], F32))
        bqt = st.enter_context(nc.sbuf_tensor("bqt", [128, NC_], F32))
        bkt = st.enter_context(nc.sbuf_tensor("bkt", [128, NC_], F32))
        s_abdt = st.enter_context(nc.sbuf_tensor("s_abdt", [1, 4], F32))
        sel16_bf = st.enter_context(nc.sbuf_tensor("sel16_bf", [8, 128], BF16))
        abdt_c = st.enter_context(nc.sbuf_tensor("abdt_c", [128, 4], F32))
        negu = st.enter_context(nc.sbuf_tensor("negu", [128, 16], F32))
        fT = st.enter_context(nc.sbuf_tensor("fT", [128, 16], F32))
        sums = st.enter_context(nc.sbuf_tensor("sums", [128, NT], F32))
        sumsq = st.enter_context(nc.sbuf_tensor("sumsq", [128, NT], F32))
        mean = st.enter_context(nc.sbuf_tensor("mean", [128, NT], F32))
        rstd = st.enter_context(nc.sbuf_tensor("rstd", [128, NT], F32))
        st1 = st.enter_context(nc.sbuf_tensor("st1", [128, NT], F32))
        st2 = st.enter_context(nc.sbuf_tensor("st2", [128, NT], F32))
        sq_scr = [st.enter_context(nc.sbuf_tensor(f"sqg{i}", [128, D], BF16)) for i in range(2)]
        ytok = [st.enter_context(nc.sbuf_tensor(f"ytok{i}", [128, D], BF16)) for i in range(2)]

        c_in = Ctr(nc, st, "c_in")
        c_w = Ctr(nc, st, "c_w")
        c_w2 = Ctr(nc, st, "c_w2")
        c_mm = Ctr(nc, st, "c_mm")
        c_v = Ctr(nc, st, "c_v")
        c_s = Ctr(nc, st, "c_s")
        c_st = Ctr(nc, st, "c_st")
        c_g = Ctr(nc, st, "c_g")
        c_tr = Ctr(nc, st, "c_tr")
        c_ev = Ctr(nc, st, "c_ev")
        c_dm = Ctr(nc, st, "c_dm")

        with nc.Block("setup", no_gpsimd_drain=True) as blk:
            @blk.sync
            def _(sync):
                c_w2.inc(sync.dma_start(out=eye_f[:], in_=eye_h), 16)
                c_w2.inc(sync.dma_start(out=ones_f[:], in_=ones_h), 16)
                c_w2.inc(sync.dma_start(out=b1t[:], in_=b1_h), 16)
                c_w2.inc(sync.dma_start(out=bqt[:], in_=bq_h), 16)
                c_w2.inc(sync.dma_start(out=bkt[:], in_=bk_h), 16)
                c_w2.inc(sync.dma_start(out=s_abdt[0:1, 0:1], in_=sa_h), 16)
                c_w2.inc(sync.dma_start(out=s_abdt[0:1, 1:2], in_=sb_h), 16)
                c_w2.inc(sync.dma_start(out=s_abdt[0:1, 2:3], in_=sdt_h), 16)

            @blk.gpsimd
            def _(g):
                c_w.inc(g.dma_start(out=eye_bf[:], in_=eye_h), 16)
                c_w.inc(g.dma_start(out=tri_bf[:], in_=tri_h), 16)
                c_w.inc(g.dma_start(out=trit_bf[:], in_=trit_h), 16)
                c_w.inc(g.dma_start(out=ones_bf[:], in_=ones_h), 16)
                c_w.inc(g.dma_start(out=sel16_bf[:], in_=sel16_h), 16)

        with (
            nc.psum_tensor("ps_abdt", [128, 4], F32) as ps_ab,
            nc.Block("abdt", no_gpsimd_drain=True) as blk,
        ):
            @blk.tensor
            def _(t):
                c_w2.wait(t)
                c_mm.inc(t.matmul(ps_ab[:, 0:3], ones_f[0:1, 0:128], s_abdt[0:1, 0:3],
                                  start=True, stop=True))
            @blk.vector
            def _(v):
                c_mm.wait(v)
                c_v.inc(v.tensor_copy(abdt_c[:, 0:3], ps_ab[:, 0:3]))


        def xs(i):
            return X[:, i * D:(i + 1) * D]

        def yt_sl(c, lo, hi):
            return YT[:, c * T + lo: c * T + hi]

        def emit_load_x(seq):
            with nc.Block(f"load{seq}", no_gpsimd_drain=True) as blk:
                @blk.sync
                def _(sync):
                    c_st.wait(sync)
                    src = x_h[seq * T:(seq + 1) * T, :].rearrange("(i p) f -> p i f", p=128)
                    c_in.inc(sync.dma_start(out=X[:].rearrange("p (i f) -> p i f", f=D), in_=src), 16)

        def emit_store_x(seq):
            with nc.Block(f"store{seq}", no_gpsimd_drain=True) as blk:
                @blk.sync
                def _(sync):
                    dst = out_h[seq * T:(seq + 1) * T, :].rearrange("(i p) f -> p i f", p=128)
                    c_st.inc(sync.dma_start(out=dst, in_=X[:].rearrange("p (i f) -> p i f", f=D)), 16)
                    c_st.wait(sync)

        def emit_ln(tag):
            with nc.Block(f"ln_{tag}", no_gpsimd_drain=True) as blk:
                s0, v0 = c_s.n, c_v.n

                @blk.scalar
                def _(s):
                    c_in.wait(s)
                    for i in range(NT):
                        if i >= 2:
                            s.wait_ge(c_v.h, v0 + i - 1)
                        c_s.inc(s.activation(sq_scr[i % 2][:], xs(i), AF.Square))

                @blk.vector
                def _(v):
                    c_in.wait(v)
                    for i in range(NT):
                        v.reduce_sum(sums[:, i:i + 1], xs(i), axis=AX.X)
                        v.wait_ge(c_s.h, s0 + i + 1)
                        c_v.inc(v.reduce_sum(sumsq[:, i:i + 1], sq_scr[i % 2][:], axis=AX.X))
                    def ch(ins):
                        ins.then_inc(c_v.h, 1)
                        c_v.n += 1
                        v.wait_ge(c_v.h, c_v.n)
                    v.wait_ge(c_v.h, c_v.n)
                    ch(v.tensor_scalar_mul(mean[:], sums[:], 1.0 / D))
                    ch(v.tensor_mul(st1[:], mean[:], mean[:]))
                    ch(v.tensor_scalar_mul(st2[:], sumsq[:], 1.0 / D))
                    ch(v.tensor_sub(st1[:], st2[:], st1[:]))
                    ch(v.tensor_scalar_add(st1[:], st1[:], EPS))

                @blk.scalar
                def _(s):
                    c_v.wait(s)
                    s.activation(st2[:], st1[:], AF.Ln).then_inc(c_s.h, 1)
                    s.wait_ge(c_s.h, c_s.n + 1)
                    s.activation(rstd[:], st2[:], AF.Exp, scale=-0.5).then_inc(c_s.h, 1)
                    s.wait_ge(c_s.h, c_s.n + 2)
                c_s.n += 2

        def emit_yt(tag):
            with (
                nc.psum_tensor(f"ps_yt0{tag}", [128, D], BF16) as pt0,
                nc.psum_tensor(f"ps_yt1{tag}", [128, D], BF16) as pt1,
                nc.Block(f"yt_{tag}", no_gpsimd_drain=True) as blk,
            ):
                pts = [pt0, pt1]
                v0, t0_, e0 = c_v.n, c_tr.n, c_ev.n

                @blk.vector
                def _(v):
                    for i in range(NT):
                        if i >= 2 and YT_STAGE >= 2:
                            v.wait_ge(c_tr.h, t0_ + (i - 1) * 8)
                        v.tensor_scalar(ytok[i % 2][:], xs(i), mean[:, i:i + 1],
                                        rstd[:, i:i + 1], ALU.subtract,
                                        ALU.mult).then_inc(c_v.h, 1)
                        if i >= 1 and YT_STAGE >= 3:
                            v.wait_ge(c_tr.h, t0_ + i * 8)
                            v.tensor_copy(
                                YT[:].rearrange("p (c t) -> p c t", t=T)[:, :, (i - 1) * 128:i * 128],
                                pts[(i - 1) % 2][:].rearrange("p (c t) -> p c t", t=128)).then_inc(c_ev.h, 1)
                    if YT_STAGE >= 3:
                        v.wait_ge(c_tr.h, t0_ + NT * 8)
                        v.tensor_copy(
                            YT[:].rearrange("p (c t) -> p c t", t=T)[:, :, (NT - 1) * 128:NT * 128],
                            pts[(NT - 1) % 2][:].rearrange("p (c t) -> p c t", t=128)).then_inc(c_ev.h, 1)
                c_v.n = v0 + NT
                if YT_STAGE >= 3:
                    c_ev.n = e0 + NT

                @blk.tensor
                def _(t):
                    if YT_STAGE < 2:
                        return
                    c_w.wait(t)
                    for i in range(NT):
                        t.wait_ge(c_v.h, v0 + i + 1)
                        if i >= 2 and YT_STAGE >= 3:
                            t.wait_ge(c_ev.h, e0 + i - 1)
                        for c in range(NC_):
                            t.transpose(pts[i % 2][:, c * 128:(c + 1) * 128],
                                        ytok[i % 2][:, c * 128:(c + 1) * 128],
                                        eye_bf[:]).then_inc(c_tr.h, 1)
                if YT_STAGE >= 2:
                    c_tr.n = t0_ + NT * 8

        def stats_epi_scalar(s, ready, s_base, v_base):
            for i in range(NT):
                r = ready(i)
                if r:
                    s.wait_ge(c_ev.h, r)
                s.activation(sq_scr[0][:], xs(i), AF.Square,
                             accum_out=sumsq[:, i:i + 1]).then_inc(c_s.h, 1)
            for h in range(2):
                s.wait_ge(c_v.h, v_base + h * 13 + 13)
                s.activation(st2[:, h * 8:(h + 1) * 8],
                             st1[:, h * 8:(h + 1) * 8], AF.Ln).then_inc(c_s.h, 1)
                s.wait_ge(c_s.h, s_base + 16 + 2 * h + 1)
                s.activation(rstd[:, h * 8:(h + 1) * 8], st2[:, h * 8:(h + 1) * 8],
                             AF.Exp, scale=-0.5).then_inc(c_s.h, 1)
                s.wait_ge(c_s.h, s_base + 16 + 2 * h + 2)

        def stats_epi_vector(v, ready, s_base, v_base):
            for h in range(2):
                nv = [v_base + h * 13]

                def ch(ins):
                    ins.then_inc(c_v.h, 1)
                    nv[0] += 1
                    v.wait_ge(c_v.h, nv[0])
                hs = slice(h * 8, (h + 1) * 8)
                for k in range(NT // 2):
                    i = h * 8 + k
                    r = ready(i)
                    if r:
                        v.wait_ge(c_ev.h, r)
                    v.reduce_sum(sums[:, i:i + 1], xs(i), axis=AX.X).then_inc(c_v.h, 1)
                    nv[0] += 1
                v.wait_ge(c_v.h, nv[0])
                v.wait_ge(c_s.h, s_base + (9 if h == 0 else 17))
                ch(v.tensor_scalar_mul(mean[:, hs], sums[:, hs], 1.0 / D))
                ch(v.tensor_mul(st1[:, hs], mean[:, hs], mean[:, hs]))
                ch(v.tensor_scalar_mul(st2[:, hs], sumsq[:, hs], 1.0 / D))
                ch(v.tensor_sub(st1[:, hs], st2[:, hs], st1[:, hs]))
                ch(v.tensor_scalar_add(st1[:, hs], st1[:, hs], EPS))

        def emit_ffn(seq, idx, skip_ln=False, stats_next=False):
            tag = f"f{seq}{idx}"
            if not skip_ln:
                emit_ln(tag)
            with (
                nc.sbuf_tensor(f"w1b_{tag}", [128, NC_ * 16 * 128], BF16) as w1b,
                nc.sbuf_tensor(f"w2b_{tag}", [128, 16 * D], BF16) as w2b,
                nc.sbuf_tensor(f"h1_{tag}", [128, 16 * 512], BF16) as h1,
                nc.sbuf_tensor(f"b2r_{tag}", [1, D], BF16) as b2row,
            ):
                for half in range(2):
                    hb = half * 16
                    with nc.Block(f"ffnw_{tag}{half}", no_gpsimd_drain=True) as blk:
                        @blk.gpsimd
                        def _(g):
                            if half == 0:
                                c_w.inc(g.dma_start(out=b2row[:], in_=b2_h), 16)
                            c_mm.wait(g)
                            for pc in range(4):
                                c_w.inc(g.dma_start(
                                    out=w1b[:, pc * 4096:(pc + 1) * 4096],
                                    in_=w1_h[:, half * 16384 + pc * 4096:
                                             half * 16384 + (pc + 1) * 4096]), 16)
                            for pc in range(4):
                                c_w.inc(g.dma_start(
                                    out=w2b[:, pc * 4096:(pc + 1) * 4096],
                                    in_=w2_h[:, half * 16384 + pc * 4096:
                                             half * 16384 + (pc + 1) * 4096]), 16)

                    with (
                        nc.psum_tensor(f"ps_h0{tag}{half}", [128, 512], F32) as ph0,
                        nc.psum_tensor(f"ps_h1{tag}{half}", [128, 512], F32) as ph1,
                        nc.psum_tensor(f"ps_o0{tag}{half}", [128, 512], F32) as po0,
                        nc.psum_tensor(f"ps_o1{tag}{half}", [128, 512], F32) as po1,
                        nc.psum_tensor(f"ps_yt0{tag}{half}", [128, D], BF16) as pt0,
                        nc.psum_tensor(f"ps_yt1{tag}{half}", [128, D], BF16) as pt1,
                        nc.Block(f"ffn_{tag}{half}", no_gpsimd_drain=True) as blk,
                    ):
                        phs, pos, pts = [ph0, ph1], [po0, po1], [pt0, pt1]
                        nyt = NT if half == 0 else 0
                        v0, s0, m0, t0_ = c_v.n, c_s.n, c_mm.n, c_tr.n
                        ev_yt = c_ev.n
                        e0 = ev_yt + nyt
                        wbase = c_w.n - 8 * 16

                        def f_ready(i):
                            mt_, s2 = i // 4, i % 4
                            return e0 + mt_ * 8 + s2 * 2 + 2

                        @blk.scalar
                        def _(s):
                            c_w2.wait(s)
                            for mt in range(4):
                                for c in range(16):
                                    s.wait_ge(c_mm.h, m0 + mt * 24 + c + 1)
                                    s.activation(h1[:, c * 512:(c + 1) * 512],
                                                 phs[c % 2][:], AF.Gelu,
                                                 bias=b1t[:, hb + c:hb + c + 1]).then_inc(c_s.h, 1)
                            if half == 1 and stats_next:
                                stats_epi_scalar(s, f_ready, s0 + 64, v0)
                        c_s.n = s0 + 64 + (20 if stats_next else 0) if half == 1 else s0 + 64


                        @blk.vector
                        def _(v):
                            if half == 0:
                                for i in range(NT):
                                    if i >= 2:
                                        v.wait_ge(c_tr.h, t0_ + (i - 1) * 8)
                                    v.tensor_scalar(ytok[i % 2][:], xs(i), mean[:, i:i + 1],
                                                    rstd[:, i:i + 1], ALU.subtract,
                                                    ALU.mult).then_inc(c_v.h, 1)
                                    if i >= 1:
                                        v.wait_ge(c_tr.h, t0_ + i * 8)
                                        v.tensor_copy(
                                            YT[:].rearrange("p (c t) -> p c t", t=T)[:, :, (i - 1) * 128:i * 128],
                                            pts[(i - 1) % 2][:].rearrange("p (c t) -> p c t", t=128)).then_inc(c_ev.h, 1)
                                v.wait_ge(c_tr.h, t0_ + NT * 8)
                                v.tensor_copy(
                                    YT[:].rearrange("p (c t) -> p c t", t=T)[:, :, (NT - 1) * 128:NT * 128],
                                    pts[(NT - 1) % 2][:].rearrange("p (c t) -> p c t", t=128)).then_inc(c_ev.h, 1)
                            for mt in range(4):
                                for j in range(8):
                                    s_, hf = j // 2, j % 2
                                    i = 4 * mt + s_
                                    v.wait_ge(c_mm.h, m0 + mt * 24 + 16 + j + 1)
                                    v.tensor_add(
                                        X[:, i * D + hf * 512: i * D + hf * 512 + 512],
                                        X[:, i * D + hf * 512: i * D + hf * 512 + 512],
                                        pos[j % 2][:]).then_inc(c_ev.h, 1)
                            if half == 1 and stats_next:
                                stats_epi_vector(v, f_ready, s0 + 64, v0)
                        c_v.n = v0 + nyt + ((26 if stats_next else 0) if half == 1 else 0)
                        c_ev.n = e0 + 32

                        @blk.tensor
                        def _(t):
                            if half == 0:
                                for i in range(NT):
                                    t.wait_ge(c_v.h, v0 + i + 1)
                                    if i >= 2:
                                        t.wait_ge(c_ev.h, ev_yt + i - 1)
                                    for c in range(NC_):
                                        t.transpose(pts[i % 2][:, c * 128:(c + 1) * 128],
                                                    ytok[i % 2][:, c * 128:(c + 1) * 128],
                                                    eye_bf[:]).then_inc(c_tr.h, 1)
                            for mt in range(4):
                                if half == 0:
                                    t.wait_ge(c_ev.h, ev_yt + min(NT, 4 * (mt + 1)))
                                for c in range(16):
                                    if c >= 2:
                                        t.wait_ge(c_s.h, s0 + mt * 16 + c - 1)
                                    if mt == 0:
                                        t.wait_ge(c_w.h, wbase + (c // 4 + 1) * 16)
                                    for i in range(NC_):
                                        w_sl = w1b[:, (c * NC_ + i) * 128:
                                                   (c * NC_ + i + 1) * 128]
                                        ins = t.matmul(phs[c % 2][:], w_sl,
                                                 yt_sl(i, mt * 512, (mt + 1) * 512),
                                                 start=(i == 0), stop=(i == NC_ - 1))
                                        if i == NC_ - 1:
                                            ins.then_inc(c_mm.h, 1)
                                t.wait_ge(c_s.h, s0 + (mt + 1) * 16)
                                for j in range(8):
                                    s_, hf = j // 2, j % 2
                                    i = 4 * mt + s_
                                    g_ = mt * 8 + j
                                    if g_ >= 2:
                                        t.wait_ge(c_ev.h, e0 + g_ - 1)
                                    for c in range(16):
                                        if mt == 0 and j == 0 and c % 4 == 0:
                                            t.wait_ge(c_w.h, wbase + (4 + c // 4 + 1) * 16)
                                        h_sl = h1[:, c * 512 + s_ * 128: c * 512 + s_ * 128 + 128]
                                        w_sl = w2b[:, c * D + hf * 512: c * D + hf * 512 + 512]
                                        ins = t.matmul(pos[j % 2][:], h_sl, w_sl,
                                                 start=(c == 0),
                                                 stop=(half == 0 and c == 15))
                                        if half == 0 and c == 15:
                                            ins.then_inc(c_mm.h, 1)
                                    if half == 1:
                                        t.matmul(pos[j % 2][:], ones_bf[0:1, 0:128],
                                                 b2row[0:1, hf * 512:hf * 512 + 512],
                                                 start=False, stop=True).then_inc(c_mm.h, 1)
                        c_mm.n = m0 + 4 * 24
                        if half == 0:
                            c_tr.n = t0_ + NT * 8

        def emit_attn_a(seq, stats_next=False):
            tag = f"a{seq}"
            emit_ln(tag)
            emit_yt(tag)
            with contextlib.ExitStack() as ast_:
                QTg = ast_.enter_context(nc.sbuf_tensor(f"QT_{tag}", [128, 4 * T], BF16))
                KTg = ast_.enter_context(nc.sbuf_tensor(f"KT_{tag}", [128, 4 * T], BF16))
                Vg = ast_.enter_context(nc.sbuf_tensor(f"Vg_{tag}", [128, 16 * 528], BF16))
                wqk0 = ast_.enter_context(nc.sbuf_tensor(f"wqk_{tag}0", [128, 8 * 128], BF16))
                wqk1 = ast_.enter_context(nc.sbuf_tensor(f"wqk_{tag}1", [128, 8 * 128], BF16))
                vrhs = ast_.enter_context(nc.sbuf_tensor(f"vrhs_{tag}", [128, 8 * 512], BF16))
                wog = ast_.enter_context(nc.sbuf_tensor(f"wog_{tag}", [128, 4 * 1024], BF16))
                PT0 = ast_.enter_context(nc.sbuf_tensor(f"PT_{tag}0", [128, 512], BF16))
                PT1 = ast_.enter_context(nc.sbuf_tensor(f"PT_{tag}1", [128, 512], BF16))
                sqb = ast_.enter_context(nc.sbuf_tensor(f"sqb_{tag}", [128, 512], BF16))
                Lsb = ast_.enter_context(nc.sbuf_tensor(f"Lsb_{tag}", [128, 512], F32))
                rLsb = ast_.enter_context(nc.sbuf_tensor(f"rLsb_{tag}", [128, 512], F32))
                ATs0 = ast_.enter_context(nc.sbuf_tensor(f"ATs_{tag}0", [64, T], BF16))
                ATs1 = ast_.enter_context(nc.sbuf_tensor(f"ATs_{tag}1", [64, T], BF16))
                bcsb = ast_.enter_context(nc.sbuf_tensor(f"bcsb_{tag}", [64, 512], F32))
                Ldist = ast_.enter_context(nc.sbuf_tensor(f"Ld_{tag}", [128, 128], F32))
                rLdist = ast_.enter_context(nc.sbuf_tensor(f"rLd_{tag}", [128, 128], F32))
                fhn = ast_.enter_context(nc.sbuf_tensor(f"fhn_{tag}", [128, 14 * 128], F32))
                ucol = ast_.enter_context(nc.sbuf_tensor(f"ucol_{tag}", [1, 16], F32))
                ubf = ast_.enter_context(nc.sbuf_tensor(f"ubf_{tag}", [1, 16], BF16))
                fcols = ast_.enter_context(nc.sbuf_tensor(f"fc_{tag}", [128, 4], F32))
                ATs = [ATs0, ATs1]
                PTs = [PT0, PT1]
                wqks = [wqk0, wqk1]

                def fsl(k):
                    return fhn[:, k * 128:(k + 1) * 128]

                for g in range(2):
                    hb8 = 8 * g

                    with contextlib.ExitStack() as pst_:
                        pss0 = pst_.enter_context(nc.psum_tensor(f"ps_s{tag}{g}", [128, 512], F32))
                        pss1 = pst_.enter_context(nc.psum_tensor(f"ps_t{tag}{g}", [128, 512], F32))
                        pv0 = pst_.enter_context(nc.psum_tensor(f"ps_pv0{tag}{g}", [65, 512], F32))
                        pv1 = pst_.enter_context(nc.psum_tensor(f"ps_pv1{tag}{g}", [65, 512], F32))
                        pv2 = pst_.enter_context(nc.psum_tensor(f"ps_pv2{tag}{g}", [65, 512], F32))
                        pv3 = pst_.enter_context(nc.psum_tensor(f"ps_pv3{tag}{g}", [65, 512], F32))
                        psbc = pst_.enter_context(nc.psum_tensor(f"ps_bc{tag}{g}", [64, 512], F32))
                        pssm = pst_.enter_context(nc.psum_tensor(f"ps_sm{tag}{g}", [128, 128], F32))
                        pss = [pss0, pss1]
                        pvs = [pv0, pv1, pv2, pv3]
                        with nc.Block(f"aproj_{tag}{g}", no_gpsimd_drain=True) as blk:
                            w0 = c_w.n
                            m0 = c_mm.n
                            e0 = c_ev.n
                            g0_ = c_g.n

                            @blk.gpsimd
                            def _(gp):
                                for oc in range(8):
                                    if oc >= 2:
                                        gp.wait_ge(c_mm.h, m0 + (oc - 1) * 4)
                                    ocg = (4 * g + oc) if oc < 4 else (8 + 4 * g + (oc - 4))
                                    c_w.inc(gp.dma_start(
                                        out=wqks[oc % 2][:],
                                        in_=wqkv_t[:, ocg * 1024:(ocg + 1) * 1024]), 16)
                                c_w.inc(gp.dma_start(
                                    out=vrhs[:],
                                    in_=wqkv_t[:, (16 + 4 * g) * 1024:(16 + 4 * g + 4) * 1024]), 16)
                                c_w.inc(gp.dma_start(
                                    out=wog[:],
                                    in_=wout_t[:, 4 * g * 1024:(4 * g + 4) * 1024]), 16)

                            @blk.tensor
                            def _(t):
                                nmm = m0
                                for oc in range(8):
                                    t.wait_ge(c_w.h, w0 + (oc + 1) * 16)
                                    for m4 in range(4):
                                        gidx = oc * 4 + m4
                                        if gidx >= 2:
                                            t.wait_ge(c_ev.h, e0 + gidx - 1)
                                        for i in range(NC_):
                                            ins = t.matmul(pss[gidx % 2][:],
                                                           wqks[oc % 2][:, i * 128:(i + 1) * 128],
                                                           yt_sl(i, m4 * 512, (m4 + 1) * 512),
                                                           start=(i == 0), stop=(i == NC_ - 1))
                                        ins.then_inc(c_mm.h, 1)
                                        nmm += 1
                                t.wait_ge(c_w.h, w0 + 9 * 16)
                                for kt in range(NT):
                                    gidx = 32 + kt
                                    t.wait_ge(c_ev.h, e0 + gidx - 1)
                                    for i in range(NC_):
                                        v_rhs = vrhs[:].rearrange("p (a i o) -> p a i o",
                                                                  a=4, i=NC_)[:, :, i, :]
                                        ins = t.matmul(pss[gidx % 2][:],
                                                       yt_sl(i, kt * 128, (kt + 1) * 128),
                                                       v_rhs,
                                                       start=(i == 0), stop=(i == NC_ - 1))
                                    ins.then_inc(c_mm.h, 1)
                                    nmm += 1
                            c_mm.n = m0 + 48

                            @blk.vector
                            def _(v):
                                v.memset(Vg[:].rearrange("p (k h w) -> p k h w", h=8, w=66)
                                         [:, :, :, 64:65], 1.0).then_inc(c_g.h, 1)
                                for oc in range(8):
                                    for m4 in range(4):
                                        gidx = oc * 4 + m4
                                        v.wait_ge(c_mm.h, m0 + gidx + 1)
                                        dst = QTg if oc < 4 else KTg
                                        v.tensor_copy(
                                            dst[:, (oc % 4) * T + m4 * 512:(oc % 4) * T + (m4 + 1) * 512],
                                            pss[gidx % 2][:]).then_inc(c_ev.h, 1)
                                for kt in range(NT):
                                    gidx = 32 + kt
                                    v.wait_ge(c_mm.h, m0 + gidx + 1)
                                    v.tensor_copy(
                                        Vg[:, kt * 528:(kt + 1) * 528]
                                        .rearrange("p (h w) -> p h w", w=66)[:, :, 0:64],
                                        pss[gidx % 2][:].rearrange("p (h w) -> p h w", w=64)
                                    ).then_inc(c_ev.h, 1)
                            c_ev.n = e0 + 48
                            c_g.n = g0_ + 1

                        with nc.Block(f"auph_{tag}{g}", no_gpsimd_drain=True) as blk:
                            v0 = c_v.n
                            m0 = c_mm.n

                            @blk.vector
                            def _(v):
                                nv = [v0]

                                def ch(ins):
                                    ins.then_inc(c_v.h, 1)
                                    nv[0] += 1
                                    v.wait_ge(c_v.h, nv[0])
                                nm = [m0]
                                for k in range(8):
                                    srci, j = k // 4, k % 4
                                    srcbuf = QTg if srci == 0 else KTg
                                    for mc in range(4):
                                        kk = k * 4 + mc
                                        if kk >= 1:
                                            v.wait_ge(c_mm.h, m0 + kk)
                                        ch(v.tensor_mul(
                                            sqb[:],
                                            srcbuf[:, j * T + mc * 512:j * T + (mc + 1) * 512],
                                            srcbuf[:, j * T + mc * 512:j * T + (mc + 1) * 512]))
                                        nm[0] += 1
                                        v.wait_ge(c_mm.h, nm[0])
                                        ch(v.reduce_max(ucol[0:1, mc:mc + 1],
                                                        psbc[0:1, 0:512], axis=AX.X))
                                        if mc == 3:
                                            ch(v.reduce_max(ucol[0:1, 4 + k:5 + k],
                                                            ucol[0:1, 0:4], axis=AX.X))
                                for j in range(4):
                                    ch(v.tensor_add(ucol[0:1, 12:13], ucol[0:1, 4 + j:5 + j],
                                                    ucol[0:1, 8 + j:9 + j]))
                                    ch(v.tensor_scalar_mul(ucol[0:1, 13:14],
                                                           ucol[0:1, 12:13], 1.0 / 16.0))
                                    ch(v.tensor_copy(ubf[0:1, 0:1], ucol[0:1, 13:14]))
                                    nm[0] += 1
                                    v.wait_ge(c_mm.h, nm[0])
                                    ch(v.tensor_scalar_mul(
                                        negu[:, hb8 + 2 * j:hb8 + 2 * j + 1],
                                        pssm[:, 0:1], -1.0))
                                    ch(v.tensor_copy(
                                        negu[:, hb8 + 2 * j + 1:hb8 + 2 * j + 2],
                                        negu[:, hb8 + 2 * j:hb8 + 2 * j + 1]))
                                c_v.n = nv[0]

                            @blk.tensor
                            def _(t):
                                nv = v0
                                nm = m0
                                for k in range(8):
                                    for mc in range(4):
                                        nv += 1
                                        t.wait_ge(c_v.h, nv)
                                        t.matmul(psbc[0:1, 0:512], ones_bf[:, 0:1], sqb[:],
                                                 start=True, stop=True).then_inc(c_mm.h, 1)
                                        nm += 1
                                        nv += 1
                                        if mc == 3:
                                            nv += 1
                                for j in range(4):
                                    nv += 3
                                    t.wait_ge(c_v.h, nv)
                                    t.matmul(pssm[:, 0:1], ones_bf[0:1, 0:128],
                                             ubf[0:1, 0:1], start=True,
                                             stop=True).then_inc(c_mm.h, 1)
                                    nm += 1
                                    nv += 2
                                c_mm.n = nm

                        with nc.Block(f"aflash_{tag}{g}", no_gpsimd_drain=True) as blk:
                            pairs = []
                            for m in range(4):
                                for kb in range(4 * m + 4):
                                    pairs.append((m, kb))
                            NP = len(pairs)

                            heads = []
                            mm_n = [c_mm.n]
                            s_n = [c_s.n]
                            v_n = [c_v.n]
                            dm_n = [c_dm.n]
                            g_n = [c_g.n]
                            for hp in range(8):
                                ev = {}
                                ids_S = {}
                                ids_PV = {}
                                prev_bc = {}
                                cnt = mm_n[0]
                                ids_S[0] = cnt = cnt + 1
                                for n in range(NP):
                                    if n + 1 < NP:
                                        ids_S[n + 1] = cnt = cnt + 1
                                    if INTERLEAVE_BC and hp >= 1 and 1 <= n <= 4:
                                        prev_bc[n - 1] = cnt = cnt + 1
                                    ids_PV[n] = cnt = cnt + 1
                                if hp == 7 or not INTERLEAVE_BC:
                                    ids_bc = {}
                                    for m in range(4):
                                        ids_bc[m] = cnt = cnt + 1
                                else:
                                    ids_bc = None
                                if INTERLEAVE_BC and hp >= 1:
                                    heads[hp - 1]["bc"] = prev_bc
                                mm_n[0] = cnt
                                ids_exp = {}
                                ids_lc = {}
                                cnt = s_n[0]
                                for n in range(NP):
                                    ids_exp[n] = cnt = cnt + 1
                                for m in range(4):
                                    ids_lc[m] = cnt = cnt + 1
                                s_n[0] = cnt
                                ids_mask = {}
                                cnt = v_n[0]
                                for n, (m, kb) in enumerate(pairs):
                                    if kb >= 4 * m:
                                        ids_mask[n] = cnt = cnt + 1
                                ids_evbc = {}
                                ids_att = {}
                                for m in range(4):
                                    ids_evbc[m] = cnt = cnt + 1
                                    ids_att[m] = cnt = cnt + 1
                                v_n[0] = cnt
                                ids_rec = {}
                                for m in range(4):
                                    ids_rec[m] = g_n[0] = g_n[0] + 1
                                ids_dml = {}
                                ids_dmr = {}
                                for m in range(4):
                                    ids_dml[m] = dm_n[0] = dm_n[0] + 16
                                    ids_dmr[m] = dm_n[0] = dm_n[0] + 16
                                id_dma = dm_n[0] = dm_n[0] + 16
                                heads.append(dict(S=ids_S, PV=ids_PV, bc=ids_bc, exp=ids_exp,
                                                  lc=ids_lc, mask=ids_mask, evbc=ids_evbc,
                                                  att=ids_att, rec=ids_rec, dml=ids_dml,
                                                  dmr=ids_dmr, dma=id_dma))

                            g_entry = c_g.n

                            @blk.tensor
                            def _(t):
                                t.wait_ge(c_g.h, g_entry)
                                c_ev.wait(t)
                                for hp in range(8):
                                    E = heads[hp]
                                    j, rb = hp // 2, 64 * (hp % 2)

                                    def emit_S(n):
                                        m, kb = pairs[n]
                                        off = max(0, (kb - 4 * m) * 128)
                                        if n >= 2:
                                            t.wait_ge(c_s.h, E["exp"][n - 2])
                                        t.matmul(pss[n % 2][:, off:512],
                                                 KTg[rb:rb + 64, j * T + kb * 128:j * T + (kb + 1) * 128],
                                                 QTg[rb:rb + 64, j * T + m * 512 + off:j * T + (m + 1) * 512],
                                                 start=True, stop=True).then_inc(c_mm.h, 1)

                                    def emit_PV(n):
                                        m, kb = pairs[n]
                                        off = max(0, (kb - 4 * m) * 128)
                                        t.wait_ge(c_s.h, E["exp"][n])
                                        if n in E["mask"]:
                                            t.wait_ge(c_v.h, E["mask"][n])
                                        if kb == 0 and hp >= 1:
                                            t.wait_ge(c_v.h, heads[hp - 1]["att"][m] if INTERLEAVE_BC else heads[hp - 1]["att"][3])
                                        t.matmul(pvs[m][:, off:512],
                                                 Vg[:, kb * 528 + 66 * hp:kb * 528 + 66 * hp + 65],
                                                 PTs[n % 2][:, 0:512 - off],
                                                 start=(kb == 0), stop=(kb == 4 * m + 3)
                                                 ).then_inc(c_mm.h, 1)

                                    def emit_bc(pe, m):
                                        t.wait_ge(c_g.h, pe["rec"][m])
                                        if m >= 1:
                                            t.wait_ge(c_v.h, pe["evbc"][m - 1])
                                        t.matmul(psbc[:, :], ones_f[64:65, 0:64],
                                                 rLsb[64:65, 0:512],
                                                 start=True, stop=True).then_inc(c_mm.h, 1)

                                    emit_S(0)
                                    for n in range(NP):
                                        if n + 1 < NP:
                                            emit_S(n + 1)
                                        if INTERLEAVE_BC and hp >= 1 and 1 <= n <= 4:
                                            emit_bc(heads[hp - 1], n - 1)
                                        emit_PV(n)
                                    if hp == 7 or not INTERLEAVE_BC:
                                        for m in range(4):
                                            emit_bc(E, m)
                            c_mm.n = mm_n[0]

                            @blk.scalar
                            def _(s):
                                for hp in range(8):
                                    E = heads[hp]
                                    for n in range(NP):
                                        m, kb = pairs[n]
                                        off = max(0, (kb - 4 * m) * 128)
                                        w = E["S"][n]
                                        if n >= 2:
                                            w = max(w, E["PV"][n - 2])
                                        s.wait_ge(c_mm.h, w)
                                        s.activation(PTs[n % 2][:, 0:512 - off],
                                                     pss[n % 2][:, off:512], AF.Exp,
                                                     bias=negu[:, hb8 + hp:hb8 + hp + 1],
                                                     scale=0.125).then_inc(c_s.h, 1)
                                    for m in range(4):
                                        last_n = [n for n, (mm_, _kb) in enumerate(pairs) if mm_ == m][-1]
                                        s.wait_ge(c_mm.h, E["PV"][last_n])
                                        if m >= 1:
                                            s.wait_ge(c_g.h, E["rec"][m - 1])
                                            s.wait_ge(c_dm.h, E["dml"][m - 1])
                                        s.copy(Lsb[64:65, 0:512],
                                               pvs[m][64:65, 0:512]).then_inc(c_s.h, 1)
                            c_s.n = s_n[0]

                            @blk.vector
                            def _(v):
                                for hp in range(8):
                                    E = heads[hp]
                                    for n, (m, kb) in enumerate(pairs):
                                        if n in E["mask"]:
                                            v.wait_ge(c_s.h, E["exp"][n])
                                            v.tensor_mul(PTs[n % 2][:, 0:128],
                                                         PTs[n % 2][:, 0:128],
                                                         tri_bf[:]).then_inc(c_v.h, 1)
                                    for m in range(4):
                                        v.wait_ge(c_s.h, E["lc"][m])
                                        if m >= 1:
                                            v.wait_ge(c_mm.h, E["bc"][m - 1])
                                            v.wait_ge(c_dm.h, E["dmr"][m - 1])
                                        v.reciprocal(rLsb[64:65, 0:512],
                                                     Lsb[64:65, 0:512]).then_inc(c_g.h, 1)
                                        v.wait_ge(c_g.h, E["rec"][m])
                                        v.wait_ge(c_mm.h, E["bc"][m])
                                        v.tensor_copy(bcsb[:], psbc[:, :]).then_inc(c_v.h, 1)
                                        v.wait_ge(c_v.h, E["evbc"][m])
                                        if hp >= 2 and m == 0:
                                            v.wait_ge(c_dm.h, heads[hp - 2]["dma"])
                                        v.tensor_mul(ATs[hp % 2][:, m * 512:(m + 1) * 512],
                                                     pvs[m][0:64, 0:512],
                                                     bcsb[:]).then_inc(c_v.h, 1)
                            c_v.n = v_n[0]

                            @blk.sync
                            def _(sy):
                                for hp in range(8):
                                    E = heads[hp]
                                    j, rb = hp // 2, 64 * (hp % 2)
                                    for m in range(4):
                                        sy.wait_ge(c_s.h, E["lc"][m])
                                        c_dm.inc(sy.dma_start(
                                            out=Ldist[16 * hp + 4 * m:16 * hp + 4 * m + 4, :],
                                            in_=Lsb[64:65, 0:512]), 16)
                                        sy.wait_ge(c_g.h, E["rec"][m])
                                        c_dm.inc(sy.dma_start(
                                            out=rLdist[16 * hp + 4 * m:16 * hp + 4 * m + 4, :],
                                            in_=rLsb[64:65, 0:512]), 16)
                                    sy.wait_ge(c_v.h, E["att"][3])
                                    c_dm.inc(sy.dma_start(
                                        out=QTg[rb:rb + 64, j * T:(j + 1) * T],
                                        in_=ATs[hp % 2][:, :]), 16)
                            c_g.n = g_n[0]
                            c_dm.n = dm_n[0]

                        with nc.Block(f"afhn_{tag}{g}", no_gpsimd_drain=True) as blk:
                            v0 = c_v.n
                            s0 = c_s.n
                            m0 = c_mm.n
                            vops = []
                            if g == 0:
                                vops += ["col1", "col2", "col3", "col4"]
                            vops += ["E", "ecol", "erow", "st8", "stmax", "st8bf",
                                     "stimcol", "rstim", "sn", "aff", "tg2", "I", "v1",
                                     "w1a", "w1b", "w1c", "vsq", "vcu", "tm3",
                                     "pv1", "mw1", "pI", "mdt", "v2", "c", "cE",
                                     "cE1", "rq", "f"]
                            if g == 0:
                                vops.append("fTcp")
                            vid = {nm: v0 + i + 1 for i, nm in enumerate(vops)}
                            tids = {"trE": m0 + 1, "trS8": m0 + 2, "sel": m0 + 3}
                            if g == 0:
                                tids["trF"] = m0 + 4
                            sid = {"Aabs": s0 + 1, "tg": s0 + 2, "gsig": s0 + 3}
                            ecol = fhn[:, 12 * 128:12 * 128 + 1]
                            st8 = fhn[0:1, 12 * 128 + 8:12 * 128 + 16]
                            stimcol = fhn[:, 12 * 128 + 20:12 * 128 + 21]
                            rstim = fhn[:, 12 * 128 + 21:12 * 128 + 22]
                            erow = fhn[0:1, 13 * 128:14 * 128]

                            @blk.vector
                            def _(v):
                                nv = [v0]

                                def ch(ins):
                                    ins.then_inc(c_v.h, 1)
                                    nv[0] += 1
                                    v.wait_ge(c_v.h, nv[0])
                                c_dm.wait(v)
                                if g == 0:
                                    ch(v.tensor_scalar_mul(fcols[:, 0:1], abdt_c[:, 2:3], 0.08))
                                    ch(v.tensor_mul(fcols[:, 1:2], fcols[:, 0:1], abdt_c[:, 1:2]))
                                    ch(v.tensor_scalar_add(fcols[:, 1:2], fcols[:, 1:2], 1.0))
                                    ch(v.reciprocal(fcols[:, 2:3], fcols[:, 1:2]))
                                ch(v.tensor_mul(fsl(0), Ldist[:], rLdist[:]))
                                ch(v.reduce_max(ecol, fsl(0), axis=AX.X,
                                                apply_absolute_value=True))
                                v.wait_ge(c_mm.h, tids["trE"])
                                ch(v.tensor_copy(erow, pssm[0:1, 0:128]))
                                ch(v.reduce_max(st8,
                                                erow.rearrange("a (h j) -> a h j", j=16),
                                                axis=AX.X))
                                ch(v.tensor_scalar_max(st8, st8, 1e-6))
                                v.wait_ge(c_mm.h, tids["trS8"])
                                ch(v.tensor_copy(sqb[0:8, 0:1], pssm[0:8, 0:1]))
                                v.wait_ge(c_mm.h, tids["sel"])
                                ch(v.tensor_copy(stimcol, pssm[:, 2:3]))
                                ch(v.reciprocal(rstim, stimcol))
                                ch(v.tensor_scalar_mul(fsl(1), fsl(0), rstim))
                                v.wait_ge(c_s.h, sid["Aabs"])
                                ch(v.tensor_scalar(fsl(9), fsl(9), 10.0, -5.0,
                                                   ALU.mult, ALU.add))
                                v.wait_ge(c_s.h, sid["tg"])
                                ch(v.tensor_scalar(fsl(2), fsl(10), 0.9, 0.1,
                                                   ALU.mult, ALU.add))
                                ch(v.tensor_mul(fsl(3), fsl(1), fsl(2)))
                                ch(v.tensor_scalar_mul(fsl(4), fsl(3), abdt_c[:, 2:3]))
                                ch(v.tensor_scalar_add(fsl(5), fsl(4), abdt_c[:, 0:1]))
                                ch(v.tensor_scalar_mul(fsl(5), fsl(5), fcols[:, 0:1]))
                                ch(v.tensor_scalar_mul(fsl(5), fsl(5), fcols[:, 2:3]))
                                ch(v.tensor_mul(fsl(6), fsl(4), fsl(4)))
                                ch(v.tensor_mul(fsl(6), fsl(6), fsl(4)))
                                ch(v.tensor_scalar_mul(fsl(6), fsl(6), -1.0 / 3.0))
                                ch(v.tensor_add(fsl(6), fsl(6), fsl(4)))
                                ch(v.tensor_sub(fsl(6), fsl(6), fsl(5)))
                                ch(v.tensor_add(fsl(6), fsl(6), fsl(3)))
                                ch(v.tensor_scalar_mul(fsl(6), fsl(6), abdt_c[:, 2:3]))
                                ch(v.tensor_add(fsl(7), fsl(6), fsl(4)))
                                v.wait_ge(c_s.h, sid["gsig"])
                                ch(v.tensor_scalar(fsl(8), fsl(11), 0.5, 0.5,
                                                   ALU.mult, ALU.add))
                                ch(v.tensor_mul(fsl(6), fsl(8), fsl(0)))
                                ch(v.tensor_scalar_add(fsl(6), fsl(6), 1e-8))
                                ch(v.reciprocal(fsl(7), fsl(6)))
                                ch(v.tensor_mul(fsl(8), fsl(8), fsl(7)))
                                if g == 0:
                                    v.wait_ge(c_mm.h, tids["trF"])
                                    ch(v.tensor_copy(fT[:], pssm[:, 16:32]))

                            @blk.scalar
                            def _(s):
                                s.wait_ge(c_v.h, vid["E"])
                                s.activation(fsl(9), fsl(0), AF.Abs).then_inc(c_s.h, 1)
                                s.wait_ge(c_v.h, vid["aff"])
                                s.activation(fsl(10), fsl(9), AF.Sigmoid).then_inc(c_s.h, 1)
                                s.wait_ge(c_v.h, vid["v2"])
                                s.activation(fsl(11), fsl(7), AF.Sigmoid).then_inc(c_s.h, 1)

                            @blk.tensor
                            def _(t):
                                t.wait_ge(c_v.h, vid["ecol"])
                                t.transpose(pssm[0:1, 0:128], ecol,
                                            eye_f[:]).then_inc(c_mm.h, 1)
                                t.wait_ge(c_v.h, vid["stmax"])
                                t.transpose(pssm[0:8, 0:1], st8,
                                            eye_f[0:1, 0:1]).then_inc(c_mm.h, 1)
                                t.wait_ge(c_v.h, vid["st8bf"])
                                t.matmul(pssm[:, 2:3], sel16_bf[:, :], sqb[0:8, 0:1],
                                         start=True, stop=True).then_inc(c_mm.h, 1)
                                if g == 0:
                                    t.wait_ge(c_v.h, vid["f"])
                                    t.transpose(pssm[:, 16:32], fhn[0:16, 8 * 128:9 * 128],
                                                eye_f[0:16, 0:16]).then_inc(c_mm.h, 1)
                            c_v.n = v0 + len(vops)
                            c_s.n = s0 + 3
                            c_mm.n = m0 + len(tids)

                    with (
                        nc.psum_tensor(f"ps_x0{tag}{g}", [128, 512], F32) as px0,
                        nc.psum_tensor(f"ps_x1{tag}{g}", [128, 512], F32) as px1,
                        nc.Block(f"awout_{tag}{g}", no_gpsimd_drain=True) as blk,
                    ):
                        pxs = [px0, px1]
                        m0 = c_mm.n
                        e0 = c_ev.n
                        s0w = c_s.n
                        v0w = c_v.n

                        def a_ready(i):
                            return e0 + i * 2 + 2

                        @blk.tensor
                        def _(t):
                            c_dm.wait(t)
                            c_v.wait(t)
                            for i in range(NT):
                                for hf in range(2):
                                    gidx = i * 2 + hf
                                    if gidx >= 2:
                                        t.wait_ge(c_ev.h, e0 + gidx - 1)
                                    for j in range(4):
                                        ins = t.matmul(pxs[gidx % 2][:],
                                                       QTg[:, j * T + i * 128:j * T + (i + 1) * 128],
                                                       wog[:, j * 1024 + hf * 512:j * 1024 + hf * 512 + 512],
                                                       start=(j == 0), stop=(j == 3))
                                    ins.then_inc(c_mm.h, 1)
                        c_mm.n = m0 + 32

                        @blk.vector
                        def _(v):
                            for i in range(NT):
                                for hf in range(2):
                                    gidx = i * 2 + hf
                                    v.wait_ge(c_mm.h, m0 + gidx + 1)
                                    v.scalar_tensor_tensor(
                                        X[:, i * D + hf * 512:i * D + hf * 512 + 512],
                                        pxs[gidx % 2][:], fT[:, i:i + 1],
                                        X[:, i * D + hf * 512:i * D + hf * 512 + 512],
                                        ALU.mult, ALU.add).then_inc(c_ev.h, 1)
                            if g == 1 and stats_next:
                                stats_epi_vector(v, a_ready, s0w, v0w)

                        if g == 1 and stats_next:
                            @blk.scalar
                            def _(s):
                                stats_epi_scalar(s, a_ready, s0w, v0w)

                            c_s.n = s0w + 20
                            c_v.n = v0w + 26
                        c_ev.n = e0 + 32


        def emit_attn_b(seq, skip_ln=False, stats_next=False):
            tag = f"b{seq}"
            if not skip_ln:
                emit_ln(tag)
            with contextlib.ExitStack() as bst_:
                KT = bst_.enter_context(nc.sbuf_tensor(f"KTb_{tag}", [128, NC_ * T], BF16))
                Vb = bst_.enter_context(nc.sbuf_tensor(f"Vb_{tag}", [128, 16 * 1028], BF16))
                qbuf = bst_.enter_context(nc.sbuf_tensor(f"qb_{tag}", [128, NC_ * 512], BF16))
                Pb = bst_.enter_context(nc.sbuf_tensor(f"Pb_{tag}", [128, T], BF16))
                PTb = bst_.enter_context(nc.sbuf_tensor(f"PTb_{tag}", [128, T], BF16))
                wk0 = bst_.enter_context(nc.sbuf_tensor(f"wk_{tag}0", [128, 8 * 128], BF16))
                wk1 = bst_.enter_context(nc.sbuf_tensor(f"wk_{tag}1", [128, 8 * 128], BF16))
                wvh = bst_.enter_context(nc.sbuf_tensor(f"wvh_{tag}", [128, 8 * 512], BF16))
                sqb = bst_.enter_context(nc.sbuf_tensor(f"sqbb_{tag}", [128, 512], BF16))
                bvrow = bst_.enter_context(nc.sbuf_tensor(f"bvr_{tag}", [1, D], BF16))
                ubf = bst_.enter_context(nc.sbuf_tensor(f"ubfb_{tag}", [1, 16], BF16))
                ucol = bst_.enter_context(nc.sbuf_tensor(f"ucolb_{tag}", [1, 16], F32))
                rlq = bst_.enter_context(nc.sbuf_tensor(f"rlq_{tag}", [128, 4], F32))

                wks = [wk0, wk1]

                with contextlib.ExitStack() as pstA_:
                    pss0 = pstA_.enter_context(nc.psum_tensor(f"bsA0{tag}", [128, 512], F32))
                    pss1 = pstA_.enter_context(nc.psum_tensor(f"bsA1{tag}", [128, 512], F32))
                    pss2 = pstA_.enter_context(nc.psum_tensor(f"bsA2{tag}", [128, 512], F32))
                    pss3 = pstA_.enter_context(nc.psum_tensor(f"bsA3{tag}", [128, 512], F32))
                    pt0 = pstA_.enter_context(nc.psum_tensor(f"bsyt0{tag}", [128, D], BF16))
                    pt1 = pstA_.enter_context(nc.psum_tensor(f"bsyt1{tag}", [128, D], BF16))
                    pss = [pss0, pss1, pss2, pss3]
                    pts = [pt0, pt1]

                    with nc.Block(f"bproj_{tag}", no_gpsimd_drain=True) as blk:
                        w0 = c_w.n
                        m0 = c_mm.n
                        v0y = c_v.n
                        t0y = c_tr.n
                        ev_yt = c_ev.n
                        e0 = ev_yt + NT

                        @blk.gpsimd
                        def _(gp):
                            c_w.inc(gp.dma_start(out=bvrow[:], in_=bv_h), 16)
                            for oc in range(8):
                                if oc >= 2:
                                    gp.wait_ge(c_mm.h, m0 + (oc - 1) * 4)
                                c_w.inc(gp.dma_start(
                                    out=wks[oc % 2][:],
                                    in_=wk_t[:, oc * 1024:(oc + 1) * 1024]), 16)
                            for half in range(2):
                                if half == 1:
                                    gp.wait_ge(c_mm.h, m0 + 32 + 16)
                                c_w.inc(gp.dma_start(
                                    out=wvh[:],
                                    in_=wv_t[:, half * 4096:(half + 1) * 4096]), 16)

                        @blk.tensor
                        def _(t):
                            for i in range(NT):
                                t.wait_ge(c_v.h, v0y + i + 1)
                                if i >= 2:
                                    t.wait_ge(c_ev.h, ev_yt + i - 1)
                                for c in range(NC_):
                                    t.transpose(pts[i % 2][:, c * 128:(c + 1) * 128],
                                                ytok[i % 2][:, c * 128:(c + 1) * 128],
                                                eye_bf[:]).then_inc(c_tr.h, 1)
                            for oc in range(8):
                                t.wait_ge(c_w.h, w0 + (oc + 2) * 16)
                                for m4 in range(4):
                                    gidx = oc * 4 + m4
                                    if gidx >= 4:
                                        t.wait_ge(c_ev.h, e0 + gidx - 3)
                                    t.wait_ge(c_ev.h, ev_yt + min(NT, 4 * (m4 + 1)))
                                    for i in range(NC_):
                                        ins = t.matmul(pss[gidx % 4][:],
                                                       wks[oc % 2][:, i * 128:(i + 1) * 128],
                                                       yt_sl(i, m4 * 512, (m4 + 1) * 512),
                                                       start=(i == 0), stop=(i == NC_ - 1))
                                    ins.then_inc(c_mm.h, 1)
                            c_tr.n = t0y + NT * 8
                            for half in range(2):
                                t.wait_ge(c_w.h, w0 + (10 + half) * 16)
                                for kt in range(NT):
                                    gidx = 32 + half * 16 + kt
                                    t.wait_ge(c_ev.h, e0 + gidx - 3)
                                    for i in range(NC_):
                                        t.matmul(pss[gidx % 4][:],
                                                 yt_sl(i, kt * 128, (kt + 1) * 128),
                                                 wvh[:, i * 512:(i + 1) * 512],
                                                 start=(i == 0), stop=False)
                                    ins = t.matmul(pss[gidx % 4][:], ones_bf[0:1, 0:128],
                                                   bvrow[0:1, half * 512:(half + 1) * 512],
                                                   start=False, stop=True)
                                    ins.then_inc(c_mm.h, 1)
                        c_mm.n = m0 + 64

                        @blk.vector
                        def _(v):
                            v.memset(Vb[:].rearrange("p (k w) -> p k w", w=1028)
                                     [:, :, 1024:1025], 1.0).then_inc(c_g.h, 1)
                            for i in range(NT):
                                if i >= 2:
                                    v.wait_ge(c_tr.h, t0y + (i - 1) * 8)
                                v.tensor_scalar(ytok[i % 2][:], xs(i), mean[:, i:i + 1],
                                                rstd[:, i:i + 1], ALU.subtract,
                                                ALU.mult).then_inc(c_v.h, 1)
                                if i >= 1:
                                    v.wait_ge(c_tr.h, t0y + i * 8)
                                    v.tensor_copy(
                                        YT[:].rearrange("p (c t) -> p c t", t=T)[:, :, (i - 1) * 128:i * 128],
                                        pts[(i - 1) % 2][:].rearrange("p (c t) -> p c t", t=128)).then_inc(c_ev.h, 1)
                            v.wait_ge(c_tr.h, t0y + NT * 8)
                            v.tensor_copy(
                                YT[:].rearrange("p (c t) -> p c t", t=T)[:, :, (NT - 1) * 128:NT * 128],
                                pts[(NT - 1) % 2][:].rearrange("p (c t) -> p c t", t=128)).then_inc(c_ev.h, 1)
                            c_v.n += NT
                            for oc in range(8):
                                for m4 in range(4):
                                    gidx = oc * 4 + m4
                                    v.wait_ge(c_mm.h, m0 + gidx + 1)
                                    v.tensor_scalar_add(
                                        KT[:, oc * T + m4 * 512:oc * T + (m4 + 1) * 512],
                                        pss[gidx % 4][:],
                                        bkt[:, oc:oc + 1]).then_inc(c_ev.h, 1)
                            for half in range(2):
                                for kt in range(NT):
                                    gidx = 32 + half * 16 + kt
                                    v.wait_ge(c_mm.h, m0 + gidx + 1)
                                    v.tensor_copy(
                                        Vb[:, kt * 1028 + half * 512:kt * 1028 + half * 512 + 512],
                                        pss[gidx % 4][:]).then_inc(c_ev.h, 1)
                        c_ev.n = e0 + 64
                        c_g.n += 1

                with contextlib.ExitStack() as pst_:
                    pss0 = pst_.enter_context(nc.psum_tensor(f"bs0{tag}", [128, 512], F32))
                    pss1 = pst_.enter_context(nc.psum_tensor(f"bs1{tag}", [128, 512], F32))
                    pss2 = pst_.enter_context(nc.psum_tensor(f"bs2{tag}", [128, 512], F32))
                    pss3 = pst_.enter_context(nc.psum_tensor(f"bs3{tag}", [128, 512], F32))
                    pv0 = pst_.enter_context(nc.psum_tensor(f"bpv0{tag}", [128, 512], F32))
                    pv1 = pst_.enter_context(nc.psum_tensor(f"bpv1{tag}", [128, 512], F32))
                    pvL = pst_.enter_context(nc.psum_tensor(f"bpvL{tag}", [128, 1], F32))
                    ptr = pst_.enter_context(nc.psum_tensor(f"bptr{tag}", [128, 128], BF16))
                    pss = [pss0, pss1, pss2, pss3]
                    pvs = [pv0, pv1]

                    with nc.Block(f"bknorm_{tag}", no_gpsimd_drain=True) as blk:
                        v0 = c_v.n
                        m0 = c_mm.n

                        @blk.vector
                        def _(v):
                            nv = [v0]

                            def ch(ins):
                                ins.then_inc(c_v.h, 1)
                                nv[0] += 1
                                v.wait_ge(c_v.h, nv[0])
                            nm = m0
                            for mc in range(4):
                                for i in range(NC_):
                                    if nm > m0:
                                        v.wait_ge(c_mm.h, nm)
                                    ch(v.tensor_mul(sqb[:],
                                                    KT[:, i * T + mc * 512:i * T + (mc + 1) * 512],
                                                    KT[:, i * T + mc * 512:i * T + (mc + 1) * 512]))
                                    nm += 1
                                v.wait_ge(c_mm.h, nm)
                                ch(v.reduce_max(ucol[0:1, mc:mc + 1], pss0[0:1, 0:512], axis=AX.X))
                            ch(v.reduce_max(ucol[0:1, 4:5], ucol[0:1, 0:4], axis=AX.X))
                        c_v.n = v0 + 4 * 9 + 1

                        @blk.tensor
                        def _(t):
                            nv = v0
                            for mc in range(4):
                                for i in range(NC_):
                                    nv += 1
                                    t.wait_ge(c_v.h, nv)
                                    t.matmul(pss0[0:1, 0:512], ones_bf[:, 0:1], sqb[:],
                                             start=(i == 0), stop=(i == NC_ - 1)
                                             ).then_inc(c_mm.h, 1)
                                nv += 1
                        c_mm.n = m0 + 32

                    for qd in range(4):
                        with nc.Block(f"bflash_{tag}{qd}", no_gpsimd_drain=True) as blk:
                            w0 = c_w.n
                            m0 = c_mm.n
                            s0 = c_s.n
                            v0 = c_v.n
                            e0 = c_ev.n
                            g0_ = c_g.n
                            t0_ = c_tr.n

                            id_qproj = {o: m0 + o + 1 for o in range(8)}
                            id_qnorm = {i: m0 + 8 + i + 1 for i in range(NC_)}
                            id_ubc = m0 + 17
                            mmn = m0 + 17
                            trn = t0_
                            sn_ = s0
                            evn = e0 + 8
                            gn_ = g0_
                            tabs = []
                            for s_ in range(4):
                                qt = qd * 4 + s_
                                nb = (qt * 128 + 128 + 511) // 512
                                d = {"S": {}, "exp": {}, "tr": {}, "trev": {}, "pv": None, "rec": None, "stt": {}}
                                for b in range(nb):
                                    d["S"][b] = mmn = mmn + 1
                                for b in range(nb):
                                    d["exp"][b] = sn_ = sn_ + 1
                                for kb in range(qt + 1):
                                    d["tr"][kb] = trn = trn + 1
                                    d["trev"][kb] = evn = evn + 1
                                d["pv"] = mmn = mmn + 1
                                d["rec"] = gn_ = gn_ + 1
                                for hf in range(2):
                                    d["stt"][hf] = evn = evn + 1
                                tabs.append(d)

                            @blk.gpsimd
                            def _(gp):
                                for oc in range(8):
                                    if oc >= 2:
                                        gp.wait_ge(c_mm.h, id_qproj[oc - 2])
                                    c_w.inc(gp.dma_start(
                                        out=wks[oc % 2][:],
                                        in_=wq_t[:, oc * 1024:(oc + 1) * 1024]), 16)


                            @blk.tensor
                            def _(t):
                                for oc in range(8):
                                    t.wait_ge(c_w.h, w0 + (oc + 1) * 16)
                                    if oc >= 2:
                                        t.wait_ge(c_ev.h, e0 + oc - 1)
                                    for i in range(NC_):
                                        ins = t.matmul(pss[oc % 2][:],
                                                       wks[oc % 2][:, i * 128:(i + 1) * 128],
                                                       yt_sl(i, qd * 512, (qd + 1) * 512),
                                                       start=(i == 0), stop=(i == NC_ - 1))
                                    ins.then_inc(c_mm.h, 1)
                                nv = v0
                                for i in range(NC_):
                                    nv += 1
                                    t.wait_ge(c_v.h, nv)
                                    t.matmul(pss2[0:1, 0:512], ones_bf[:, 0:1], sqb[:],
                                             start=(i == 0), stop=(i == NC_ - 1)
                                             ).then_inc(c_mm.h, 1)
                                t.wait_ge(c_v.h, v0 + 8 + 4)
                                t.matmul(pss3[0:1 * 128, 0:1], ones_bf[0:1, 0:128],
                                         ubf[0:1, 0:1], start=True, stop=True).then_inc(c_mm.h, 1)
                                for s_ in range(4):
                                    qt = qd * 4 + s_
                                    d = tabs[s_]
                                    nb = len(d["S"])
                                    for b in range(nb):
                                        if s_ >= 1 and b in tabs[s_ - 1]["exp"]:
                                            t.wait_ge(c_s.h, tabs[s_ - 1]["exp"][b])
                                        elif s_ == 0 and b < 2:
                                            t.wait_ge(c_ev.h, e0 + 8)
                                        elif s_ == 0 and b >= 2:
                                            t.wait_ge(c_v.h, v0 + 13)
                                        hi = min(512, qt * 128 + 128 - b * 512)
                                        for i in range(NC_):
                                            ins = t.matmul(
                                                pss[b][:, 0:hi],
                                                qbuf[:, i * 512 + s_ * 128:i * 512 + (s_ + 1) * 128],
                                                KT[:, i * T + b * 512:i * T + b * 512 + hi],
                                                start=(i == 0), stop=(i == NC_ - 1))
                                        ins.then_inc(c_mm.h, 1)
                                    for kb in range(qt + 1):
                                        t.wait_ge(c_s.h, d["exp"][kb // 4])
                                        if kb == qt:
                                            t.wait_ge(c_v.h, v0 + 13 + s_ + 1)
                                        if kb >= 1:
                                            t.wait_ge(c_ev.h, d["trev"][kb - 1])
                                        elif s_ >= 1:
                                            t.wait_ge(c_ev.h, tabs[s_ - 1]["trev"][qt - 1])
                                        t.transpose(ptr[:, :], Pb[:, kb * 128:(kb + 1) * 128],
                                                    eye_bf[:]).then_inc(c_tr.h, 1)
                                    if s_ >= 1:
                                        t.wait_ge(c_ev.h, tabs[s_ - 1]["stt"][1])
                                    for kb in range(qt + 1):
                                        t.wait_ge(c_ev.h, d["trev"][kb])
                                        t.matmul(pvs[0][:], PTb[:, kb * 128:(kb + 1) * 128],
                                                 Vb[:, kb * 1028:kb * 1028 + 512],
                                                 start=(kb == 0), stop=(kb == qt))
                                        t.matmul(pvs[1][:], PTb[:, kb * 128:(kb + 1) * 128],
                                                 Vb[:, kb * 1028 + 512:kb * 1028 + 1024],
                                                 start=(kb == 0), stop=(kb == qt))
                                        ins = t.matmul(pvL[:], PTb[:, kb * 128:(kb + 1) * 128],
                                                       Vb[:, kb * 1028 + 1024:kb * 1028 + 1025],
                                                       start=(kb == 0), stop=(kb == qt))
                                    ins.then_inc(c_mm.h, 1)
                            c_mm.n = mmn
                            c_tr.n = trn

                            def b_ready(i):
                                if i < qd * 4:
                                    return 0
                                return tabs[i - qd * 4]["stt"][1]

                            @blk.scalar
                            def _(s):
                                for s_ in range(4):
                                    qt = qd * 4 + s_
                                    d = tabs[s_]
                                    for b in range(len(d["S"])):
                                        s.wait_ge(c_mm.h, d["S"][b])
                                        if s_ >= 1:
                                            s.wait_ge(c_tr.h, tabs[s_ - 1]["tr"][qt - 1])
                                        hi = min(512, qt * 128 + 128 - b * 512)
                                        s.activation(Pb[:, b * 512:b * 512 + hi],
                                                     pss[b][:, 0:hi], AF.Exp,
                                                     bias=negu[:, qd:qd + 1],
                                                     scale=1.0 / 32.0).then_inc(c_s.h, 1)
                                if qd == 3 and stats_next:
                                    stats_epi_scalar(s, b_ready, sn_, v0 + 13 + 4)
                            c_s.n = sn_ + (20 if (qd == 3 and stats_next) else 0)

                            @blk.vector
                            def _(v):
                                nv = [v0]

                                def ch(ins):
                                    ins.then_inc(c_v.h, 1)
                                    nv[0] += 1
                                    v.wait_ge(c_v.h, nv[0])
                                for oc in range(8):
                                    v.wait_ge(c_mm.h, id_qproj[oc])
                                    v.tensor_scalar_add(qbuf[:, oc * 512:(oc + 1) * 512],
                                                        pss[oc % 2][:],
                                                        bqt[:, oc:oc + 1]).then_inc(c_ev.h, 1)
                                for i in range(NC_):
                                    if i >= 1:
                                        v.wait_ge(c_mm.h, id_qnorm[i - 1])
                                    ch(v.tensor_mul(sqb[:], qbuf[:, i * 512:(i + 1) * 512],
                                                    qbuf[:, i * 512:(i + 1) * 512]))
                                v.wait_ge(c_mm.h, id_qnorm[NC_ - 1])
                                ch(v.reduce_max(ucol[0:1, 5:6], pss2[0:1, 0:512], axis=AX.X))
                                ch(v.tensor_add(ucol[0:1, 6:7], ucol[0:1, 4:5], ucol[0:1, 5:6]))
                                ch(v.tensor_scalar_mul(ucol[0:1, 7:8], ucol[0:1, 6:7], 1.0 / 64.0))
                                ch(v.tensor_copy(ubf[0:1, 0:1], ucol[0:1, 7:8]))
                                v.wait_ge(c_mm.h, id_ubc)
                                ch(v.tensor_scalar_mul(negu[:, qd:qd + 1], pss3[:, 0:1], -1.0))
                                for s_ in range(4):
                                    qt = qd * 4 + s_
                                    d = tabs[s_]
                                    v.wait_ge(c_s.h, d["exp"][qt // 4])
                                    ch(v.tensor_mul(Pb[:, qt * 128:(qt + 1) * 128],
                                                    Pb[:, qt * 128:(qt + 1) * 128],
                                                    trit_bf[:]))
                                    for kb in range(qt + 1):
                                        v.wait_ge(c_tr.h, d["tr"][kb])
                                        v.tensor_copy(PTb[:, kb * 128:(kb + 1) * 128],
                                                      ptr[:, :]).then_inc(c_ev.h, 1)
                                    v.wait_ge(c_mm.h, d["pv"])
                                    v.reciprocal(rlq[:, s_:s_ + 1], pvL[:, :]).then_inc(c_g.h, 1)
                                    v.wait_ge(c_g.h, d["rec"])
                                    for hf in range(2):
                                        v.scalar_tensor_tensor(
                                            X[:, qt * D + hf * 512:qt * D + hf * 512 + 512],
                                            pvs[hf][:], rlq[:, s_:s_ + 1],
                                            X[:, qt * D + hf * 512:qt * D + hf * 512 + 512],
                                            ALU.mult, ALU.add).then_inc(c_ev.h, 1)
                                if qd == 3 and stats_next:
                                    stats_epi_vector(v, b_ready, sn_, v0 + 13 + 4)
                            c_v.n = v0 + 13 + 4 + (26 if (qd == 3 and stats_next) else 0)
                            c_ev.n = evn
                            c_g.n = gn_

        for seq in range(NSEQ):
            emit_load_x(seq)
            prev_stats = False
            if EMIT["attn_a"]:
                emit_attn_a(seq, stats_next=EMIT["ffn1"])
                prev_stats = EMIT["ffn1"]
            if EMIT["ffn1"]:
                emit_ffn(seq, 1, skip_ln=prev_stats, stats_next=EMIT["attn_b"])
                prev_stats = EMIT["attn_b"]
            if EMIT["attn_b"]:
                emit_attn_b(seq, skip_ln=prev_stats, stats_next=EMIT["ffn2"])
                prev_stats = EMIT["ffn2"]
            if EMIT["ffn2"]:
                emit_ffn(seq, 2, skip_ln=prev_stats)
            emit_store_x(seq)

    return nc


def kernel(L0_attn_a_x, L0_attn_a_Wqkv, L0_attn_a_Wout, L0_attn_a_a, L0_attn_a_b,
           L0_attn_a_dt, L0_attn_b_Wq, L0_attn_b_bq, L0_attn_b_Wk, L0_attn_b_bk,
           L0_attn_b_Wv, L0_attn_b_bv, L0_ffn_w1, L0_ffn_b1, L0_ffn_w2, L0_ffn_b2):
    x = np.ascontiguousarray(np.asarray(L0_attn_a_x, dtype=np.float32))
    B = x.shape[0]
    ncores = 8
    per = B // ncores

    wqkv_t = np.ascontiguousarray(
        np.asarray(L0_attn_a_Wqkv, np.float32).T.reshape(8, 128, 24, 128)
        .transpose(1, 2, 0, 3).reshape(128, 24 * 8 * 128))
    wout_t = np.ascontiguousarray(
        np.asarray(L0_attn_a_Wout, np.float32).T.reshape(8, 128, 1024)
        .transpose(1, 0, 2).reshape(128, 8 * 1024))
    wq_t = np.ascontiguousarray(
        np.asarray(L0_attn_b_Wq, np.float32).T.reshape(8, 128, 8, 128)
        .transpose(1, 2, 0, 3).reshape(128, 8 * 8 * 128))
    wk_t = np.ascontiguousarray(
        np.asarray(L0_attn_b_Wk, np.float32).T.reshape(8, 128, 8, 128)
        .transpose(1, 2, 0, 3).reshape(128, 8 * 8 * 128))
    wv_t = np.ascontiguousarray(
        np.asarray(L0_attn_b_Wv, np.float32).T.reshape(8, 128, 2, 512)
        .transpose(1, 2, 0, 3).reshape(128, 2 * 8 * 512))
    w1 = np.ascontiguousarray(
        np.asarray(L0_ffn_w1, np.float32).reshape(8, 128, 2, 16, 128)
        .transpose(1, 2, 3, 0, 4).reshape(128, 2 * 16 * 8 * 128))
    w2 = np.ascontiguousarray(
        np.asarray(L0_ffn_w2, np.float32).reshape(2, 16, 128, 1024)
        .transpose(2, 0, 1, 3).reshape(128, 2 * 16 * 1024))
    b1r = np.ascontiguousarray(np.asarray(L0_ffn_b1, np.float32).reshape(FF // 128, 128).T)
    b2r = np.ascontiguousarray(np.asarray(L0_ffn_b2, np.float32).reshape(1, D))
    bqr = np.ascontiguousarray(np.asarray(L0_attn_b_bq, np.float32).reshape(NC_, 128).T)
    bkr = np.ascontiguousarray(np.asarray(L0_attn_b_bk, np.float32).reshape(NC_, 128).T)
    bvr = np.ascontiguousarray(np.asarray(L0_attn_b_bv, np.float32).reshape(1, D))
    s_a = np.asarray(L0_attn_a_a, np.float32).reshape(1, 1)
    s_b = np.asarray(L0_attn_a_b, np.float32).reshape(1, 1)
    s_dt = np.asarray(L0_attn_a_dt, np.float32).reshape(1, 1)
    c_eye = np.eye(128, dtype=np.float32)
    kk, qq = np.meshgrid(np.arange(128), np.arange(128), indexing="ij")
    c_tri = (kk <= qq).astype(np.float32)
    c_trit = np.ascontiguousarray(c_tri.T)
    c_ones = np.ones((128, 128), dtype=np.float32)
    c_sel16 = np.zeros((8, 128), dtype=np.float32)
    for hh in range(8):
        c_sel16[hh, hh * 16:(hh + 1) * 16] = 1.0

    nc = build_nc()
    in_maps = []
    for c in range(ncores):
        xc = np.ascontiguousarray(x[c * per:(c + 1) * per].reshape(per * T, D))
        in_maps.append({
            "x": xc, "wqkv_t": wqkv_t, "wout_t": wout_t, "wq_t": wq_t,
            "wk_t": wk_t, "wv_t": wv_t, "w1": w1, "w2": w2, "b1r": b1r,
            "b2r": b2r, "bqr": bqr, "bkr": bkr, "bvr": bvr, "s_a": s_a,
            "s_b": s_b, "s_dt": s_dt, "c_eye": c_eye, "c_tri": c_tri,
            "c_trit": c_trit, "c_ones": c_ones, "c_sel16": c_sel16,
        })
    res = run_bass_kernel_spmd(nc, in_maps, core_ids=list(range(ncores)))
    globals()["LAST_RES"] = res.results
    out = np.concatenate([r["out"].reshape(per, T, D) for r in res.results], axis=0)
    return out.astype(np.float32)
